# version 10
# baseline (speedup 1.0000x reference)
import math
import os
import numpy as np
import ml_dtypes
import concourse.bass as bass
import concourse.mybir as mybir
from concourse.bass_utils import run_bass_kernel_spmd

F32 = mybir.dt.float32
BF16 = mybir.dt.bfloat16
I8 = mybir.dt.int8
AF = mybir.ActivationFunctionType
ALU = mybir.AluOpType
AXL = mybir.AxisListType

E = 2048
H = 16
NTOK = 2048
NC = 16
NE = 16
CH = 10
BF = ml_dtypes.bfloat16


def _slopes(n):
    start = 2.0 ** (-2.0 ** (-(math.log2(n) - 3)))
    return np.array([start * start ** i for i in range(n)], dtype=np.float64)


_R = np.exp(-_slopes(H))
_RC = _R ** 128
_RC16 = _RC ** 16


class Prog:
    ENGINES = ("sync", "gpsimd", "tensor", "vector", "scalar")

    def __init__(self, nc):
        self.nc = nc
        self.ops = {e: [] for e in self.ENGINES}
        self.semh = {}
        self.semc = {}
        self.waited = {e: {} for e in self.ENGINES}

    def sem(self, name, guards):
        g = self.nc.semaphore(name)
        guards.append(g)
        self.semh[name] = g.__enter__()
        self.semc[name] = 0
        return name

    def inc(self, name, amt):
        self.semc[name] += amt
        return self.semh[name], (name, self.semc[name])

    def tok(self, name):
        return (name, self.semc[name])

    def add(self, eng, fn, waits=()):
        w = []
        for t in waits:
            if t is None:
                continue
            s, v = t
            if v <= 0 or self.waited[eng].get(s, 0) >= v:
                continue
            self.waited[eng][s] = v
            w.append((self.semh[s], v))
        self.ops[eng].append((w, fn))

    def flush(self, block):
        for ename in self.ENGINES:
            ops = self.ops[ename]
            if not ops:
                continue
            self.ops[ename] = []

            def body(eng, ops=ops):
                for waits, fn in ops:
                    for h, v in waits:
                        eng.wait_ge(h, v)
                    fn(eng)
            getattr(block, ename)(body)


_ENG_SEM = {"tensor": "s_pe", "vector": "s_ve", "scalar": "s_ac", "gpsimd": "s_po"}


_PH = int(os.environ.get("KPHASE", "99"))


def build_core_program():
    nc = bass.Bass("TRN2", target_bir_lowering=False)

    def din(name, shape, dt=F32):
        return nc.dram_tensor(name, shape, dt, kind="ExternalInput").ap()

    x_tok = din("x_tok", [NTOK, E])
    xTin = din("xT", [E, NTOK])
    wqkv = din("wqkv", [8, E, 768])
    wffnT = din("wffnT", [E, E])
    wo_t = din("wo_t", [4, E, 512])
    cosc = din("cosc", [128, NC * 2 * 32])
    sinc = din("sinc", [128, NC * 2 * 32])
    dmaskT_in = din("dmaskT", [H, 128, 128])
    qdb_in = din("qdb", [H, 128, 128])
    kdecT_in = din("kdecT", [128, H])
    coef_in = din("coef", [128, CH * 8])
    rmsw_in = din("rmsw_t", [128, H])
    gamma_in = din("gamma_t", [128, NE])
    gammab_in = din("gamma_b", [128, E])
    ident_in = din("identb", [128, 128], BF16)
    ones_in = din("ones_f", [1, 128])
    onesc_in = din("ones_c32", [128, 1])
    onescb_in = din("ones_cb", [128, 1], BF16)

    out = nc.dram_tensor("out", [NTOK, E], F32, kind="ExternalOutput").ap()

    q_scr = nc.dram_tensor("q_scr", [H, NTOK, 128], BF16).ap()
    k_scr = nc.dram_tensor("k_scr", [H, NTOK, 128], BF16).ap()
    kp_scr = nc.dram_tensor("kp_scr", [H, NTOK, 128], BF16).ap()
    v_scr = nc.dram_tensor("v_scr", [H, NTOK, 128], BF16).ap()
    ffn_scr = nc.dram_tensor("ffn_scr", [NTOK, E], BF16).ap()
    row_scr = nc.dram_tensor("row_scr", [128, 48], F32).ap()
    cc_i = nc.dram_tensor("cc_i", [CH * 128, 128], F32).ap()
    cc_o = nc.dram_tensor("cc_o", [8 * CH * 128, 128], F32, addr_space="Shared").ap()

    guards = []

    def sb(name, shape, dt):
        g = nc.sbuf_tensor(name, shape, dt)
        guards.append(g)
        return g.__enter__()

    def mkpsum(name, shape, dt=F32):
        g = nc.psum_tensor(name, shape, dt)
        guards.append(g)
        return g.__enter__()

    arX = sb("arX", [128, NE * 2048], BF16)
    arW = sb("arW", [128, NE * 2048], BF16)
    arQ = sb("arQ", [128, 16384], BF16)
    arF = sb("arF", [128, 4096], F32)
    arG = sb("arG", [128, 4096], F32)
    arQi = arQ[:].bitcast(I8)
    arQf = arQ[:].bitcast(F32)
    arGb = arG[:].bitcast(BF16)

    c_kd = sb("c_kd", [128, H], F32)
    c_cf = sb("c_cf", [128, CH * 8], F32)
    c_rw = sb("c_rw", [128, H], F32)
    c_gt = sb("c_gt", [128, NE], F32)
    c_id = sb("c_id", [128, 128], BF16)
    c_on = sb("c_on", [1, 128], F32)
    c_oc = sb("c_oc", [128, 1], F32)
    c_ob = sb("c_ob", [128, 1], BF16)

    epsv = sb("epsv", [128, 2], F32)
    junk = arQ[:, 14336:16384]
    junki = junk.bitcast(I8)
    sv = sb("sv", [128, 16 * NC], F32)
    stats = sb("stats", [128, 64], F32)
    rows = sb("rows", [1, 2564], F32)
    wsc = sb("wsc", [128, 4], F32)
    arXf = arX[:].bitcast(F32)
    arWf = arW[:].bitcast(F32)
    bcast = arQ[:, 8192:14336]
    c_gb = arWf[:, 0:2048]
    zs = arX[:, 0:2048]
    ssb = arX[:, 2048:4096]
    ccst = arXf[:, 2048:2048 + CH * 128]
    ccg = arXf[:, 3328:3328 + 2 * CH * 128]
    kvini = arX[:, 11776:11776 + CH * 128]
    kvs = arX[:, 13056:13312]
    c_dm = arXf[:, 6720:8768]
    c_qdb = arXf[:, 8768:10816]

    psA = mkpsum("psA", [128, 1024], F32)
    psB = mkpsum("psB", [128, 1024], F32)
    psC = mkpsum("psC", [128, 2048], F32)
    psAb = psA[:].bitcast(BF16)
    psBb = psB[:].bitcast(BF16)

    P = Prog(nc)
    for s in ("s_sy", "s_gp", "s_pe", "s_ve", "s_ac", "s_po", "s_cc", "s_out"):
        P.sem(s, guards)

    blk_guard = nc.Block()
    block = blk_guard.__enter__()

    def dma(queue, dst, src, waits=(), sem=None, **kw):
        sname = sem or ("s_sy" if queue == "sync" else "s_gp")
        h, t = P.inc(sname, 16)
        P.add(queue, lambda eng, d=dst, s=src: eng.dma_start(out=d, in_=s, **kw)
              .then_inc(h, 16), waits)
        return t

    def op(eng, fn, waits=()):
        h, t = P.inc(_ENG_SEM[eng], 1)
        P.add(eng, lambda e, f=fn: f(e).then_inc(h, 1), waits)
        return t

    def op_group(eng, fns, waits=()):
        h, t = P.inc(_ENG_SEM[eng], 1)

        def run(e, fns=tuple(fns)):
            for f in fns[:-1]:
                f(e)
            fns[-1](e).then_inc(h, 1)
        P.add(eng, run, waits)
        return t

    t_eps = op("gpsimd", lambda e: e.memset(epsv[:, 0:1], 1e-5), [])
    t_eps = op("gpsimd", lambda e: e.memset(epsv[:, 1:2], 1e-6), [t_eps])
    dma("sync", c_kd[:], kdecT_in)
    dma("sync", c_cf[:], coef_in)
    dma("sync", c_rw[:], rmsw_in)
    t_gb = dma("sync", c_gb, gammab_in)
    t_id = dma("sync", c_id[:], ident_in)
    t_on = dma("sync", c_on[:], ones_in)
    t_oc = dma("sync", c_oc[:], onesc_in)
    t_ob = dma("sync", c_ob[:], onescb_in)
    t_consts = dma("sync", c_gt[:], gamma_in)

    t_xT = []
    for e in range(NE):
        t_xT.append(dma("gpsimd", arX[:, e * 2048:(e + 1) * 2048],
                        xTin[e * 128:(e + 1) * 128, :]))
    t_xT_all = t_xT[-1]

    xstage = [arF[:, 0:2048], arF[:, 2048:4096]]
    xnbuf = [arG[:, 0:2048], arG[:, 2048:4096]]
    t_xfree = [None, None]
    t_nfree = [None, None]
    t_K = [None] * NC
    for c in range(NC):
        b = xstage[c % 2]
        xn = xnbuf[c % 2]
        V = sv[:, c * 16:(c + 1) * 16]
        t_ld = dma("sync", b, x_tok[c * 128:(c + 1) * 128, :], waits=[t_xfree[c % 2]])
        t_s1 = op("scalar", lambda e, b=b, V=V: e.activation(
            junk, b, AF.Copy, accum_out=V[:, 0:1]), [t_ld])
        t_s2 = op("scalar", lambda e, b=b, V=V: e.activation(
            junk, b, AF.Square, accum_out=V[:, 1:2]), [t_ld])
        t_m = op("vector", lambda e, V=V, c=c: e.tensor_scalar(
            stats[:, c:c + 1], V[:, 0:1], 1.0 / E, None, ALU.mult), [t_s1])
        t_v1 = op("vector", lambda e, V=V: e.tensor_scalar(
            V[:, 2:3], V[:, 1:2], 1.0 / E, None, ALU.mult), [t_s2])
        t_m2 = op("vector", lambda e, V=V, c=c: e.tensor_tensor(
            V[:, 3:4], stats[:, c:c + 1], stats[:, c:c + 1], ALU.mult), [t_m])
        t_var = op("vector", lambda e, V=V: e.tensor_tensor(
            V[:, 4:5], V[:, 2:3], V[:, 3:4], ALU.subtract), [t_v1, t_m2])
        t_sd = op("scalar", lambda e, V=V: e.activation(
            V[:, 5:6], V[:, 4:5], AF.Sqrt, bias=epsv[:, 0:1]), [t_var, t_eps])
        t_rstd = op("vector", lambda e, V=V, c=c: e.reciprocal(
            stats[:, 16 + c:17 + c], V[:, 5:6]), [t_sd])
        t_u = op("vector", lambda e, V=V: e.tensor_scalar(
            V[:, 6:7], V[:, 4:5], 1e-5, None, ALU.add), [t_var])
        t_ur = op("vector", lambda e, V=V: e.reciprocal(V[:, 7:8], V[:, 6:7]), [t_u])
        t_r2 = op("vector", lambda e, V=V: e.tensor_tensor(
            V[:, 8:9], V[:, 4:5], V[:, 7:8], ALU.mult), [t_ur])
        t_r2e = op("scalar", lambda e, V=V: e.activation(
            V[:, 9:10], V[:, 8:9], AF.Sqrt, bias=epsv[:, 1:2]), [t_r2, t_eps])
        t_rstd2 = op("vector", lambda e, V=V: e.reciprocal(V[:, 10:11], V[:, 9:10]),
                     [t_r2e])
        t_xn = op("vector", lambda e, b=b, xn=xn, c=c: e.tensor_scalar(
            xn, b, stats[:, c:c + 1], stats[:, 16 + c:17 + c],
            ALU.subtract, ALU.mult), [t_rstd, t_nfree[c % 2]])
        t_xg = op("gpsimd", lambda e, b=b, xn=xn: e.tensor_tensor(
            b, xn, c_gb, ALU.mult), [t_xn, t_gb, t_s1, t_s2])
        t_am = op("vector", lambda e, b=b, V=V: e.tensor_reduce(
            V[:, 11:12], b, AXL.X, ALU.max, apply_absolute_value=True), [t_xg])
        t_amx = op("vector", lambda e, V=V: e.tensor_tensor(
            V[:, 12:13], V[:, 11:12], V[:, 10:11], ALU.mult), [t_am, t_rstd2])
        t_ap = op("vector", lambda e, V=V: e.tensor_scalar(
            V[:, 13:14], V[:, 12:13], 1e-5, None, ALU.add), [t_amx])
        t_ar = op("vector", lambda e, V=V: e.reciprocal(V[:, 14:15], V[:, 13:14]),
                  [t_ap])
        t_as = op("vector", lambda e, V=V, c=c: e.tensor_scalar(
            stats[:, 48 + c:49 + c], V[:, 14:15], 127.0, None, ALU.mult), [t_ar])
        t_k1 = op("vector", lambda e, V=V, c=c: e.tensor_tensor(
            V[:, 15:16], stats[:, 16 + c:17 + c], V[:, 10:11], ALU.mult),
            [t_rstd, t_rstd2])
        t_K[c] = op("vector", lambda e, V=V, c=c: e.tensor_tensor(
            stats[:, 32 + c:33 + c], V[:, 15:16], stats[:, 48 + c:49 + c],
            ALU.mult), [t_k1, t_as])
        t_xfree[c % 2] = t_am
        t_nfree[c % 2] = t_xg
    t_p1done = t_K[NC - 1]

    if _PH <= 1:
        dma("sync", out[0:128, 0:64], stats[:], sem="s_out")
        P.add("gpsimd", lambda e: e.nop(), [P.tok("s_out")])
        P.flush(block)
        blk_guard.__exit__(None, None, None)
        for g in reversed(guards):
            g.__exit__(None, None, None)
        return nc
    t_wslab = [None, None]
    t_wacc = None
    for i in range(NE):
        b = xnbuf[i % 2]
        gate = t_nfree[i % 2] if i < 2 else t_wslab[i % 2]
        t_wl = dma("sync", b, wffnT[i * 128:(i + 1) * 128, :], waits=[gate])
        t_abs = op("scalar", lambda e, b=b: e.activation(
            junk, b, AF.Abs, accum_out=wsc[:, 1:2]), [t_wl, t_wacc])
        if i == 0:
            t_wacc = op("vector", lambda e: e.tensor_copy(
                wsc[:, 2:3], wsc[:, 1:2]), [t_abs])
        else:
            t_wacc = op("vector", lambda e: e.tensor_tensor(
                wsc[:, 2:3], wsc[:, 2:3], wsc[:, 1:2], ALU.add), [t_abs])
        t_wslab[i % 2] = t_wacc
    t_rw1_ph = [None]
    t_wred = op("tensor", lambda e: e.matmul(
        psA[0:1, 0:1], wsc[:, 2:3], c_oc[:], start=True, stop=True),
        [t_wacc, t_oc])
    t_ws1 = op("vector", lambda e: e.tensor_scalar(
        rows[0:1, 2560:2561], psA[0:1, 0:1], 1.0 / (E * E), 1e-5,
        ALU.mult, ALU.add), [t_wred])
    t_ws2 = op("vector", lambda e: e.reciprocal(
        rows[0:1, 2561:2562], rows[0:1, 2560:2561]), [t_ws1])
    t_wsb = op("tensor", lambda e: e.matmul(
        psA[:, 0:1], c_on[:], rows[0:1, 2561:2562], start=True, stop=True),
        [t_ws2, t_on])
    t_wsv = op("scalar", lambda e: e.copy(wsc[:, 0:1], psA[:, 0:1]), [t_wsb])
    t_f1 = op("vector", lambda e: e.tensor_scalar(
        stats[:, 0:16], stats[:, 48:64], wsc[:, 0:1], None, ALU.mult),
        [t_wsv, t_p1done, t_rw1_ph[0]])
    t_ffsc = op("vector", lambda e: e.reciprocal(stats[:, 48:64], stats[:, 0:16]),
                [t_f1])


    if _PH <= 2:
        dma("sync", out[0:128, 0:64], stats[:], sem="s_out")
        P.add("gpsimd", lambda e: e.nop(), [P.tok("s_out")])
        P.flush(block)
        blk_guard.__exit__(None, None, None)
        for g in reversed(guards):
            g.__exit__(None, None, None)
        return nc
    t_rw1 = dma("sync", row_scr[:, 0:48], stats[:, 0:48], waits=[t_p1done])
    t_rw1_ph[0] = t_rw1
    t_bc = []
    t_rdfree = [None, None]
    t_mmq = [None, None]
    kq = 0
    for j in range(3):
        tev = None
        for q in range(4):
            sl = rows[0:1, (kq % 2) * 512:(kq % 2) * 512 + 512]
            src_ap = row_scr.rearrange("p c -> c p")[
                j * 16 + q * 4:j * 16 + (q + 1) * 4, :].unsqueeze(0)
            t_rd = dma("gpsimd", sl.rearrange("a (c p) -> a c p", c=4), src_ap,
                       waits=[t_rw1, t_rdfree[kq % 2]],
                       allow_slow_non_contiguous=True)
            tmm = op("tensor", lambda e, kq=kq, sl=sl: e.matmul(
                psB[:, (kq % 2) * 512:(kq % 2) * 512 + 512], c_on[:], sl,
                start=True, stop=True),
                [t_rd, t_on, t_wsb, t_mmq[kq % 2]])
            t_rdfree[kq % 2] = tmm
            tev = op("scalar", lambda e, j=j, q=q, kq=kq: e.copy(
                bcast[:, j * 2048 + q * 512:j * 2048 + (q + 1) * 512],
                psB[:, (kq % 2) * 512:(kq % 2) * 512 + 512]), [tmm])
            t_mmq[kq % 2] = tev
            kq += 1
        t_bc.append(tev)
    m_b = bcast[:, 0:2048]
    rstd_b = bcast[:, 2048:4096]
    K_b = bcast[:, 4096:6144]


    if _PH <= 3:
        dma("sync", out[0:128, 0:64], stats[:], sem="s_out")
        P.add("gpsimd", lambda e: e.nop(), [P.tok("s_out")])
        P.flush(block)
        blk_guard.__exit__(None, None, None)
        for g in reversed(guards):
            g.__exit__(None, None, None)
        return nc
    t_wr = [t_wslab[0], t_wslab[1]]
    for i in range(NE):
        b = xnbuf[i % 2]
        t_wl = dma("sync", b, wffnT[i * 128:(i + 1) * 128, :], waits=[t_wr[i % 2]])
        ti8 = op("vector", lambda e, b=b: e.tensor_scalar(
            junki[:, 0:2048], b, wsc[:, 0:1], None, ALU.mult),
            [t_wl, t_wsv] + ([t_nfree[0], t_nfree[1]] if i < 2 else []))
        t_wr[i % 2] = op("vector", lambda e, i=i: e.tensor_scalar(
            arW[:, i * 2048:(i + 1) * 2048], junki[:, 0:2048],
            1.0, -1.0, ALU.min, ALU.max), [ti8])
    t_wrdone = op("vector", lambda e: e.tensor_copy(wsc[:, 1:2], wsc[:, 0:1]),
                  [t_wr[0], t_wr[1]])


    if _PH <= 4:
        dma("sync", out[0:128, 0:64], stats[:], sem="s_out")
        P.add("gpsimd", lambda e: e.nop(), [P.tok("s_out")])
        P.flush(block)
        blk_guard.__exit__(None, None, None)
        for g in reversed(guards):
            g.__exit__(None, None, None)
        return nc
    t_ffn_ev = {}
    t_ps_free = {0: t_wsv, 1: None}
    t_mm_prev = None
    for half in range(2):
        hs = slice(half * 1024, (half + 1) * 1024)
        t_xq = [None] * NE
        for e in range(NE):
            t_t1 = op("vector", lambda e_, e=e, half=half, hs=hs: e_.tensor_tensor(
                arGb[:, 0:1024],
                arX[:, e * 2048 + half * 1024:e * 2048 + half * 1024 + 1024],
                m_b[:, hs], ALU.subtract),
                [t_bc[0], t_xT[e], t_wrdone, t_wslab[0], t_wslab[1]])
            t_xq[e] = op("vector", lambda e_, e=e, hs=hs: e_.scalar_tensor_tensor(
                arQi[:, e * 1024:(e + 1) * 1024], arGb[:, 0:1024],
                c_gt[:, e:e + 1], K_b[:, hs], ALU.mult, ALU.mult),
                [t_t1, t_bc[2], t_consts])
        for tci in range(8):
            tc = half * 8 + tci
            t_lhs = op("vector", lambda e_, tci=tci: e_.tensor_copy(
                arGb[:, 4096:6144].rearrange("p (e d) -> p e d", e=16),
                arQi[:, 0:16384].rearrange("p (e t) -> p e t", e=16)
                [:, :, tci * 128:(tci + 1) * 128]),
                t_xq + [t_mm_prev])
            for oh in range(4):
                pp = psA if oh % 2 == 0 else psB
                fns = [lambda eng, e=e, oh=oh, pp=pp: eng.matmul(
                    pp[:, 0:512], arGb[:, 4096 + e * 128:4096 + (e + 1) * 128],
                    arW[:, e * 2048 + oh * 512:e * 2048 + (oh + 1) * 512],
                    start=(e == 0), stop=(e == NE - 1)) for e in range(NE)]
                tmm = op_group("tensor", fns,
                               [t_lhs, t_wrdone, t_ps_free[oh % 2], t_mm_prev])
                t_mm_prev = tmm
                t_ps_free[oh % 2] = op("vector",
                                       lambda e_, oh=oh, pp=pp, tc=tc: e_.tensor_scalar(
                                           junk[:, oh * 512:(oh + 1) * 512], pp[:, 0:512],
                                           stats[:, 48 + tc:49 + tc], None, ALU.mult),
                                       [tmm, t_ffsc,
                                        t_ffn_ev.get(tc - 1)])
            t_ffn_ev[tc] = dma("sync", ffn_scr[tc * 128:(tc + 1) * 128, :],
                               junk, waits=[t_ps_free[0], t_ps_free[1]])
    t_ffn_done = t_ffn_ev[15]


    if _PH <= 5:
        dma("gpsimd", out, ffn_scr, waits=[t_ffn_done], sem="s_out")
        dma("sync", out[0:128, 0:64], stats[:, 0:64] if False else stats[:], sem="s_out") if False else None
        P.add("gpsimd", lambda e: e.nop(), [P.tok("s_out")])
        P.flush(block)
        blk_guard.__exit__(None, None, None)
        for g in reversed(guards):
            g.__exit__(None, None, None)
        return nc
    t_xnT = None
    for e in range(NE):
        t_a = op("vector", lambda e_, e=e: e_.tensor_tensor(
            arGb[:, 4096:6144], arX[:, e * 2048:(e + 1) * 2048], m_b[:],
            ALU.subtract), [t_bc[1], t_mm_prev if e == 0 else None])
        t_xnT = op("vector", lambda e_, e=e: e_.tensor_tensor(
            arX[:, e * 2048:(e + 1) * 2048], arGb[:, 4096:6144], rstd_b[:],
            ALU.mult), [t_a])


    if _PH <= 6:
        dma("sync", out[0:128, 0:64], stats[:], sem="s_out")
        P.add("gpsimd", lambda e: e.nop(), [P.tok("s_out")])
        P.flush(block)
        blk_guard.__exit__(None, None, None)
        for g in reversed(guards):
            g.__exit__(None, None, None)
        return nc
    c_cosR = arQf[:, 4480:5504]
    c_sinR = arQf[:, 5504:6528]
    t_cosR = dma("sync", c_cosR, cosc, waits=[t_xnT])
    t_sinR = dma("sync", c_sinR, sinc, waits=[t_xnT])

    stg = [arQ[:, 0:3072], arQ[:, 3072:6144]]
    kstg = [arQ[:, 6144:7168], arQ[:, 7168:8192]]
    rsc = [arQ[:, 8192:8448], arQ[:, 8448:8704], arQ[:, 8704:8960]]
    t_wq = [None, None]
    t_wq_free = [t_mm_prev, t_mm_prev]
    t_stg_free = [None, None]
    t_kp_free = [None, None]
    t_last_spill = None
    for pair in range(8):
        wb = arW[:, (pair % 2) * 12288:(pair % 2) * 12288 + 12288]
        t_wq[pair % 2] = dma("gpsimd", wb.rearrange("p (e n) -> p e n", e=16),
                             wqkv[pair].rearrange("(e p) n -> p e n", p=128),
                             waits=[t_wq_free[pair % 2],
                                    t_mm_prev if pair < 2 else None])
        for g4 in range(4):
            st = stg[g4 % 2]
            kp = kstg[g4 % 2]
            stv = st.rearrange("p (c h x) -> p c h x", c=4, h=2)
            t_evs = []
            for c4 in range(4):
                c = g4 * 4 + c4
                pp = psA if c % 2 == 0 else psB
                fns = []
                for e in range(NE):
                    for hh in range(2):
                        fns.append(lambda eng, e=e, hh=hh, pp=pp, c=c, wb=wb:
                                   eng.matmul(
                                       pp[:, hh * 512:hh * 512 + 384],
                                       arX[:, e * 2048 + c * 128:e * 2048 + (c + 1) * 128],
                                       wb[:, e * 768 + hh * 384:e * 768 + (hh + 1) * 384],
                                       start=(e == 0), stop=(e == NE - 1)))
                tmm = op_group("tensor", fns,
                               [t_xnT, t_wq[pair % 2], t_ps_free[c % 2]])
                if c4 % 2 == 0:
                    t_ev = op("scalar", lambda e_, pp=pp, st=st, c4=c4: e_.copy(
                        st[:, c4 * 768:(c4 + 1) * 768]
                        .rearrange("p (h x) -> p h x", h=2),
                        pp[:].rearrange("p (h x) -> p h x", h=2)[:, :, 0:384]),
                        [tmm, t_stg_free[g4 % 2]])
                else:
                    t_ev = op("vector", lambda e_, pp=pp, st=st, c4=c4: e_.tensor_copy(
                        st[:, c4 * 768:(c4 + 1) * 768]
                        .rearrange("p (h x) -> p h x", h=2),
                        pp[:].rearrange("p (h x) -> p h x", h=2)[:, :, 0:384]),
                        [tmm, t_stg_free[g4 % 2]])
                t_ps_free[c % 2] = t_ev
                t_evs.append(t_ev)
            stm = st.rearrange("p (m x) -> p m x", m=8)
            cosv = c_cosR.rearrange("p (c f) -> p c f", f=32)[:, g4 * 8:(g4 + 1) * 8, :]
            sinv = c_sinR.rearrange("p (c f) -> p c f", f=32)[:, g4 * 8:(g4 + 1) * 8, :]
            sA = rsc[0].rearrange("p (m f) -> p m f", m=8)
            sB = rsc[1].rearrange("p (m f) -> p m f", m=8)
            sC = rsc[2].rearrange("p (m f) -> p m f", m=8)
            t_rot = t_evs[-1]
            for base in (0, 128):
                x1 = stm[:, :, base:base + 32]
                x2 = stm[:, :, base + 64:base + 96]
                tA = op("vector", lambda e, x1=x1, cosv=cosv, sinv=sinv: e.tensor_tensor(
                    sA, x1, sinv, ALU.mult), [t_rot] + t_evs + [t_cosR, t_sinR])
                tB = op("vector", lambda e, x1=x1, cosv=cosv: e.tensor_tensor(
                    sB, x1, cosv, ALU.mult), [tA])
                tC = op("vector", lambda e, x2=x2, sinv=sinv: e.tensor_tensor(
                    sC, x2, sinv, ALU.mult), [tB])
                tY1 = op("vector", lambda e, x1=x1: e.tensor_tensor(x1, sB, sC, ALU.add), [tC])
                tD = op("vector", lambda e, x2=x2, cosv=cosv: e.tensor_tensor(
                    sB, x2, cosv, ALU.mult), [tY1])
                t_rot = op("vector", lambda e, x2=x2: e.tensor_tensor(x2, sB, sA, ALU.subtract), [tD])
            t_kp = t_rot
            for mh in range(8):
                head = pair * 2 + (mh % 2)
                t_kp = op("vector", lambda e, kp=kp, mh=mh, head=head, stm=stm: e.tensor_scalar(
                    kp[:, mh * 128:(mh + 1) * 128],
                    stm[:, mh, 128:256], c_kd[:, head:head + 1], None, ALU.mult),
                    [t_rot, t_kp_free[g4 % 2]])
            t_sp = []
            for scr, base in ((q_scr, 0), (k_scr, 128), (v_scr, 256)):
                for hh in range(2):
                    dst = scr[pair * 2 + hh, g4 * 512:(g4 + 1) * 512, :] \
                        .rearrange("(c p) d -> p c d", p=128)
                    t_sp.append(dma("sync", dst, stv[:, :, hh, base:base + 128],
                                    waits=[t_rot]))
            kpv = kp.rearrange("p (c h d) -> p c h d", c=4, h=2)
            for hh in range(2):
                dstp = kp_scr[pair * 2 + hh, g4 * 512:(g4 + 1) * 512, :] \
                    .rearrange("(c p) d -> p c d", p=128)
                t_sp.append(dma("sync", dstp, kpv[:, :, hh, :], waits=[t_kp]))
            t_stg_free[g4 % 2] = t_sp[7]
            t_kp_free[g4 % 2] = t_sp[7]
            t_last_spill = t_sp[7]
        t_wq_free[pair % 2] = t_last_spill
    t_p2done = t_last_spill


    if _PH <= 7:
        dma("sync", out[0:128, 0:64], stats[:], sem="s_out")
        P.add("gpsimd", lambda e: e.nop(), [P.tok("s_out")])
        P.flush(block)
        blk_guard.__exit__(None, None, None)
        for g in reversed(guards):
            g.__exit__(None, None, None)
        return nc
    t_dm = dma("sync", c_dm.rearrange("p (h c) -> p h c", h=H),
               dmaskT_in.rearrange("h s c -> s h c"), waits=[t_p2done])
    t_qdbl = dma("sync", c_qdb.rearrange("p (h c) -> p h c", h=H),
                 qdb_in.rearrange("h d c -> d h c"), waits=[t_p2done])

    q_sb = arQ[:, 0:2048]
    k_sb = arQ[:, 2048:4096]
    kp_sb = arQ[:, 4096:6144]
    v_sb = arQ[:, 6144:8192]
    qT = arQ[:, 8192:10240]
    kT = arQ[:, 10240:12288]
    uperm = arF[:, 0:2048]
    hlist = list(range(H - CH, H)) + list(range(H - CH))

    t_rd_free = [t_p2done] * 4
    t_zs_free = []
    t_sd0_free = None
    t_psC_free = t_p2done
    t_psT_free = {"q": t_ps_free[0], "k": t_ps_free[1]}
    t_attn_ev = {}
    t_ssb_free = t_p2done
    t_uperm_free = None
    t_qp_free = None
    t_cc_st = []

    for hi, h in enumerate(hlist):
        kept = h >= H - CH
        qpT = arQ[:, 12288:14336]
        t_q = dma("sync", q_sb.rearrange("p (l d) -> p l d", l=16),
                  q_scr[h].rearrange("(l p) d -> p l d", p=128),
                  waits=[t_rd_free[0]])
        t_k = dma("sync", k_sb.rearrange("p (l d) -> p l d", l=16),
                  k_scr[h].rearrange("(l p) d -> p l d", p=128),
                  waits=[t_rd_free[1]])
        t_kp2 = dma("sync", kp_sb.rearrange("p (l d) -> p l d", l=16),
                    kp_scr[h].rearrange("(l p) d -> p l d", p=128),
                    waits=[t_rd_free[2]])
        t_v = dma("sync", v_sb.rearrange("p (l d) -> p l d", l=16),
                  v_scr[h].rearrange("(l p) d -> p l d", p=128),
                  waits=[t_rd_free[3]])

        fns = [lambda eng, l=l: eng.matmul(
            psC[:, l * 128:(l + 1) * 128], kp_sb[:, l * 128:(l + 1) * 128],
            v_sb[:, l * 128:(l + 1) * 128], start=True, stop=True)
            for l in range(16)]
        t_U = op_group("tensor", fns, [t_kp2, t_v, t_psC_free])
        t_up = op("scalar", lambda e: e.activation(
            uperm.rearrange("p (e l) -> p e l", l=16),
            psC[:].rearrange("p (l e) -> p e l", l=16), AF.Copy),
            [t_U, t_uperm_free])
        sd0m = arF[:, 2048:4096]
        t_d0a = op("gpsimd", lambda e, h=h: e.memset(sd0m, float(_RC[h])),
                   [t_sd0_free])
        t_d0b = op("gpsimd", lambda e: e.memset(
            sd0m.rearrange("p (e l) -> p e l", l=16)[:, :, 0], 0.0), [t_d0a])
        t_zs = op("vector", lambda e: e.tensor_tensor_scan(
            zs[:], sd0m, uperm[:] if False else uperm,
            0.0, ALU.mult, ALU.add), [t_up, t_d0b] + list(t_zs_free))
        t_uperm_free = t_zs
        t_sd0_free = t_zs

        t_tr_done = {}
        for tn, src, dst, psx, t_src in (("q", q_sb, qT, psAb, t_q),
                                         ("k", k_sb, kT, psBb, t_k)):
            t_prev = t_psT_free[tn]
            for bidx in range(2):
                fns = [lambda eng, l=l, src=src, psx=psx: eng.transpose(
                    psx[:, (l % 8) * 128:(l % 8 + 1) * 128],
                    src[:, l * 128:(l + 1) * 128], c_id[:])
                    for l in range(bidx * 8, (bidx + 1) * 8)]
                t_tr = op_group("tensor", fns, [t_src, t_id, t_prev])
                if tn == "q":
                    t_prev = op("vector", lambda e, bidx=bidx, dst=dst, psx=psx:
                                e.tensor_copy(dst[:, bidx * 1024:(bidx + 1) * 1024],
                                              psx[:, 0:1024]), [t_tr])
                else:
                    t_prev = op("scalar", lambda e, bidx=bidx, dst=dst, psx=psx:
                                e.copy(dst[:, bidx * 1024:(bidx + 1) * 1024],
                                       psx[:, 0:1024]), [t_tr])
            t_tr_done[tn] = t_prev
            t_psT_free[tn] = t_prev

        t_qp = op("vector", lambda e, qpT=qpT, h=h: e.tensor_tensor(
            qpT.rearrange("p (l c) -> p l c", l=16),
            qT.rearrange("p (l c) -> p l c", l=16),
            c_qdb[:, h * 128:(h + 1) * 128].unsqueeze(1)
            .broadcast_to([128, 16, 128]), ALU.mult),
            [t_tr_done["q"], t_qdbl, t_qp_free])

        fns = [lambda eng, l=l: eng.matmul(
            psC[:, l * 128:(l + 1) * 128], kT[:, l * 128:(l + 1) * 128],
            qT[:, l * 128:(l + 1) * 128], start=True, stop=True)
            for l in range(16)]
        t_S = op_group("tensor", fns, [t_tr_done["q"], t_tr_done["k"], t_up])
        t_mask = op("vector", lambda e, h=h: e.tensor_tensor(
            ssb[:].rearrange("p (l c) -> p l c", l=16),
            psC[:].rearrange("p (l c) -> p l c", l=16),
            c_dm[:, h * 128:(h + 1) * 128].unsqueeze(1)
            .broadcast_to([128, 16, 128]), ALU.mult), [t_S, t_ssb_free, t_dm])

        fns = []
        for l in range(16):
            fns.append(lambda eng, l=l: eng.matmul(
                psC[:, l * 128:(l + 1) * 128], v_sb[:, l * 128:(l + 1) * 128],
                ssb[:, l * 128:(l + 1) * 128], start=True, stop=(l == 0),
                skip_group_check=True))
            if l >= 1:
                fns.append(lambda eng, l=l, qpT=qpT: eng.matmul(
                    psC[:, l * 128:(l + 1) * 128],
                    zs[:].rearrange("p (e l) -> p e l", l=16)[:, :, l - 1],
                    qpT[:, l * 128:(l + 1) * 128], start=False, stop=True,
                    skip_group_check=True))
        t_O = op_group("tensor", fns, [t_mask, t_zs, t_qp])
        t_ssb_free = t_O
        t_qp_free = t_O

        t_aev = op("scalar", lambda e, h=h: e.copy(
            arW[:, h * 2048:(h + 1) * 2048], psC[:]), [t_O])
        t_attn_ev[h] = t_aev
        t_psC_free = t_aev

        if kept:
            t_st = op("scalar", lambda e, h=h: e.copy(
                ccst[:, (h - (H - CH)) * 128:(h - (H - CH) + 1) * 128],
                zs[:].rearrange("p (e l) -> p e l", l=16)[:, :, 15]), [t_zs])
            t_cc_st.append(t_st)
            t_zs_free = [t_O, t_st]
        else:
            t_zs_free = [t_O]

        t_rd_free = [t_tr_done["q"], t_tr_done["k"], t_U, t_O]


    if _PH <= 8:
        w8 = [t_psC_free, t_uperm_free, t_qp_free] + list(t_zs_free)
        dma("gpsimd", out[0:128, :], zs, waits=w8, sem="s_out")
        dma("sync", out[128:256, :], uperm, waits=w8, sem="s_out")
        dma("gpsimd", out[256:384, :], kp_sb, waits=w8, sem="s_out")
        dma("gpsimd", out[384:512, :], v_sb, waits=w8, sem="s_out")
        dma("gpsimd", out[512:640, :], q_sb, waits=w8, sem="s_out")
        dma("gpsimd", out[640:768, :], k_sb, waits=w8, sem="s_out")
        dma("gpsimd", out[768:896, :], qT, waits=w8, sem="s_out")
        dma("gpsimd", out[896:1024, :], arQ[:, 12288:14336], waits=w8, sem="s_out")
        dma("gpsimd", out[1024:1152, :], ssb, waits=w8, sem="s_out")
        dma("gpsimd", out[1152:1280, 0:1536], ccst.bitcast(BF16)[:, 0:1536], waits=w8, sem="s_out")
        dma("sync", out[1280:1408, 0:64], stats[:], sem="s_out")
        P.add("gpsimd", lambda e: e.nop(), [P.tok("s_out")])
        P.flush(block)
        blk_guard.__exit__(None, None, None)
        for g in reversed(guards):
            g.__exit__(None, None, None)
        return nc
    t_ccw = dma("gpsimd", cc_i.rearrange("(a p) d -> p a d", p=128),
                ccst[:].rearrange("p (a d) -> p a d", a=CH), waits=t_cc_st)
    hcc, t_cc = P.inc("s_cc", 1)
    P.add("gpsimd", lambda e: e.collective_compute(
        "AllGather", ALU.bypass,
        replica_groups=[[0, 1, 2, 3, 4, 5, 6, 7]],
        ins=[cc_i], outs=[cc_o]).then_inc(hcc, 1), [t_ccw])

    t_last_attn = t_attn_ev[hlist[-1]]
    t_comb = [None] * CH
    t_ccr_free = None
    for r in range(8):
        buf = ccg[:, (r % 2) * CH * 128:(r % 2) * CH * 128 + CH * 128]
        t_ccr = dma("sync", buf.rearrange("p (a d) -> p a d", a=CH),
                    cc_o[r * CH * 128:(r + 1) * CH * 128, :]
                    .rearrange("(a p) d -> p a d", p=128),
                    waits=[("s_cc", 1), t_ccr_free])
        tlast = None
        for h6 in range(CH):
            if r == 0:
                tc2 = op("vector", lambda e, h6=h6, buf=buf: e.tensor_scalar(
                    arG[:, h6 * 128:(h6 + 1) * 128],
                    buf[:, h6 * 128:(h6 + 1) * 128],
                    c_cf[:, h6 * 8:h6 * 8 + 1], None, ALU.mult),
                    [t_ccr, t_last_attn])
            else:
                tc2 = op("vector", lambda e, h6=h6, buf=buf, r=r: e.scalar_tensor_tensor(
                    arG[:, h6 * 128:(h6 + 1) * 128],
                    buf[:, h6 * 128:(h6 + 1) * 128],
                    c_cf[:, h6 * 8 + r:h6 * 8 + r + 1],
                    arG[:, h6 * 128:(h6 + 1) * 128], ALU.mult, ALU.add),
                    [t_ccr, t_comb[h6]])
            t_comb[h6] = tc2
            tlast = tc2
        t_ccr_free = tlast
    t_kvb = op("vector", lambda e: e.tensor_copy(kvini[:], arG[:, 0:CH * 128]),
               t_comb)

    t_corr_ev = {}
    t_kvs_free = [None, None]
    t_psC2 = t_psC_free
    t_qrd_free = t_rd_free[0]
    t_trc_free = t_psT_free["q"]
    for h6 in range(CH):
        h = (H - CH) + h6
        t_qr = dma("sync", q_sb.rearrange("p (l d) -> p l d", l=16),
                   q_scr[h].rearrange("(l p) d -> p l d", p=128),
                   waits=[t_qrd_free])
        t_prev = t_trc_free
        for bidx in range(2):
            fns = [lambda eng, l=l: eng.transpose(
                psAb[:, (l % 8) * 128:(l % 8 + 1) * 128],
                q_sb[:, l * 128:(l + 1) * 128], c_id[:])
                for l in range(bidx * 8, (bidx + 1) * 8)]
            t_tr = op_group("tensor", fns, [t_qr, t_prev])
            t_prev = op("vector", lambda e, bidx=bidx: e.tensor_copy(
                qT[:, bidx * 1024:(bidx + 1) * 1024], psAb[:, 0:1024]), [t_tr])
        t_trc_free = t_prev
        t_mm_toks = []
        for l in range(16):
            ksl = kvs[:, (l % 2) * 128:(l % 2) * 128 + 128]
            t_s = op("vector", lambda e, h6=h6, ksl=ksl, l=l, h=h: e.tensor_scalar(
                ksl, kvini[:, h6 * 128:(h6 + 1) * 128],
                float(_RC[h] ** l), None, ALU.mult),
                [t_kvb, t_kvs_free[l % 2]])
            t_mm = op("tensor", lambda eng, l=l, ksl=ksl: eng.matmul(
                psC[:, l * 128:(l + 1) * 128], ksl,
                qT[:, l * 128:(l + 1) * 128], start=True, stop=True),
                [t_s, t_prev] + ([t_psC2] if l == 0 else []))
            t_kvs_free[l % 2] = t_mm
            t_mm_toks.append(t_mm)
        t_qrd_free = t_mm_toks[-1]
        t_e1 = op("vector", lambda e, h=h: e.tensor_tensor(
            ssb[:].rearrange("p (l c) -> p l c", l=16),
            psC[:].rearrange("p (l c) -> p l c", l=16),
            c_qdb[:, h * 128:(h + 1) * 128].unsqueeze(1)
            .broadcast_to([128, 16, 128]), ALU.mult),
            [t_mm_toks[-1], t_ssb_free])
        t_add = op("vector", lambda e, h=h: e.tensor_tensor(
            arW[:, h * 2048:(h + 1) * 2048], arW[:, h * 2048:(h + 1) * 2048],
            ssb, ALU.add), [t_e1, t_attn_ev[h]])
        t_ssb_free = t_add
        t_psC2 = t_add
        t_corr_ev[h] = t_add


    if _PH <= 9:
        w9 = [t_psC2, t_kvb]
        dma("gpsimd", out[0:128, 0:768], kvini, waits=w9, sem="s_out")
        dma("sync", out[128:256, 0:768], ccst, waits=w9, sem="s_out")
        dma("sync", out[256:384, 0:1536], ccg[:, 0:1536], waits=w9, sem="s_out")
        dma("gpsimd", out[384:512, :].rearrange("(a p) d -> p a d", p=128)
            .rearrange("p a d -> p (a d)") if False else out[384:512, :], arW[:, 0:2048], waits=w9, sem="s_out")
        dma("gpsimd", out[512:640, :], arW[:, 15 * 2048:16 * 2048], waits=w9, sem="s_out")
        dma("gpsimd", out[640:768, :], arW[:, 12 * 2048:13 * 2048], waits=w9, sem="s_out")
        dma("sync", out[768:896, 0:64], stats[:], sem="s_out")
        P.add("gpsimd", lambda e: e.nop(), [P.tok("s_out")])
        P.flush(block)
        blk_guard.__exit__(None, None, None)
        for g in reversed(guards):
            g.__exit__(None, None, None)
        return nc
    sqb = arQ[:, 0:2048]
    t_ss = [None] * 4
    t_sq_free = t_psC2
    for h in range(H):
        w_h = [t_attn_ev[h], t_corr_ev.get(h), t_sq_free]
        t_sq = op("vector", lambda e, h=h: e.tensor_tensor(
            sqb, arW[:, h * 2048:(h + 1) * 2048],
            arW[:, h * 2048:(h + 1) * 2048], ALU.mult), w_h)
        for q in range(4):
            pp, off = [(psA, 0), (psA, 512), (psB, 0), (psB, 512)][q]
            t_ss[q] = op("tensor", lambda e, q=q, pp=pp, off=off, h=h: e.matmul(
                pp[0:1, off:off + 512], c_ob[:], sqb[:, q * 512:(q + 1) * 512],
                start=(h == 0), stop=(h == H - 1)),
                [t_sq, t_ob] +
                ([t_psT_free["q"], t_psT_free["k"], t_psC2] if h == 0 else []))
        t_sq_free = t_ss[3]
    t_c1s = []
    for q in range(4):
        pp, off = [(psA, 0), (psA, 512), (psB, 0), (psB, 512)][q]
        t_c1s.append(op("vector", lambda e, pp=pp, off=off, q=q: e.tensor_copy(
            rows[0:1, (q % 2) * 512:(q % 2) * 512 + 512]
            if False else rows[0:1, 0:512].tensor[0:1, q * 512:(q + 1) * 512]
            if False else rows[0:1, 0:2048][0:1, q * 512:(q + 1) * 512],
            pp[0:1, off:off + 512]), [t_ss[q]]))
    t_rb = []
    for q in range(4):
        sl = rows[0:1, q * 512:(q + 1) * 512]
        t_r1 = op("vector", lambda e, sl=sl: e.tensor_scalar(
            sl, sl, 1.0 / E, None, ALU.mult), [t_c1s[q]])
        tmpsl = rows[0:1, 2048:2560]
        t_r2b = op("scalar", lambda e, sl=sl, tmpsl=tmpsl: e.activation(
            tmpsl, sl, AF.Sqrt, bias=epsv[0:1, 0:1]),
            [t_r1] + ([t_rb[q - 1][1]] if q >= 1 else []))
        t_r3 = op("vector", lambda e, sl=sl, tmpsl=tmpsl: e.reciprocal(sl, tmpsl),
                  [t_r2b])
        tmm = op("tensor", lambda e, q=q, sl=sl: e.matmul(
            psB[:, (q % 2) * 512:(q % 2) * 512 + 512], c_on[:], sl,
            start=True, stop=True),
            [t_r3] + t_c1s + ([t_rb[q - 2][0]] if q >= 2 else []))
        tev = op("scalar", lambda e, q=q: e.copy(
            arG[:, 2048 + q * 512:2048 + (q + 1) * 512],
            psB[:, (q % 2) * 512:(q % 2) * 512 + 512]), [tmm])
        t_rb.append((tev, t_r2b))
    t_rb = [t[0] for t in t_rb]
    rstdb = arG[:, 2048:4096]
    t_attn_n = None
    for h in range(H):
        t_n1 = op("vector", lambda e, h=h: e.scalar_tensor_tensor(
            sqb, arW[:, h * 2048:(h + 1) * 2048],
            c_rw[:, h:h + 1], rstdb, ALU.mult, ALU.mult),
            [t_rb[-1], t_attn_ev[h], t_corr_ev.get(h), t_attn_n])
        t_attn_n = op("vector", lambda e, h=h: e.tensor_copy(
            arW[:, h * 2048:(h + 1) * 2048], sqb), [t_n1])


    if _PH <= 10:
        for a_ in range(16):
            dma("gpsimd", out[a_ * 128:(a_ + 1) * 128, :],
                arW[:, a_ * 2048:(a_ + 1) * 2048], waits=[t_attn_n], sem="s_out")
        P.add("gpsimd", lambda e: e.nop(), [P.tok("s_out")])
        P.flush(block)
        blk_guard.__exit__(None, None, None)
        for g in reversed(guards):
            g.__exit__(None, None, None)
        return nc
    wob = [arQ[:, 0:8192], arQ[:, 8192:16384]]
    fft = [arX[:, 2048:2560], arX[:, 2560:3072]]
    xft = [arF[:, 2048:2560], arF[:, 2560:3072]]
    t1f = [arF[:, 3072:3584], arF[:, 3584:4096]]
    outstg = [arG[:, 0:512], arG[:, 512:1024]]
    t_wo = [None, None]
    t_wo_free = [t_sq_free, t_sq_free]
    t_ff_free = [t_psC2, t_psC2]
    t_xf_free = [t_rb[-1], t_rb[-1]]
    t_t1_free = [t_kvb, t_kvb]
    t_out_free = [t_kvb, t_kvb]
    t_psP = [t_r3, t_rb[-1]]
    t_outs = []
    for ot in range(4):
        t_wo[ot % 2] = dma("gpsimd", wob[ot % 2].rearrange("p (e n) -> p e n", e=16),
                           wo_t[ot].rearrange("(e p) n -> p e n", p=128),
                           waits=[t_attn_n if ot < 2 else None, t_wo_free[ot % 2]])
        for tc in range(16):
            i2 = tc % 2
            pp = psA if i2 == 0 else psB
            t_ff = dma("sync", fft[i2],
                       ffn_scr[tc * 128:(tc + 1) * 128, ot * 512:(ot + 1) * 512],
                       waits=[t_ff_free[i2], t_ffn_done])
            t_xf = dma("sync", xft[i2],
                       x_tok[tc * 128:(tc + 1) * 128, ot * 512:(ot + 1) * 512],
                       waits=[t_xf_free[i2]])
            fns = [lambda eng, e=e, tc=tc, ot=ot, pp=pp: eng.matmul(
                pp[:, 0:512],
                arW[:, e * 2048 + tc * 128:e * 2048 + (tc + 1) * 128],
                wob[ot % 2][:, e * 512:(e + 1) * 512],
                start=(e == 0), stop=(e == NE - 1)) for e in range(NE)]
            t_mm = op_group("tensor", fns, [t_attn_n, t_wo[ot % 2], t_psP[i2],
                                            t_outs[-1] if t_outs else None])
            t_1 = op("vector", lambda e, pp=pp, i2=i2: e.scalar_tensor_tensor(
                t1f[i2], pp[:, 0:512], 1.0, fft[i2], ALU.mult, ALU.add),
                [t_mm, t_ff, t_t1_free[i2]])
            t_psP[i2] = t_1
            t_ff_free[i2] = t_1
            t_2 = op("vector", lambda e, i2=i2: e.tensor_tensor(
                outstg[i2], t1f[i2], xft[i2], ALU.add),
                [t_1, t_xf, t_out_free[i2]])
            t_t1_free[i2] = t_2
            t_xf_free[i2] = t_2
            t_st = dma("sync", out[tc * 128:(tc + 1) * 128, ot * 512:(ot + 1) * 512],
                       outstg[i2], waits=[t_2], sem="s_out")
            t_out_free[i2] = t_st
            t_outs.append(t_st)
        t_wo_free[ot % 2] = t_outs[-1]

    P.add("gpsimd", lambda e: e.nop(), [t_outs[-1]])

    P.flush(block)
    blk_guard.__exit__(None, None, None)
    for g in reversed(guards):
        g.__exit__(None, None, None)
    return nc


def _make_consts(core):
    g = core % 4
    pos = g * 2048 + np.arange(2048, dtype=np.float64)
    freq = (1.0 / 1024.0) ** np.linspace(0.0, 1.0, 32)
    theta = pos[:, None] * freq[None, :]
    cos = np.cos(theta).astype(np.float32).reshape(NC, 128, 1, 32)
    sin = np.sin(theta).astype(np.float32).reshape(NC, 128, 1, 32)
    cos = np.ascontiguousarray(
        np.repeat(cos, 2, axis=2).transpose(1, 0, 2, 3).reshape(128, NC * 2 * 32))
    sin = np.ascontiguousarray(
        np.repeat(sin, 2, axis=2).transpose(1, 0, 2, 3).reshape(128, NC * 2 * 32))

    i = np.arange(128, dtype=np.float64)
    diff = i[None, :, None] - i[None, None, :]
    dm = np.where(diff >= 0, _R[:, None, None] ** np.maximum(diff, 0.0), 0.0)
    dmaskT = np.ascontiguousarray(np.transpose(dm, (0, 2, 1))).astype(np.float32)

    qdb = (_R[:, None] ** (i[None, :] + 1.0))
    qdb_t = np.ascontiguousarray(
        np.broadcast_to(qdb[:, None, :], (H, 128, 128))).astype(np.float32)
    kdecT = np.ascontiguousarray(
        (_R[None, :] ** (127.0 - i[:, None])).astype(np.float32))

    coef = np.zeros((CH * 8,), dtype=np.float64)
    for h6 in range(CH):
        for r in range(8):
            gr = r % 4
            if r // 4 == core // 4 and gr < g:
                coef[h6 * 8 + r] = _RC16[(H - CH) + h6] ** (g - 1 - gr)
    coefm = np.ascontiguousarray(
        np.broadcast_to(coef.astype(np.float32)[None, :], (128, CH * 8)))

    return dict(cosc=np.ascontiguousarray(cos), sinc=np.ascontiguousarray(sin),
                dmaskT=dmaskT, qdb=qdb_t, kdecT=kdecT, coef=coefm)


def kernel(L0_attn_x, L0_attn_Wq, L0_attn_Wk, L0_attn_Wv, L0_attn_Wo,
           L0_attn_rms_w, L0_ffn_gamma, L0_ffn_W):
    x = np.asarray(L0_attn_x, dtype=np.float32)
    Wq = np.asarray(L0_attn_Wq, dtype=np.float32)
    Wk = np.asarray(L0_attn_Wk, dtype=np.float32)
    Wv = np.asarray(L0_attn_Wv, dtype=np.float32)
    Wo = np.asarray(L0_attn_Wo, dtype=np.float32)
    rms_w = np.asarray(L0_attn_rms_w, dtype=np.float32)
    gamma = np.asarray(L0_ffn_gamma, dtype=np.float32)
    Wf = np.asarray(L0_ffn_W, dtype=np.float32)

    x2d = x.reshape(16384, E)
    pairs = []
    for pair in range(8):
        cols = []
        for hh in range(2):
            hd = (pair * 2 + hh) * 128
            cols += [Wq[:, hd:hd + 128], Wk[:, hd:hd + 128], Wv[:, hd:hd + 128]]
        pairs.append(np.concatenate(cols, axis=1))
    wqkv = np.ascontiguousarray(np.stack(pairs))
    wffnT = np.ascontiguousarray(Wf.T)
    wo_t = np.ascontiguousarray(Wo.reshape(E, 4, 512).transpose(1, 0, 2))
    gamma_t = np.ascontiguousarray(gamma.reshape(NE, 128).T)
    gamma_b = np.ascontiguousarray(np.broadcast_to(gamma[None, :], (128, E)))
    rmsw_t = np.ascontiguousarray(rms_w.reshape(H, 128).T)
    identb = np.eye(128, dtype=BF)
    ones_f = np.ones((1, 128), dtype=np.float32)
    ones_c32 = np.ones((128, 1), dtype=np.float32)
    ones_cb = np.ones((128, 1), dtype=BF)

    in_maps = []
    for core in range(8):
        sl = slice(core * NTOK, (core + 1) * NTOK)
        im = dict(
            x_tok=np.ascontiguousarray(x2d[sl]),
            xT=np.ascontiguousarray(x2d[sl].T),
            wqkv=wqkv, wffnT=wffnT, wo_t=wo_t,
            gamma_t=gamma_t, gamma_b=gamma_b, rmsw_t=rmsw_t,
            identb=identb, ones_f=ones_f, ones_c32=ones_c32, ones_cb=ones_cb,
        )
        im.update(_make_consts(core))
        in_maps.append(im)

    nc = build_core_program()
    res = run_bass_kernel_spmd(nc, in_maps, core_ids=list(range(8)))
    outs = [res.results[c]["out"] for c in range(8)]
    return np.concatenate(outs, axis=0).reshape(2, 8192, E).astype(np.float32)


# version 14
# speedup vs baseline: 1.0609x; 1.0609x over previous
import math
import os
import numpy as np
import ml_dtypes
import concourse.bass as bass
import concourse.mybir as mybir
from concourse.bass_utils import run_bass_kernel_spmd

F32 = mybir.dt.float32
BF16 = mybir.dt.bfloat16
I8 = mybir.dt.int8
AF = mybir.ActivationFunctionType
ALU = mybir.AluOpType
AXL = mybir.AxisListType

E = 2048
H = 16
NTOK = 2048
NC = 16
NE = 16
CH = 10
BF = ml_dtypes.bfloat16


def _slopes(n):
    start = 2.0 ** (-2.0 ** (-(math.log2(n) - 3)))
    return np.array([start * start ** i for i in range(n)], dtype=np.float64)


_R = np.exp(-_slopes(H))
_RC = _R ** 128
_RC16 = _RC ** 16


class Prog:
    ENGINES = ("sync", "gpsimd", "tensor", "vector", "scalar")

    def __init__(self, nc):
        self.nc = nc
        self.ops = {e: [] for e in self.ENGINES}
        self.semh = {}
        self.semc = {}
        self.waited = {e: {} for e in self.ENGINES}

    def sem(self, name, guards):
        g = self.nc.semaphore(name)
        guards.append(g)
        self.semh[name] = g.__enter__()
        self.semc[name] = 0
        return name

    def inc(self, name, amt):
        self.semc[name] += amt
        return self.semh[name], (name, self.semc[name])

    def tok(self, name):
        return (name, self.semc[name])

    def add(self, eng, fn, waits=()):
        w = []
        for t in waits:
            if t is None:
                continue
            s, v = t
            if v <= 0 or self.waited[eng].get(s, 0) >= v:
                continue
            self.waited[eng][s] = v
            w.append((self.semh[s], v))
        self.ops[eng].append((w, fn))

    def flush(self, block):
        for ename in self.ENGINES:
            ops = self.ops[ename]
            if not ops:
                continue
            self.ops[ename] = []

            def body(eng, ops=ops):
                for waits, fn in ops:
                    for h, v in waits:
                        eng.wait_ge(h, v)
                    fn(eng)
            getattr(block, ename)(body)


_ENG_SEM = {"tensor": "s_pe", "vector": "s_ve", "scalar": "s_ac", "gpsimd": "s_po"}


_PH = int(os.environ.get("KPHASE", "99"))


def build_core_program():
    nc = bass.Bass("TRN2", target_bir_lowering=False)

    def din(name, shape, dt=F32):
        return nc.dram_tensor(name, shape, dt, kind="ExternalInput").ap()

    x_tok = din("x_tok", [NTOK, E])
    xTin = din("xT", [E, NTOK])
    wqkv = din("wqkv", [8, E, 768])
    wffnT = din("wffnT", [E, E])
    wo_t = din("wo_t", [4, E, 512])
    cosc = din("cosc", [128, NC * 2 * 32])
    sinc = din("sinc", [128, NC * 2 * 32])
    dmaskT_in = din("dmaskT", [H, 128, 128])
    qdb_in = din("qdb", [H, 128, 128])
    kdecT_in = din("kdecT", [128, H])
    coef_in = din("coef", [128, CH * 8])
    rmsw_in = din("rmsw_t", [128, H])
    gamma_in = din("gamma_t", [128, NE])
    gammab_in = din("gamma_b", [128, E])
    ident_in = din("identb", [128, 128], BF16)
    ones_in = din("ones_f", [1, 128])
    onesc_in = din("ones_c32", [128, 1])
    onescb_in = din("ones_cb", [128, 1], BF16)

    out = nc.dram_tensor("out", [NTOK, E], F32, kind="ExternalOutput").ap()

    q_scr = nc.dram_tensor("q_scr", [H, NTOK, 128], BF16).ap()
    k_scr = nc.dram_tensor("k_scr", [H, NTOK, 128], BF16).ap()
    kp_scr = nc.dram_tensor("kp_scr", [H, NTOK, 128], BF16).ap()
    v_scr = nc.dram_tensor("v_scr", [H, NTOK, 128], BF16).ap()
    ffn_scr = nc.dram_tensor("ffn_scr", [NTOK, E], BF16).ap()
    row_scr = nc.dram_tensor("row_scr", [128, 48], F32).ap()
    cc_i = nc.dram_tensor("cc_i", [CH * 128, 128], F32).ap()
    cc_o = nc.dram_tensor("cc_o", [8 * CH * 128, 128], F32, addr_space="Shared").ap()

    guards = []

    def sb(name, shape, dt):
        g = nc.sbuf_tensor(name, shape, dt)
        guards.append(g)
        return g.__enter__()

    def mkpsum(name, shape, dt=F32):
        g = nc.psum_tensor(name, shape, dt)
        guards.append(g)
        return g.__enter__()

    arX = sb("arX", [128, NE * 2048], BF16)
    arW = sb("arW", [128, NE * 2048], BF16)
    arQ = sb("arQ", [128, 16384], BF16)
    arF = sb("arF", [128, 4096], F32)
    arG = sb("arG", [128, 4096], F32)
    arQi = arQ[:].bitcast(I8)
    arQf = arQ[:].bitcast(F32)
    arGb = arG[:].bitcast(BF16)

    c_kd = sb("c_kd", [128, H], F32)
    c_cf = sb("c_cf", [128, CH * 8], F32)
    c_rw = sb("c_rw", [128, H], F32)
    c_gt = sb("c_gt", [128, NE], F32)
    c_id = sb("c_id", [128, 128], BF16)
    c_on = sb("c_on", [1, 128], F32)
    c_oc = sb("c_oc", [128, 1], F32)
    c_ob = sb("c_ob", [128, 1], BF16)

    epsv = sb("epsv", [128, 2], F32)
    junk = arQ[:, 14336:16384]
    junki = junk.bitcast(I8)
    wsink = arQ[:, 8192:10240]
    wstb = [arQ[:, 14336:16384], arQ[:, 10240:12288]]
    sv = sb("sv", [128, 16 * NC], F32)
    stats = sb("stats", [128, 64], F32)
    rows = sb("rows", [1, 2564], F32)
    wsc = sb("wsc", [128, 4], F32)
    arXf = arX[:].bitcast(F32)
    arWf = arW[:].bitcast(F32)
    bcast = arQ[:, 8192:14336]
    c_gb = arWf[:, 0:2048]
    zs = arX[:, 0:2048]
    ssb = arX[:, 2048:4096]
    ccst = arXf[:, 2048:2048 + CH * 128]
    ccg = arXf[:, 3328:3328 + 2 * CH * 128]
    kvini = arX[:, 11776:11776 + CH * 128]
    kvs = arX[:, 13056:13312]
    c_dm = arXf[:, 6720:8768]
    c_qdb = arXf[:, 8768:10816]

    psA = mkpsum("psA", [128, 1024], F32)
    psB = mkpsum("psB", [128, 1024], F32)
    psC = mkpsum("psC", [128, 2048], F32)
    psAb = psA[:].bitcast(BF16)
    psBb = psB[:].bitcast(BF16)

    P = Prog(nc)
    for s in ("s_sy", "s_gp", "s_pe", "s_ve", "s_ac", "s_po", "s_cc", "s_out"):
        P.sem(s, guards)

    blk_guard = nc.Block()
    block = blk_guard.__enter__()

    def dma(queue, dst, src, waits=(), sem=None, **kw):
        sname = sem or ("s_sy" if queue == "sync" else "s_gp")
        h, t = P.inc(sname, 16)
        P.add(queue, lambda eng, d=dst, s=src: eng.dma_start(out=d, in_=s, **kw)
              .then_inc(h, 16), waits)
        return t

    def op(eng, fn, waits=()):
        h, t = P.inc(_ENG_SEM[eng], 1)
        P.add(eng, lambda e, f=fn: f(e).then_inc(h, 1), waits)
        return t

    def op_group(eng, fns, waits=()):
        h, t = P.inc(_ENG_SEM[eng], 1)

        def run(e, fns=tuple(fns)):
            for f in fns[:-1]:
                f(e)
            fns[-1](e).then_inc(h, 1)
        P.add(eng, run, waits)
        return t

    t_eps = op("gpsimd", lambda e: e.memset(epsv[:, 0:1], 1e-5), [])
    t_eps = op("gpsimd", lambda e: e.memset(epsv[:, 1:2], 1e-6), [t_eps])
    dma("sync", c_kd[:], kdecT_in)
    dma("sync", c_cf[:], coef_in)
    dma("sync", c_rw[:], rmsw_in)
    t_gb = dma("sync", c_gb, gammab_in)
    t_id = dma("sync", c_id[:], ident_in)
    t_on = dma("sync", c_on[:], ones_in)
    t_oc = dma("sync", c_oc[:], onesc_in)
    t_ob = dma("sync", c_ob[:], onescb_in)
    t_consts = dma("sync", c_gt[:], gamma_in)

    t_xT = []
    for e in range(NE):
        t_xT.append(dma("gpsimd", arX[:, e * 2048:(e + 1) * 2048],
                        xTin[e * 128:(e + 1) * 128, :]))
    t_xT_all = t_xT[-1]

    xstage = [arF[:, 0:2048], arF[:, 2048:4096]]
    xnbuf = [arG[:, 0:2048], arG[:, 2048:4096]]
    t_xfree = [None, None]
    t_nfree = [None, None]
    t_wslab = [None, None]
    t_wacc = None
    t_K = [None] * NC
    for c in range(NC):
        b = xstage[c % 2]
        xn = xnbuf[c % 2]
        V = sv[:, c * 16:(c + 1) * 16]
        t_ld = dma("sync", b, x_tok[c * 128:(c + 1) * 128, :], waits=[t_xfree[c % 2]])
        t_s1 = op("scalar", lambda e, b=b, V=V: e.activation(
            wsink, b, AF.Copy, accum_out=V[:, 0:1]), [t_ld])
        t_s2 = op("scalar", lambda e, b=b, V=V: e.activation(
            wsink, b, AF.Square, accum_out=V[:, 1:2]), [t_ld])
        t_m = op("vector", lambda e, V=V, c=c: e.tensor_scalar(
            stats[:, c:c + 1], V[:, 0:1], 1.0 / E, None, ALU.mult), [t_s1])
        t_v1 = op("vector", lambda e, V=V: e.tensor_scalar(
            V[:, 2:3], V[:, 1:2], 1.0 / E, None, ALU.mult), [t_s2])
        t_m2 = op("vector", lambda e, V=V, c=c: e.tensor_tensor(
            V[:, 3:4], stats[:, c:c + 1], stats[:, c:c + 1], ALU.mult), [t_m])
        t_var = op("vector", lambda e, V=V: e.tensor_tensor(
            V[:, 4:5], V[:, 2:3], V[:, 3:4], ALU.subtract), [t_v1, t_m2])
        t_sd = op("scalar", lambda e, V=V: e.activation(
            V[:, 5:6], V[:, 4:5], AF.Sqrt, bias=epsv[:, 0:1]), [t_var, t_eps])
        t_rstd = op("vector", lambda e, V=V, c=c: e.reciprocal(
            stats[:, 16 + c:17 + c], V[:, 5:6]), [t_sd])
        t_u = op("vector", lambda e, V=V: e.tensor_scalar(
            V[:, 6:7], V[:, 4:5], 1e-5, None, ALU.add), [t_var])
        t_ur = op("vector", lambda e, V=V: e.reciprocal(V[:, 7:8], V[:, 6:7]), [t_u])
        t_r2 = op("vector", lambda e, V=V: e.tensor_tensor(
            V[:, 8:9], V[:, 4:5], V[:, 7:8], ALU.mult), [t_ur])
        t_r2e = op("scalar", lambda e, V=V: e.activation(
            V[:, 9:10], V[:, 8:9], AF.Sqrt, bias=epsv[:, 1:2]), [t_r2, t_eps])
        t_rstd2 = op("vector", lambda e, V=V: e.reciprocal(V[:, 10:11], V[:, 9:10]),
                     [t_r2e])
        t_xn = op("vector", lambda e, b=b, xn=xn, c=c: e.tensor_scalar(
            xn, b, stats[:, c:c + 1], stats[:, 16 + c:17 + c],
            ALU.subtract, ALU.mult), [t_rstd, t_nfree[c % 2]])
        t_xg = op("gpsimd", lambda e, b=b, xn=xn: e.tensor_tensor(
            b, xn, c_gb, ALU.mult), [t_xn, t_gb, t_s1, t_s2])
        t_am = op("vector", lambda e, b=b, V=V: e.tensor_reduce(
            V[:, 11:12], b, AXL.X, ALU.max, apply_absolute_value=True), [t_xg])
        t_amx = op("vector", lambda e, V=V: e.tensor_tensor(
            V[:, 12:13], V[:, 11:12], V[:, 10:11], ALU.mult), [t_am, t_rstd2])
        t_ap = op("vector", lambda e, V=V: e.tensor_scalar(
            V[:, 13:14], V[:, 12:13], 1e-5, None, ALU.add), [t_amx])
        t_ar = op("vector", lambda e, V=V: e.reciprocal(V[:, 14:15], V[:, 13:14]),
                  [t_ap])
        t_as = op("vector", lambda e, V=V, c=c: e.tensor_scalar(
            stats[:, 48 + c:49 + c], V[:, 14:15], 127.0, None, ALU.mult), [t_ar])
        t_k1 = op("vector", lambda e, V=V, c=c: e.tensor_tensor(
            V[:, 15:16], stats[:, 16 + c:17 + c], V[:, 10:11], ALU.mult),
            [t_rstd, t_rstd2])
        t_K[c] = op("vector", lambda e, V=V, c=c: e.tensor_tensor(
            stats[:, 32 + c:33 + c], V[:, 15:16], stats[:, 48 + c:49 + c],
            ALU.mult), [t_k1, t_as])
        t_xfree[c % 2] = t_am
        t_nfree[c % 2] = t_xg
        wb_ = wstb[c % 2]
        t_wl = dma("gpsimd", wb_, wffnT[c * 128:(c + 1) * 128, :],
                   waits=[t_wslab[c % 2]])
        t_abs = op("scalar", lambda e, wb_=wb_: e.activation(
            wsink, wb_, AF.Abs, accum_out=wsc[:, 1:2]), [t_wl, t_wacc])
        if c == 0:
            t_wacc = op("vector", lambda e: e.tensor_copy(
                wsc[:, 2:3], wsc[:, 1:2]), [t_abs])
        else:
            t_wacc = op("vector", lambda e: e.tensor_tensor(
                wsc[:, 2:3], wsc[:, 2:3], wsc[:, 1:2], ALU.add), [t_abs])
        t_wslab[c % 2] = t_wacc
    t_p1done = t_K[NC - 1]

    if _PH <= 1:
        dma("sync", out[0:128, 0:64], stats[:], sem="s_out")
        P.add("gpsimd", lambda e: e.nop(), [P.tok("s_out")])
        P.flush(block)
        blk_guard.__exit__(None, None, None)
        for g in reversed(guards):
            g.__exit__(None, None, None)
        return nc
    t_rw1_ph = [None]
    t_wred = op("tensor", lambda e: e.matmul(
        psA[0:1, 0:1], wsc[:, 2:3], c_oc[:], start=True, stop=True),
        [t_wacc, t_oc])
    t_ws1 = op("vector", lambda e: e.tensor_scalar(
        rows[0:1, 2560:2561], psA[0:1, 0:1], 1.0 / (E * E), 1e-5,
        ALU.mult, ALU.add), [t_wred])
    t_ws2 = op("vector", lambda e: e.reciprocal(
        rows[0:1, 2561:2562], rows[0:1, 2560:2561]), [t_ws1])
    t_wsb = op("tensor", lambda e: e.matmul(
        psA[:, 0:1], c_on[:], rows[0:1, 2561:2562], start=True, stop=True),
        [t_ws2, t_on])
    t_wsv = op("scalar", lambda e: e.copy(wsc[:, 0:1], psA[:, 0:1]), [t_wsb])
    t_f1 = op("vector", lambda e: e.tensor_scalar(
        stats[:, 0:16], stats[:, 48:64], wsc[:, 0:1], None, ALU.mult),
        [t_wsv, t_p1done, t_rw1_ph[0]])
    t_ffsc = op("vector", lambda e: e.reciprocal(stats[:, 48:64], stats[:, 0:16]),
                [t_f1])


    if _PH <= 2:
        dma("sync", out[0:128, 0:64], stats[:], sem="s_out")
        P.add("gpsimd", lambda e: e.nop(), [P.tok("s_out")])
        P.flush(block)
        blk_guard.__exit__(None, None, None)
        for g in reversed(guards):
            g.__exit__(None, None, None)
        return nc
    t_rw1 = dma("sync", row_scr[:, 0:48], stats[:, 0:48], waits=[t_p1done])
    t_rw1_ph[0] = t_rw1
    t_bc = []
    t_rdfree = [None, None]
    t_mmq = [None, None]
    kq = 0
    for j in range(3):
        tev = None
        for q in range(4):
            sl = rows[0:1, (kq % 2) * 512:(kq % 2) * 512 + 512]
            src_ap = row_scr.rearrange("p c -> c p")[
                j * 16 + q * 4:j * 16 + (q + 1) * 4, :].unsqueeze(0)
            t_rd = dma("gpsimd", sl.rearrange("a (c p) -> a c p", c=4), src_ap,
                       waits=[t_rw1, t_rdfree[kq % 2]],
                       allow_slow_non_contiguous=True)
            tmm = op("tensor", lambda e, kq=kq, sl=sl: e.matmul(
                psB[:, (kq % 2) * 512:(kq % 2) * 512 + 512], c_on[:], sl,
                start=True, stop=True),
                [t_rd, t_on, t_wsb, t_mmq[kq % 2]])
            t_rdfree[kq % 2] = tmm
            tev = op("scalar", lambda e, j=j, q=q, kq=kq: e.copy(
                bcast[:, j * 2048 + q * 512:j * 2048 + (q + 1) * 512],
                psB[:, (kq % 2) * 512:(kq % 2) * 512 + 512]), [tmm])
            t_mmq[kq % 2] = tev
            kq += 1
        t_bc.append(tev)
    m_b = bcast[:, 0:2048]
    rstd_b = bcast[:, 2048:4096]
    K_b = bcast[:, 4096:6144]


    if _PH <= 3:
        dma("sync", out[0:128, 0:64], stats[:], sem="s_out")
        P.add("gpsimd", lambda e: e.nop(), [P.tok("s_out")])
        P.flush(block)
        blk_guard.__exit__(None, None, None)
        for g in reversed(guards):
            g.__exit__(None, None, None)
        return nc
    t_wr = [t_nfree[0], t_nfree[1]]
    for i in range(NE):
        b = xnbuf[i % 2]
        t_wl = dma("sync", b, wffnT[i * 128:(i + 1) * 128, :], waits=[t_wr[i % 2]])
        ti8 = op("vector", lambda e, b=b: e.tensor_scalar(
            junki[:, 0:2048], b, wsc[:, 0:1], None, ALU.mult),
            [t_wl, t_wsv] + ([t_nfree[0], t_nfree[1]] if i < 2 else []))
        t_wr[i % 2] = op("vector", lambda e, i=i: e.tensor_scalar(
            arW[:, i * 2048:(i + 1) * 2048], junki[:, 0:2048],
            1.0, -1.0, ALU.min, ALU.max), [ti8])
    t_wrdone = op("vector", lambda e: e.tensor_copy(wsc[:, 1:2], wsc[:, 0:1]),
                  [t_wr[0], t_wr[1]])


    if _PH <= 4:
        dma("sync", out[0:128, 0:64], stats[:], sem="s_out")
        P.add("gpsimd", lambda e: e.nop(), [P.tok("s_out")])
        P.flush(block)
        blk_guard.__exit__(None, None, None)
        for g in reversed(guards):
            g.__exit__(None, None, None)
        return nc
    t_ffn_ev = {}
    t_ps_free = {0: t_wsv, 1: None}
    t_mm_prev = None
    for half in range(2):
        hs = slice(half * 1024, (half + 1) * 1024)
        t_xq = [None] * NE
        for e in range(NE):
            t_t1 = op("vector", lambda e_, e=e, half=half, hs=hs: e_.tensor_tensor(
                arGb[:, 0:1024],
                arX[:, e * 2048 + half * 1024:e * 2048 + half * 1024 + 1024],
                m_b[:, hs], ALU.subtract),
                [t_bc[0], t_xT[e], t_wrdone, t_wslab[0], t_wslab[1]])
            t_xq[e] = op("vector", lambda e_, e=e, hs=hs: e_.scalar_tensor_tensor(
                arQi[:, e * 1024:(e + 1) * 1024], arGb[:, 0:1024],
                c_gt[:, e:e + 1], K_b[:, hs], ALU.mult, ALU.mult),
                [t_t1, t_bc[2], t_consts])
        for tci in range(8):
            tc = half * 8 + tci
            t_lhs = op("vector", lambda e_, tci=tci: e_.tensor_copy(
                arGb[:, 4096:6144].rearrange("p (e d) -> p e d", e=16),
                arQi[:, 0:16384].rearrange("p (e t) -> p e t", e=16)
                [:, :, tci * 128:(tci + 1) * 128]),
                t_xq + [t_mm_prev])
            for oh in range(4):
                pp = psA if oh % 2 == 0 else psB
                fns = [lambda eng, e=e, oh=oh, pp=pp: eng.matmul(
                    pp[:, 0:512], arGb[:, 4096 + e * 128:4096 + (e + 1) * 128],
                    arW[:, e * 2048 + oh * 512:e * 2048 + (oh + 1) * 512],
                    start=(e == 0), stop=(e == NE - 1)) for e in range(NE)]
                tmm = op_group("tensor", fns,
                               [t_lhs, t_wrdone, t_ps_free[oh % 2], t_mm_prev])
                t_mm_prev = tmm
                t_ps_free[oh % 2] = op("vector",
                                       lambda e_, oh=oh, pp=pp, tc=tc: e_.tensor_scalar(
                                           junk[:, oh * 512:(oh + 1) * 512], pp[:, 0:512],
                                           stats[:, 48 + tc:49 + tc], None, ALU.mult),
                                       [tmm, t_ffsc,
                                        t_ffn_ev.get(tc - 1)])
            t_ffn_ev[tc] = dma("sync", ffn_scr[tc * 128:(tc + 1) * 128, :],
                               junk, waits=[t_ps_free[0], t_ps_free[1]])
    t_ffn_done = t_ffn_ev[15]


    if _PH <= 5:
        dma("gpsimd", out, ffn_scr, waits=[t_ffn_done], sem="s_out")
        dma("sync", out[0:128, 0:64], stats[:, 0:64] if False else stats[:], sem="s_out") if False else None
        P.add("gpsimd", lambda e: e.nop(), [P.tok("s_out")])
        P.flush(block)
        blk_guard.__exit__(None, None, None)
        for g in reversed(guards):
            g.__exit__(None, None, None)
        return nc
    t_xnT = None
    for e in range(NE):
        t_a = op("vector", lambda e_, e=e: e_.tensor_tensor(
            arGb[:, 4096:6144], arX[:, e * 2048:(e + 1) * 2048], m_b[:],
            ALU.subtract), [t_bc[1], t_mm_prev if e == 0 else None])
        t_xnT = op("vector", lambda e_, e=e: e_.tensor_tensor(
            arX[:, e * 2048:(e + 1) * 2048], arGb[:, 4096:6144], rstd_b[:],
            ALU.mult), [t_a])


    if _PH <= 6:
        dma("sync", out[0:128, 0:64], stats[:], sem="s_out")
        P.add("gpsimd", lambda e: e.nop(), [P.tok("s_out")])
        P.flush(block)
        blk_guard.__exit__(None, None, None)
        for g in reversed(guards):
            g.__exit__(None, None, None)
        return nc
    c_cosR = arQf[:, 4480:5504]
    c_sinR = arQf[:, 5504:6528]
    t_cosR = dma("sync", c_cosR, cosc, waits=[t_xnT])
    t_sinR = dma("sync", c_sinR, sinc, waits=[t_xnT])

    stg = [arQ[:, 0:3072], arQ[:, 3072:6144]]
    kstg = [arQ[:, 6144:7168], arQ[:, 7168:8192]]
    rsc = [arQ[:, 8192:8448], arQ[:, 8448:8704], arQ[:, 8704:8960]]
    t_wq = [None, None]
    t_wq_free = [t_mm_prev, t_mm_prev]
    t_stg_free = [None, None]
    t_kp_free = [None, None]
    t_last_spill = None
    for pair in range(8):
        wb = arW[:, (pair % 2) * 12288:(pair % 2) * 12288 + 12288]
        t_wq[pair % 2] = dma("gpsimd", wb.rearrange("p (e n) -> p e n", e=16),
                             wqkv[pair].rearrange("(e p) n -> p e n", p=128),
                             waits=[t_wq_free[pair % 2],
                                    t_mm_prev if pair < 2 else None])
        for g4 in range(4):
            st = stg[g4 % 2]
            kp = kstg[g4 % 2]
            stv = st.rearrange("p (c h x) -> p c h x", c=4, h=2)
            t_evs = []
            for c4 in range(4):
                c = g4 * 4 + c4
                pp = psA if c % 2 == 0 else psB
                fns = []
                for e in range(NE):
                    for hh in range(2):
                        fns.append(lambda eng, e=e, hh=hh, pp=pp, c=c, wb=wb:
                                   eng.matmul(
                                       pp[:, hh * 512:hh * 512 + 384],
                                       arX[:, e * 2048 + c * 128:e * 2048 + (c + 1) * 128],
                                       wb[:, e * 768 + hh * 384:e * 768 + (hh + 1) * 384],
                                       start=(e == 0), stop=(e == NE - 1)))
                tmm = op_group("tensor", fns,
                               [t_xnT, t_wq[pair % 2], t_ps_free[c % 2]])
                if c4 % 2 == 0:
                    t_ev = op("scalar", lambda e_, pp=pp, st=st, c4=c4: e_.copy(
                        st[:, c4 * 768:(c4 + 1) * 768]
                        .rearrange("p (h x) -> p h x", h=2),
                        pp[:].rearrange("p (h x) -> p h x", h=2)[:, :, 0:384]),
                        [tmm, t_stg_free[g4 % 2]])
                else:
                    t_ev = op("vector", lambda e_, pp=pp, st=st, c4=c4: e_.tensor_copy(
                        st[:, c4 * 768:(c4 + 1) * 768]
                        .rearrange("p (h x) -> p h x", h=2),
                        pp[:].rearrange("p (h x) -> p h x", h=2)[:, :, 0:384]),
                        [tmm, t_stg_free[g4 % 2]])
                t_ps_free[c % 2] = t_ev
                t_evs.append(t_ev)
            stm = st.rearrange("p (m x) -> p m x", m=8)
            cosv = c_cosR.rearrange("p (c f) -> p c f", f=32)[:, g4 * 8:(g4 + 1) * 8, :]
            sinv = c_sinR.rearrange("p (c f) -> p c f", f=32)[:, g4 * 8:(g4 + 1) * 8, :]
            sA = rsc[0].rearrange("p (m f) -> p m f", m=8)
            sB = rsc[1].rearrange("p (m f) -> p m f", m=8)
            sC = rsc[2].rearrange("p (m f) -> p m f", m=8)
            t_rot = t_evs[-1]
            for base in (0, 128):
                x1 = stm[:, :, base:base + 32]
                x2 = stm[:, :, base + 64:base + 96]
                tA = op("vector", lambda e, x1=x1, cosv=cosv, sinv=sinv: e.tensor_tensor(
                    sA, x1, sinv, ALU.mult), [t_rot] + t_evs + [t_cosR, t_sinR])
                tB = op("vector", lambda e, x1=x1, cosv=cosv: e.tensor_tensor(
                    sB, x1, cosv, ALU.mult), [tA])
                tC = op("vector", lambda e, x2=x2, sinv=sinv: e.tensor_tensor(
                    sC, x2, sinv, ALU.mult), [tB])
                tY1 = op("vector", lambda e, x1=x1: e.tensor_tensor(x1, sB, sC, ALU.add), [tC])
                tD = op("vector", lambda e, x2=x2, cosv=cosv: e.tensor_tensor(
                    sB, x2, cosv, ALU.mult), [tY1])
                t_rot = op("vector", lambda e, x2=x2: e.tensor_tensor(x2, sB, sA, ALU.subtract), [tD])
            t_kp = t_rot
            for mh in range(8):
                head = pair * 2 + (mh % 2)
                t_kp = op("vector", lambda e, kp=kp, mh=mh, head=head, stm=stm: e.tensor_scalar(
                    kp[:, mh * 128:(mh + 1) * 128],
                    stm[:, mh, 128:256], c_kd[:, head:head + 1], None, ALU.mult),
                    [t_rot, t_kp_free[g4 % 2]])
            t_sp = []
            for scr, base in ((q_scr, 0), (k_scr, 128), (v_scr, 256)):
                for hh in range(2):
                    dst = scr[pair * 2 + hh, g4 * 512:(g4 + 1) * 512, :] \
                        .rearrange("(c p) d -> p c d", p=128)
                    t_sp.append(dma("sync", dst, stv[:, :, hh, base:base + 128],
                                    waits=[t_rot]))
            kpv = kp.rearrange("p (c h d) -> p c h d", c=4, h=2)
            for hh in range(2):
                dstp = kp_scr[pair * 2 + hh, g4 * 512:(g4 + 1) * 512, :] \
                    .rearrange("(c p) d -> p c d", p=128)
                t_sp.append(dma("sync", dstp, kpv[:, :, hh, :], waits=[t_kp]))
            t_stg_free[g4 % 2] = t_sp[7]
            t_kp_free[g4 % 2] = t_sp[7]
            t_last_spill = t_sp[7]
        t_wq_free[pair % 2] = t_last_spill
    t_p2done = t_last_spill


    if _PH <= 7:
        dma("sync", out[0:128, 0:64], stats[:], sem="s_out")
        P.add("gpsimd", lambda e: e.nop(), [P.tok("s_out")])
        P.flush(block)
        blk_guard.__exit__(None, None, None)
        for g in reversed(guards):
            g.__exit__(None, None, None)
        return nc
    t_dm = dma("sync", c_dm.rearrange("p (h c) -> p h c", h=H),
               dmaskT_in.rearrange("h s c -> s h c"), waits=[t_p2done])
    t_qdbl = dma("sync", c_qdb.rearrange("p (h c) -> p h c", h=H),
                 qdb_in.rearrange("h d c -> d h c"), waits=[t_p2done])

    q_sb = arQ[:, 0:2048]
    k_sb = arQ[:, 2048:4096]
    kp_sb = arQ[:, 4096:6144]
    v_sb = arQ[:, 6144:8192]
    qT = arQ[:, 8192:10240]
    kT = arQ[:, 10240:12288]
    uperm = arF[:, 0:2048]
    hlist = list(range(H - CH, H)) + list(range(H - CH))

    t_rd_free = [t_p2done] * 4
    t_zs_free = []
    t_sd0_free = None
    t_psC_free = t_p2done
    t_psT_free = {"q": t_ps_free[0], "k": t_ps_free[1]}
    t_attn_ev = {}
    t_ssb_free = t_p2done
    t_uperm_free = None
    t_qp_free = None
    t_cc_st = []

    for hi, h in enumerate(hlist):
        kept = h >= H - CH
        qpT = arQ[:, 12288:14336]
        t_q = dma("sync", q_sb.rearrange("p (l d) -> p l d", l=16),
                  q_scr[h].rearrange("(l p) d -> p l d", p=128),
                  waits=[t_rd_free[0]])
        t_k = dma("sync", k_sb.rearrange("p (l d) -> p l d", l=16),
                  k_scr[h].rearrange("(l p) d -> p l d", p=128),
                  waits=[t_rd_free[1]])
        t_kp2 = dma("sync", kp_sb.rearrange("p (l d) -> p l d", l=16),
                    kp_scr[h].rearrange("(l p) d -> p l d", p=128),
                    waits=[t_rd_free[2]])
        t_v = dma("sync", v_sb.rearrange("p (l d) -> p l d", l=16),
                  v_scr[h].rearrange("(l p) d -> p l d", p=128),
                  waits=[t_rd_free[3]])

        fns = [lambda eng, l=l: eng.matmul(
            psC[:, l * 128:(l + 1) * 128], kp_sb[:, l * 128:(l + 1) * 128],
            v_sb[:, l * 128:(l + 1) * 128], start=True, stop=True)
            for l in range(16)]
        t_U = op_group("tensor", fns, [t_kp2, t_v, t_psC_free])
        t_up = op("scalar", lambda e: e.activation(
            uperm.rearrange("p (e l) -> p e l", l=16),
            psC[:].rearrange("p (l e) -> p e l", l=16), AF.Copy),
            [t_U, t_uperm_free])
        sd0m = arF[:, 2048:4096]
        t_d0a = op("gpsimd", lambda e, h=h: e.memset(sd0m, float(_RC[h])),
                   [t_sd0_free])
        t_d0b = op("gpsimd", lambda e: e.memset(
            sd0m.rearrange("p (e l) -> p e l", l=16)[:, :, 0], 0.0), [t_d0a])
        t_zs = op("vector", lambda e: e.tensor_tensor_scan(
            zs[:], sd0m, uperm[:] if False else uperm,
            0.0, ALU.mult, ALU.add), [t_up, t_d0b] + list(t_zs_free))
        t_uperm_free = t_zs
        t_sd0_free = t_zs

        t_tr_done = {}
        for tn, src, dst, psx, t_src in (("q", q_sb, qT, psAb, t_q),
                                         ("k", k_sb, kT, psBb, t_k)):
            t_prev = t_psT_free[tn]
            for bidx in range(2):
                fns = [lambda eng, l=l, src=src, psx=psx: eng.transpose(
                    psx[:, (l % 8) * 128:(l % 8 + 1) * 128],
                    src[:, l * 128:(l + 1) * 128], c_id[:])
                    for l in range(bidx * 8, (bidx + 1) * 8)]
                t_tr = op_group("tensor", fns, [t_src, t_id, t_prev])
                if tn == "q":
                    t_prev = op("vector", lambda e, bidx=bidx, dst=dst, psx=psx:
                                e.tensor_copy(dst[:, bidx * 1024:(bidx + 1) * 1024],
                                              psx[:, 0:1024]), [t_tr])
                else:
                    t_prev = op("scalar", lambda e, bidx=bidx, dst=dst, psx=psx:
                                e.copy(dst[:, bidx * 1024:(bidx + 1) * 1024],
                                       psx[:, 0:1024]), [t_tr])
            t_tr_done[tn] = t_prev
            t_psT_free[tn] = t_prev

        t_qp = op("vector", lambda e, qpT=qpT, h=h: e.tensor_tensor(
            qpT.rearrange("p (l c) -> p l c", l=16),
            qT.rearrange("p (l c) -> p l c", l=16),
            c_qdb[:, h * 128:(h + 1) * 128].unsqueeze(1)
            .broadcast_to([128, 16, 128]), ALU.mult),
            [t_tr_done["q"], t_qdbl, t_qp_free])

        fns = [lambda eng, l=l: eng.matmul(
            psC[:, l * 128:(l + 1) * 128], kT[:, l * 128:(l + 1) * 128],
            qT[:, l * 128:(l + 1) * 128], start=True, stop=True)
            for l in range(16)]
        t_S = op_group("tensor", fns, [t_tr_done["q"], t_tr_done["k"], t_up])
        t_mask = op("vector", lambda e, h=h: e.tensor_tensor(
            ssb[:].rearrange("p (l c) -> p l c", l=16),
            psC[:].rearrange("p (l c) -> p l c", l=16),
            c_dm[:, h * 128:(h + 1) * 128].unsqueeze(1)
            .broadcast_to([128, 16, 128]), ALU.mult), [t_S, t_ssb_free, t_dm])

        fns = []
        for l in range(16):
            fns.append(lambda eng, l=l: eng.matmul(
                psC[:, l * 128:(l + 1) * 128], v_sb[:, l * 128:(l + 1) * 128],
                ssb[:, l * 128:(l + 1) * 128], start=True, stop=(l == 0),
                skip_group_check=True))
            if l >= 1:
                fns.append(lambda eng, l=l, qpT=qpT: eng.matmul(
                    psC[:, l * 128:(l + 1) * 128],
                    zs[:].rearrange("p (e l) -> p e l", l=16)[:, :, l - 1],
                    qpT[:, l * 128:(l + 1) * 128], start=False, stop=True,
                    skip_group_check=True))
        t_O = op_group("tensor", fns, [t_mask, t_zs, t_qp])
        t_ssb_free = t_O
        t_qp_free = t_O

        t_aev = op("scalar", lambda e, h=h: e.copy(
            arW[:, h * 2048:(h + 1) * 2048], psC[:]), [t_O])
        t_attn_ev[h] = t_aev
        t_psC_free = t_aev

        if kept:
            t_st = op("scalar", lambda e, h=h: e.copy(
                ccst[:, (h - (H - CH)) * 128:(h - (H - CH) + 1) * 128],
                zs[:].rearrange("p (e l) -> p e l", l=16)[:, :, 15]), [t_zs])
            t_cc_st.append(t_st)
            t_zs_free = [t_O, t_st]
        else:
            t_zs_free = [t_O]

        t_rd_free = [t_tr_done["q"], t_tr_done["k"], t_U, t_O]


    if _PH <= 8:
        w8 = [t_psC_free, t_uperm_free, t_qp_free] + list(t_zs_free)
        dma("gpsimd", out[0:128, :], zs, waits=w8, sem="s_out")
        dma("sync", out[128:256, :], uperm, waits=w8, sem="s_out")
        dma("gpsimd", out[256:384, :], kp_sb, waits=w8, sem="s_out")
        dma("gpsimd", out[384:512, :], v_sb, waits=w8, sem="s_out")
        dma("gpsimd", out[512:640, :], q_sb, waits=w8, sem="s_out")
        dma("gpsimd", out[640:768, :], k_sb, waits=w8, sem="s_out")
        dma("gpsimd", out[768:896, :], qT, waits=w8, sem="s_out")
        dma("gpsimd", out[896:1024, :], arQ[:, 12288:14336], waits=w8, sem="s_out")
        dma("gpsimd", out[1024:1152, :], ssb, waits=w8, sem="s_out")
        dma("gpsimd", out[1152:1280, 0:1536], ccst.bitcast(BF16)[:, 0:1536], waits=w8, sem="s_out")
        dma("sync", out[1280:1408, 0:64], stats[:], sem="s_out")
        P.add("gpsimd", lambda e: e.nop(), [P.tok("s_out")])
        P.flush(block)
        blk_guard.__exit__(None, None, None)
        for g in reversed(guards):
            g.__exit__(None, None, None)
        return nc
    t_ccw = dma("gpsimd", cc_i.rearrange("(a p) d -> p a d", p=128),
                ccst[:].rearrange("p (a d) -> p a d", a=CH), waits=t_cc_st)
    hcc, t_cc = P.inc("s_cc", 1)
    P.add("gpsimd", lambda e: e.collective_compute(
        "AllGather", ALU.bypass,
        replica_groups=[[0, 1, 2, 3, 4, 5, 6, 7]],
        ins=[cc_i], outs=[cc_o]).then_inc(hcc, 1), [t_ccw])

    t_last_attn = t_attn_ev[hlist[-1]]
    t_comb = [None] * CH
    t_ccr_free = None
    for r in range(8):
        buf = ccg[:, (r % 2) * CH * 128:(r % 2) * CH * 128 + CH * 128]
        t_ccr = dma("sync", buf.rearrange("p (a d) -> p a d", a=CH),
                    cc_o[r * CH * 128:(r + 1) * CH * 128, :]
                    .rearrange("(a p) d -> p a d", p=128),
                    waits=[("s_cc", 1), t_ccr_free])
        tlast = None
        for h6 in range(CH):
            if r == 0:
                tc2 = op("vector", lambda e, h6=h6, buf=buf: e.tensor_scalar(
                    arG[:, h6 * 128:(h6 + 1) * 128],
                    buf[:, h6 * 128:(h6 + 1) * 128],
                    c_cf[:, h6 * 8:h6 * 8 + 1], None, ALU.mult),
                    [t_ccr, t_last_attn])
            else:
                tc2 = op("vector", lambda e, h6=h6, buf=buf, r=r: e.scalar_tensor_tensor(
                    arG[:, h6 * 128:(h6 + 1) * 128],
                    buf[:, h6 * 128:(h6 + 1) * 128],
                    c_cf[:, h6 * 8 + r:h6 * 8 + r + 1],
                    arG[:, h6 * 128:(h6 + 1) * 128], ALU.mult, ALU.add),
                    [t_ccr, t_comb[h6]])
            t_comb[h6] = tc2
            tlast = tc2
        t_ccr_free = tlast
    t_kvb = op("vector", lambda e: e.tensor_copy(kvini[:], arG[:, 0:CH * 128]),
               t_comb)

    t_corr_ev = {}
    t_kvs_free = [None, None]
    t_psC2 = t_psC_free
    t_qrd_free = t_rd_free[0]
    t_trc_free = t_psT_free["q"]
    for h6 in range(CH):
        h = (H - CH) + h6
        t_qr = dma("sync", q_sb.rearrange("p (l d) -> p l d", l=16),
                   q_scr[h].rearrange("(l p) d -> p l d", p=128),
                   waits=[t_qrd_free])
        t_prev = t_trc_free
        for bidx in range(2):
            fns = [lambda eng, l=l: eng.transpose(
                psAb[:, (l % 8) * 128:(l % 8 + 1) * 128],
                q_sb[:, l * 128:(l + 1) * 128], c_id[:])
                for l in range(bidx * 8, (bidx + 1) * 8)]
            t_tr = op_group("tensor", fns, [t_qr, t_prev])
            t_prev = op("vector", lambda e, bidx=bidx: e.tensor_copy(
                qT[:, bidx * 1024:(bidx + 1) * 1024], psAb[:, 0:1024]), [t_tr])
        t_trc_free = t_prev
        t_mm_toks = []
        for l in range(16):
            ksl = kvs[:, (l % 2) * 128:(l % 2) * 128 + 128]
            t_s = op("vector", lambda e, h6=h6, ksl=ksl, l=l, h=h: e.tensor_scalar(
                ksl, kvini[:, h6 * 128:(h6 + 1) * 128],
                float(_RC[h] ** l), None, ALU.mult),
                [t_kvb, t_kvs_free[l % 2]])
            t_mm = op("tensor", lambda eng, l=l, ksl=ksl: eng.matmul(
                psC[:, l * 128:(l + 1) * 128], ksl,
                qT[:, l * 128:(l + 1) * 128], start=True, stop=True),
                [t_s, t_prev] + ([t_psC2] if l == 0 else []))
            t_kvs_free[l % 2] = t_mm
            t_mm_toks.append(t_mm)
        t_qrd_free = t_mm_toks[-1]
        t_e1 = op("vector", lambda e, h=h: e.tensor_tensor(
            ssb[:].rearrange("p (l c) -> p l c", l=16),
            psC[:].rearrange("p (l c) -> p l c", l=16),
            c_qdb[:, h * 128:(h + 1) * 128].unsqueeze(1)
            .broadcast_to([128, 16, 128]), ALU.mult),
            [t_mm_toks[-1], t_ssb_free])
        t_add = op("vector", lambda e, h=h: e.tensor_tensor(
            arW[:, h * 2048:(h + 1) * 2048], arW[:, h * 2048:(h + 1) * 2048],
            ssb, ALU.add), [t_e1, t_attn_ev[h]])
        t_ssb_free = t_add
        t_psC2 = t_add
        t_corr_ev[h] = t_add


    if _PH <= 9:
        w9 = [t_psC2, t_kvb]
        dma("gpsimd", out[0:128, 0:768], kvini, waits=w9, sem="s_out")
        dma("sync", out[128:256, 0:768], ccst, waits=w9, sem="s_out")
        dma("sync", out[256:384, 0:1536], ccg[:, 0:1536], waits=w9, sem="s_out")
        dma("gpsimd", out[384:512, :].rearrange("(a p) d -> p a d", p=128)
            .rearrange("p a d -> p (a d)") if False else out[384:512, :], arW[:, 0:2048], waits=w9, sem="s_out")
        dma("gpsimd", out[512:640, :], arW[:, 15 * 2048:16 * 2048], waits=w9, sem="s_out")
        dma("gpsimd", out[640:768, :], arW[:, 12 * 2048:13 * 2048], waits=w9, sem="s_out")
        dma("sync", out[768:896, 0:64], stats[:], sem="s_out")
        P.add("gpsimd", lambda e: e.nop(), [P.tok("s_out")])
        P.flush(block)
        blk_guard.__exit__(None, None, None)
        for g in reversed(guards):
            g.__exit__(None, None, None)
        return nc
    sqb = arQ[:, 0:2048]
    t_ss = [None] * 4
    t_sq_free = t_psC2
    for h in range(H):
        w_h = [t_attn_ev[h], t_corr_ev.get(h), t_sq_free]
        t_sq = op("vector", lambda e, h=h: e.tensor_tensor(
            sqb, arW[:, h * 2048:(h + 1) * 2048],
            arW[:, h * 2048:(h + 1) * 2048], ALU.mult), w_h)
        for q in range(4):
            pp, off = [(psA, 0), (psA, 512), (psB, 0), (psB, 512)][q]
            t_ss[q] = op("tensor", lambda e, q=q, pp=pp, off=off, h=h: e.matmul(
                pp[0:1, off:off + 512], c_ob[:], sqb[:, q * 512:(q + 1) * 512],
                start=(h == 0), stop=(h == H - 1)),
                [t_sq, t_ob] +
                ([t_psT_free["q"], t_psT_free["k"], t_psC2] if h == 0 else []))
        t_sq_free = t_ss[3]
    t_c1s = []
    for q in range(4):
        pp, off = [(psA, 0), (psA, 512), (psB, 0), (psB, 512)][q]
        t_c1s.append(op("vector", lambda e, pp=pp, off=off, q=q: e.tensor_copy(
            rows[0:1, (q % 2) * 512:(q % 2) * 512 + 512]
            if False else rows[0:1, 0:512].tensor[0:1, q * 512:(q + 1) * 512]
            if False else rows[0:1, 0:2048][0:1, q * 512:(q + 1) * 512],
            pp[0:1, off:off + 512]), [t_ss[q]]))
    t_rb = []
    for q in range(4):
        sl = rows[0:1, q * 512:(q + 1) * 512]
        t_r1 = op("vector", lambda e, sl=sl: e.tensor_scalar(
            sl, sl, 1.0 / E, None, ALU.mult), [t_c1s[q]])
        tmpsl = rows[0:1, 2048:2560]
        t_r2b = op("scalar", lambda e, sl=sl, tmpsl=tmpsl: e.activation(
            tmpsl, sl, AF.Sqrt, bias=epsv[0:1, 0:1]),
            [t_r1] + ([t_rb[q - 1][1]] if q >= 1 else []))
        t_r3 = op("vector", lambda e, sl=sl, tmpsl=tmpsl: e.reciprocal(sl, tmpsl),
                  [t_r2b])
        tmm = op("tensor", lambda e, q=q, sl=sl: e.matmul(
            psB[:, (q % 2) * 512:(q % 2) * 512 + 512], c_on[:], sl,
            start=True, stop=True),
            [t_r3] + t_c1s + ([t_rb[q - 2][0]] if q >= 2 else []))
        tev = op("scalar", lambda e, q=q: e.copy(
            arG[:, 2048 + q * 512:2048 + (q + 1) * 512],
            psB[:, (q % 2) * 512:(q % 2) * 512 + 512]), [tmm])
        t_rb.append((tev, t_r2b))
    t_rb = [t[0] for t in t_rb]
    rstdb = arG[:, 2048:4096]
    t_attn_n = None
    for h in range(H):
        t_n1 = op("vector", lambda e, h=h: e.scalar_tensor_tensor(
            sqb, arW[:, h * 2048:(h + 1) * 2048],
            c_rw[:, h:h + 1], rstdb, ALU.mult, ALU.mult),
            [t_rb[-1], t_attn_ev[h], t_corr_ev.get(h), t_attn_n])
        t_attn_n = op("vector", lambda e, h=h: e.tensor_copy(
            arW[:, h * 2048:(h + 1) * 2048], sqb), [t_n1])


    if _PH <= 10:
        for a_ in range(16):
            dma("gpsimd", out[a_ * 128:(a_ + 1) * 128, :],
                arW[:, a_ * 2048:(a_ + 1) * 2048], waits=[t_attn_n], sem="s_out")
        P.add("gpsimd", lambda e: e.nop(), [P.tok("s_out")])
        P.flush(block)
        blk_guard.__exit__(None, None, None)
        for g in reversed(guards):
            g.__exit__(None, None, None)
        return nc
    wob = [arQ[:, 0:8192], arQ[:, 8192:16384]]
    fft4 = arX[:, 2048:4096]
    xft4 = arF[:, 0:2048]
    t1f4 = arF[:, 2048:4096]
    outst = arG[:, 0:2048]
    t_wo = [None, None]
    t_wo_free = [t_sq_free, t_sq_free]
    t_prev_st = t_psC2
    t_prev_1 = [t_rb[-1], t_kvb]
    t_outs = []
    for ot in range(4):
        t_wo[ot % 2] = dma("gpsimd", wob[ot % 2].rearrange("p (e n) -> p e n", e=16),
                           wo_t[ot].rearrange("(e p) n -> p e n", p=128),
                           waits=[t_attn_n if ot < 2 else None, t_wo_free[ot % 2]])
        for tq in range(4):
            r0 = tq * 4 * 128
            t_ff = dma("sync", fft4.rearrange("p (c d) -> p c d", c=4),
                       ffn_scr[r0:r0 + 512, ot * 512:(ot + 1) * 512]
                       .rearrange("(c p) d -> p c d", p=128),
                       waits=[t_prev_1[0], t_ffn_done])
            t_xf = dma("sync", xft4.rearrange("p (c d) -> p c d", c=4),
                       x_tok[r0:r0 + 512, ot * 512:(ot + 1) * 512]
                       .rearrange("(c p) d -> p c d", p=128),
                       waits=[t_prev_1[1]])
            fns = []
            for c4 in range(4):
                tc = tq * 4 + c4
                for e in range(NE):
                    fns.append(lambda eng, e=e, tc=tc, ot=ot, c4=c4: eng.matmul(
                        psC[:, c4 * 512:(c4 + 1) * 512],
                        arW[:, e * 2048 + tc * 128:e * 2048 + (tc + 1) * 128],
                        wob[ot % 2][:, e * 512:(e + 1) * 512],
                        start=(e == 0), stop=(e == NE - 1)))
            t_mm = op_group("tensor", fns,
                            [t_attn_n, t_wo[ot % 2], t_prev_st])
            t_1 = op("vector", lambda e: e.scalar_tensor_tensor(
                t1f4, psC[:], 1.0, fft4, ALU.mult, ALU.add), [t_mm, t_ff])
            t_2 = op("vector", lambda e: e.tensor_tensor(
                outst, t1f4, xft4, ALU.add), [t_1, t_xf])
            t_st = dma("sync", out[r0:r0 + 512, ot * 512:(ot + 1) * 512]
                       .rearrange("(c p) d -> p c d", p=128),
                       outst.rearrange("p (c d) -> p c d", c=4),
                       waits=[t_2], sem="s_out")
            t_prev_st = t_st
            t_prev_1 = [t_1, t_2]
            t_outs.append(t_st)
        t_wo_free[ot % 2] = t_outs[-1]

    P.add("gpsimd", lambda e: e.nop(), [t_outs[-1]])

    P.flush(block)
    blk_guard.__exit__(None, None, None)
    for g in reversed(guards):
        g.__exit__(None, None, None)
    return nc


def _make_consts(core):
    g = core % 4
    pos = g * 2048 + np.arange(2048, dtype=np.float64)
    freq = (1.0 / 1024.0) ** np.linspace(0.0, 1.0, 32)
    theta = pos[:, None] * freq[None, :]
    cos = np.cos(theta).astype(np.float32).reshape(NC, 128, 1, 32)
    sin = np.sin(theta).astype(np.float32).reshape(NC, 128, 1, 32)
    cos = np.ascontiguousarray(
        np.repeat(cos, 2, axis=2).transpose(1, 0, 2, 3).reshape(128, NC * 2 * 32))
    sin = np.ascontiguousarray(
        np.repeat(sin, 2, axis=2).transpose(1, 0, 2, 3).reshape(128, NC * 2 * 32))

    i = np.arange(128, dtype=np.float64)
    diff = i[None, :, None] - i[None, None, :]
    dm = np.where(diff >= 0, _R[:, None, None] ** np.maximum(diff, 0.0), 0.0)
    dmaskT = np.ascontiguousarray(np.transpose(dm, (0, 2, 1))).astype(np.float32)

    qdb = (_R[:, None] ** (i[None, :] + 1.0))
    qdb_t = np.ascontiguousarray(
        np.broadcast_to(qdb[:, None, :], (H, 128, 128))).astype(np.float32)
    kdecT = np.ascontiguousarray(
        (_R[None, :] ** (127.0 - i[:, None])).astype(np.float32))

    coef = np.zeros((CH * 8,), dtype=np.float64)
    for h6 in range(CH):
        for r in range(8):
            gr = r % 4
            if r // 4 == core // 4 and gr < g:
                coef[h6 * 8 + r] = _RC16[(H - CH) + h6] ** (g - 1 - gr)
    coefm = np.ascontiguousarray(
        np.broadcast_to(coef.astype(np.float32)[None, :], (128, CH * 8)))

    return dict(cosc=np.ascontiguousarray(cos), sinc=np.ascontiguousarray(sin),
                dmaskT=dmaskT, qdb=qdb_t, kdecT=kdecT, coef=coefm)


def kernel(L0_attn_x, L0_attn_Wq, L0_attn_Wk, L0_attn_Wv, L0_attn_Wo,
           L0_attn_rms_w, L0_ffn_gamma, L0_ffn_W):
    x = np.asarray(L0_attn_x, dtype=np.float32)
    Wq = np.asarray(L0_attn_Wq, dtype=np.float32)
    Wk = np.asarray(L0_attn_Wk, dtype=np.float32)
    Wv = np.asarray(L0_attn_Wv, dtype=np.float32)
    Wo = np.asarray(L0_attn_Wo, dtype=np.float32)
    rms_w = np.asarray(L0_attn_rms_w, dtype=np.float32)
    gamma = np.asarray(L0_ffn_gamma, dtype=np.float32)
    Wf = np.asarray(L0_ffn_W, dtype=np.float32)

    x2d = x.reshape(16384, E)
    pairs = []
    for pair in range(8):
        cols = []
        for hh in range(2):
            hd = (pair * 2 + hh) * 128
            cols += [Wq[:, hd:hd + 128], Wk[:, hd:hd + 128], Wv[:, hd:hd + 128]]
        pairs.append(np.concatenate(cols, axis=1))
    wqkv = np.ascontiguousarray(np.stack(pairs))
    wffnT = np.ascontiguousarray(Wf.T)
    wo_t = np.ascontiguousarray(Wo.reshape(E, 4, 512).transpose(1, 0, 2))
    gamma_t = np.ascontiguousarray(gamma.reshape(NE, 128).T)
    gamma_b = np.ascontiguousarray(np.broadcast_to(gamma[None, :], (128, E)))
    rmsw_t = np.ascontiguousarray(rms_w.reshape(H, 128).T)
    identb = np.eye(128, dtype=BF)
    ones_f = np.ones((1, 128), dtype=np.float32)
    ones_c32 = np.ones((128, 1), dtype=np.float32)
    ones_cb = np.ones((128, 1), dtype=BF)

    in_maps = []
    for core in range(8):
        sl = slice(core * NTOK, (core + 1) * NTOK)
        im = dict(
            x_tok=np.ascontiguousarray(x2d[sl]),
            xT=np.ascontiguousarray(x2d[sl].T),
            wqkv=wqkv, wffnT=wffnT, wo_t=wo_t,
            gamma_t=gamma_t, gamma_b=gamma_b, rmsw_t=rmsw_t,
            identb=identb, ones_f=ones_f, ones_c32=ones_c32, ones_cb=ones_cb,
        )
        im.update(_make_consts(core))
        in_maps.append(im)

    nc = build_core_program()
    res = run_bass_kernel_spmd(nc, in_maps, core_ids=list(range(8)))
    outs = [res.results[c]["out"] for c in range(8)]
    return np.concatenate(outs, axis=0).reshape(2, 8192, E).astype(np.float32)


# version 15
# speedup vs baseline: 1.1406x; 1.0752x over previous
import math
import os
import numpy as np
import ml_dtypes
import concourse.bass as bass
import concourse.mybir as mybir
from concourse.bass_utils import run_bass_kernel_spmd

F32 = mybir.dt.float32
BF16 = mybir.dt.bfloat16
I8 = mybir.dt.int8
AF = mybir.ActivationFunctionType
ALU = mybir.AluOpType
AXL = mybir.AxisListType

E = 2048
H = 16
NTOK = 2048
NC = 16
NE = 16
CH = 10
BF = ml_dtypes.bfloat16


def _slopes(n):
    start = 2.0 ** (-2.0 ** (-(math.log2(n) - 3)))
    return np.array([start * start ** i for i in range(n)], dtype=np.float64)


_R = np.exp(-_slopes(H))
_RC = _R ** 128
_RC16 = _RC ** 16


class Prog:
    ENGINES = ("sync", "gpsimd", "tensor", "vector", "scalar")

    def __init__(self, nc):
        self.nc = nc
        self.ops = {e: [] for e in self.ENGINES}
        self.semh = {}
        self.semc = {}
        self.waited = {e: {} for e in self.ENGINES}

    def sem(self, name, guards):
        g = self.nc.semaphore(name)
        guards.append(g)
        self.semh[name] = g.__enter__()
        self.semc[name] = 0
        return name

    def inc(self, name, amt):
        self.semc[name] += amt
        return self.semh[name], (name, self.semc[name])

    def tok(self, name):
        return (name, self.semc[name])

    def add(self, eng, fn, waits=()):
        w = []
        for t in waits:
            if t is None:
                continue
            s, v = t
            if v <= 0 or self.waited[eng].get(s, 0) >= v:
                continue
            self.waited[eng][s] = v
            w.append((self.semh[s], v))
        self.ops[eng].append((w, fn))

    def flush(self, block):
        for ename in self.ENGINES:
            ops = self.ops[ename]
            if not ops:
                continue
            self.ops[ename] = []

            def body(eng, ops=ops):
                for waits, fn in ops:
                    for h, v in waits:
                        eng.wait_ge(h, v)
                    fn(eng)
            getattr(block, ename)(body)


_ENG_SEM = {"tensor": "s_pe", "vector": "s_ve", "scalar": "s_ac", "gpsimd": "s_po"}


_PH = int(os.environ.get("KPHASE", "99"))


def build_core_program():
    nc = bass.Bass("TRN2", target_bir_lowering=False)

    def din(name, shape, dt=F32):
        return nc.dram_tensor(name, shape, dt, kind="ExternalInput").ap()

    x_tok = din("x_tok", [NTOK, E])
    xTin = din("xT", [E, NTOK])
    wqkv = din("wqkv", [8, E, 768])
    wffnT = din("wffnT", [E, E])
    wo_t = din("wo_t", [4, E, 512])
    cosc = din("cosc", [128, NC * 2 * 32])
    sinc = din("sinc", [128, NC * 2 * 32])
    dmaskT_in = din("dmaskT", [H, 128, 128])
    qdb_in = din("qdb", [H, 128, 128])
    kdecT_in = din("kdecT", [128, H])
    coef_in = din("coef", [128, CH * 8])
    rmsw_in = din("rmsw_t", [128, H])
    gamma_in = din("gamma_t", [128, NE])
    gammab_in = din("gamma_b", [128, E])
    ident_in = din("identb", [128, 128], BF16)
    ones_in = din("ones_f", [1, 128])
    onesc_in = din("ones_c32", [128, 1])
    onescb_in = din("ones_cb", [128, 1], BF16)

    out = nc.dram_tensor("out", [NTOK, E], F32, kind="ExternalOutput").ap()

    q_scr = nc.dram_tensor("q_scr", [H, NTOK, 128], BF16).ap()
    k_scr = nc.dram_tensor("k_scr", [H, NTOK, 128], BF16).ap()
    kp_scr = nc.dram_tensor("kp_scr", [H, NTOK, 128], BF16).ap()
    v_scr = nc.dram_tensor("v_scr", [H, NTOK, 128], BF16).ap()
    ffn_scr = nc.dram_tensor("ffn_scr", [NTOK, E], BF16).ap()
    row_scr = nc.dram_tensor("row_scr", [128, 48], F32).ap()
    cc_i = nc.dram_tensor("cc_i", [CH * 128, 128], F32).ap()
    cc_o = nc.dram_tensor("cc_o", [8 * CH * 128, 128], F32, addr_space="Shared").ap()

    guards = []

    def sb(name, shape, dt):
        g = nc.sbuf_tensor(name, shape, dt)
        guards.append(g)
        return g.__enter__()

    def mkpsum(name, shape, dt=F32):
        g = nc.psum_tensor(name, shape, dt)
        guards.append(g)
        return g.__enter__()

    arX = sb("arX", [128, NE * 2048], BF16)
    arW = sb("arW", [128, NE * 2048], BF16)
    arQ = sb("arQ", [128, 16384], BF16)
    arF = sb("arF", [128, 4096], F32)
    arG = sb("arG", [128, 4096], F32)
    arQi = arQ[:].bitcast(I8)
    arQf = arQ[:].bitcast(F32)
    arGb = arG[:].bitcast(BF16)

    c_kd = sb("c_kd", [128, H], F32)
    c_cf = sb("c_cf", [128, CH * 8], F32)
    c_rw = sb("c_rw", [128, H], F32)
    c_gt = sb("c_gt", [128, NE], F32)
    c_id = sb("c_id", [128, 128], BF16)
    c_on = sb("c_on", [1, 128], F32)
    c_oc = sb("c_oc", [128, 1], F32)
    c_ob = sb("c_ob", [128, 1], BF16)

    epsv = sb("epsv", [128, 2], F32)
    junk = arQ[:, 14336:16384]
    junki = junk.bitcast(I8)
    sv = sb("sv", [128, 16 * NC], F32)
    stats = sb("stats", [128, 64], F32)
    rows = sb("rows", [1, 2564], F32)
    wsc = sb("wsc", [128, 4], F32)
    arXf = arX[:].bitcast(F32)
    arWf = arW[:].bitcast(F32)
    bcast = arQ[:, 8192:14336]
    c_gb = arWf[:, 0:2048]
    zs = arX[:, 0:2048]
    ssb = arX[:, 2048:4096]
    ccst = arXf[:, 2048:2048 + CH * 128]
    ccg = arXf[:, 3328:3328 + 2 * CH * 128]
    kvini = arX[:, 11776:11776 + CH * 128]
    kvs = arX[:, 13056:13312]
    c_dm = arXf[:, 6720:8768]
    c_qdb = arXf[:, 8768:10816]

    psA = mkpsum("psA", [128, 1024], F32)
    psB = mkpsum("psB", [128, 1024], F32)
    psC = mkpsum("psC", [128, 2048], F32)
    psAb = psA[:].bitcast(BF16)
    psBb = psB[:].bitcast(BF16)

    P = Prog(nc)
    for s in ("s_sy", "s_gp", "s_pe", "s_ve", "s_ac", "s_po", "s_cc", "s_out"):
        P.sem(s, guards)

    blk_guard = nc.Block()
    block = blk_guard.__enter__()

    def dma(queue, dst, src, waits=(), sem=None, **kw):
        sname = sem or ("s_sy" if queue == "sync" else "s_gp")
        h, t = P.inc(sname, 16)
        P.add(queue, lambda eng, d=dst, s=src: eng.dma_start(out=d, in_=s, **kw)
              .then_inc(h, 16), waits)
        return t

    def op(eng, fn, waits=()):
        h, t = P.inc(_ENG_SEM[eng], 1)
        P.add(eng, lambda e, f=fn: f(e).then_inc(h, 1), waits)
        return t

    def op_group(eng, fns, waits=()):
        h, t = P.inc(_ENG_SEM[eng], 1)

        def run(e, fns=tuple(fns)):
            for f in fns[:-1]:
                f(e)
            fns[-1](e).then_inc(h, 1)
        P.add(eng, run, waits)
        return t

    t_eps = op("gpsimd", lambda e: e.memset(epsv[:, 0:1], 1e-5), [])
    t_eps = op("gpsimd", lambda e: e.memset(epsv[:, 1:2], 1e-6), [t_eps])
    dma("sync", c_kd[:], kdecT_in)
    dma("sync", c_cf[:], coef_in)
    dma("sync", c_rw[:], rmsw_in)
    t_gb = dma("sync", c_gb, gammab_in)
    t_id = dma("sync", c_id[:], ident_in)
    t_on = dma("sync", c_on[:], ones_in)
    t_oc = dma("sync", c_oc[:], onesc_in)
    t_ob = dma("sync", c_ob[:], onescb_in)
    t_consts = dma("sync", c_gt[:], gamma_in)

    t_xT = []
    for e in range(NE):
        t_xT.append(dma("gpsimd", arX[:, e * 2048:(e + 1) * 2048],
                        xTin[e * 128:(e + 1) * 128, :]))
    t_xT_all = t_xT[-1]

    xstage = [arF[:, 0:2048], arF[:, 2048:4096]]
    xnbuf = [arG[:, 0:2048], arG[:, 2048:4096]]
    t_xfree = [None, None]
    t_nfree = [None, None]
    t_K = [None] * NC
    for c in range(NC):
        b = xstage[c % 2]
        xn = xnbuf[c % 2]
        V = sv[:, c * 16:(c + 1) * 16]
        t_ld = dma("sync", b, x_tok[c * 128:(c + 1) * 128, :], waits=[t_xfree[c % 2]])
        t_s1 = op("scalar", lambda e, b=b, V=V: e.activation(
            junk, b, AF.Copy, accum_out=V[:, 0:1]), [t_ld])
        t_s2 = op("scalar", lambda e, b=b, V=V: e.activation(
            junk, b, AF.Square, accum_out=V[:, 1:2]), [t_ld])
        t_m = op("vector", lambda e, V=V, c=c: e.tensor_scalar(
            stats[:, c:c + 1], V[:, 0:1], 1.0 / E, None, ALU.mult), [t_s1])
        t_v1 = op("vector", lambda e, V=V: e.tensor_scalar(
            V[:, 2:3], V[:, 1:2], 1.0 / E, None, ALU.mult), [t_s2])
        t_m2 = op("vector", lambda e, V=V, c=c: e.tensor_tensor(
            V[:, 3:4], stats[:, c:c + 1], stats[:, c:c + 1], ALU.mult), [t_m])
        t_var = op("vector", lambda e, V=V: e.tensor_tensor(
            V[:, 4:5], V[:, 2:3], V[:, 3:4], ALU.subtract), [t_v1, t_m2])
        t_sd = op("scalar", lambda e, V=V: e.activation(
            V[:, 5:6], V[:, 4:5], AF.Sqrt, bias=epsv[:, 0:1]), [t_var, t_eps])
        t_rstd = op("vector", lambda e, V=V, c=c: e.reciprocal(
            stats[:, 16 + c:17 + c], V[:, 5:6]), [t_sd])
        t_u = op("vector", lambda e, V=V: e.tensor_scalar(
            V[:, 6:7], V[:, 4:5], 1e-5, None, ALU.add), [t_var])
        t_ur = op("vector", lambda e, V=V: e.reciprocal(V[:, 7:8], V[:, 6:7]), [t_u])
        t_r2 = op("vector", lambda e, V=V: e.tensor_tensor(
            V[:, 8:9], V[:, 4:5], V[:, 7:8], ALU.mult), [t_ur])
        t_r2e = op("scalar", lambda e, V=V: e.activation(
            V[:, 9:10], V[:, 8:9], AF.Sqrt, bias=epsv[:, 1:2]), [t_r2, t_eps])
        t_rstd2 = op("vector", lambda e, V=V: e.reciprocal(V[:, 10:11], V[:, 9:10]),
                     [t_r2e])
        t_xn = op("vector", lambda e, b=b, xn=xn, c=c: e.tensor_scalar(
            xn, b, stats[:, c:c + 1], stats[:, 16 + c:17 + c],
            ALU.subtract, ALU.mult), [t_rstd, t_nfree[c % 2]])
        t_xg = op("gpsimd", lambda e, b=b, xn=xn: e.tensor_tensor(
            b, xn, c_gb, ALU.mult), [t_xn, t_gb, t_s1, t_s2])
        t_am = op("vector", lambda e, b=b, V=V: e.tensor_reduce(
            V[:, 11:12], b, AXL.X, ALU.max, apply_absolute_value=True), [t_xg])
        t_amx = op("vector", lambda e, V=V: e.tensor_tensor(
            V[:, 12:13], V[:, 11:12], V[:, 10:11], ALU.mult), [t_am, t_rstd2])
        t_ap = op("vector", lambda e, V=V: e.tensor_scalar(
            V[:, 13:14], V[:, 12:13], 1e-5, None, ALU.add), [t_amx])
        t_ar = op("vector", lambda e, V=V: e.reciprocal(V[:, 14:15], V[:, 13:14]),
                  [t_ap])
        t_as = op("vector", lambda e, V=V, c=c: e.tensor_scalar(
            stats[:, 48 + c:49 + c], V[:, 14:15], 127.0, None, ALU.mult), [t_ar])
        t_k1 = op("vector", lambda e, V=V, c=c: e.tensor_tensor(
            V[:, 15:16], stats[:, 16 + c:17 + c], V[:, 10:11], ALU.mult),
            [t_rstd, t_rstd2])
        t_K[c] = op("vector", lambda e, V=V, c=c: e.tensor_tensor(
            stats[:, 32 + c:33 + c], V[:, 15:16], stats[:, 48 + c:49 + c],
            ALU.mult), [t_k1, t_as])
        t_xfree[c % 2] = t_am
        t_nfree[c % 2] = t_xg
    t_p1done = t_K[NC - 1]

    if _PH <= 1:
        dma("sync", out[0:128, 0:64], stats[:], sem="s_out")
        P.add("gpsimd", lambda e: e.nop(), [P.tok("s_out")])
        P.flush(block)
        blk_guard.__exit__(None, None, None)
        for g in reversed(guards):
            g.__exit__(None, None, None)
        return nc
    t_wslab = [None, None]
    t_wacc = None
    for i in range(NE):
        b = xnbuf[i % 2]
        gate = t_nfree[i % 2] if i < 2 else t_wslab[i % 2]
        t_wl = dma("sync", b, wffnT[i * 128:(i + 1) * 128, :], waits=[gate])
        t_abs = op("scalar", lambda e, b=b: e.activation(
            junk, b, AF.Abs, accum_out=wsc[:, 1:2]), [t_wl, t_wacc])
        if i == 0:
            t_wacc = op("vector", lambda e: e.tensor_copy(
                wsc[:, 2:3], wsc[:, 1:2]), [t_abs])
        else:
            t_wacc = op("vector", lambda e: e.tensor_tensor(
                wsc[:, 2:3], wsc[:, 2:3], wsc[:, 1:2], ALU.add), [t_abs])
        t_wslab[i % 2] = t_wacc
    t_rw1_ph = [None]
    t_wred = op("tensor", lambda e: e.matmul(
        psA[0:1, 0:1], wsc[:, 2:3], c_oc[:], start=True, stop=True),
        [t_wacc, t_oc])
    t_ws1 = op("vector", lambda e: e.tensor_scalar(
        rows[0:1, 2560:2561], psA[0:1, 0:1], 1.0 / (E * E), 1e-5,
        ALU.mult, ALU.add), [t_wred])
    t_ws2 = op("vector", lambda e: e.reciprocal(
        rows[0:1, 2561:2562], rows[0:1, 2560:2561]), [t_ws1])
    t_wsb = op("tensor", lambda e: e.matmul(
        psA[:, 0:1], c_on[:], rows[0:1, 2561:2562], start=True, stop=True),
        [t_ws2, t_on])
    t_wsv = op("scalar", lambda e: e.copy(wsc[:, 0:1], psA[:, 0:1]), [t_wsb])
    t_f1 = op("vector", lambda e: e.tensor_scalar(
        stats[:, 0:16], stats[:, 48:64], wsc[:, 0:1], None, ALU.mult),
        [t_wsv, t_p1done, t_rw1_ph[0]])
    t_ffsc = op("vector", lambda e: e.reciprocal(stats[:, 48:64], stats[:, 0:16]),
                [t_f1])


    if _PH <= 2:
        dma("sync", out[0:128, 0:64], stats[:], sem="s_out")
        P.add("gpsimd", lambda e: e.nop(), [P.tok("s_out")])
        P.flush(block)
        blk_guard.__exit__(None, None, None)
        for g in reversed(guards):
            g.__exit__(None, None, None)
        return nc
    t_rw1 = dma("sync", row_scr[:, 0:48], stats[:, 0:48], waits=[t_p1done])
    t_rw1_ph[0] = t_rw1
    t_bc = []
    t_rdfree = [None, None]
    t_mmq = [None, None]
    kq = 0
    for j in range(3):
        tev = None
        for q in range(4):
            sl = rows[0:1, (kq % 2) * 512:(kq % 2) * 512 + 512]
            src_ap = row_scr.rearrange("p c -> c p")[
                j * 16 + q * 4:j * 16 + (q + 1) * 4, :].unsqueeze(0)
            t_rd = dma("gpsimd", sl.rearrange("a (c p) -> a c p", c=4), src_ap,
                       waits=[t_rw1, t_rdfree[kq % 2]],
                       allow_slow_non_contiguous=True)
            tmm = op("tensor", lambda e, kq=kq, sl=sl: e.matmul(
                psB[:, (kq % 2) * 512:(kq % 2) * 512 + 512], c_on[:], sl,
                start=True, stop=True),
                [t_rd, t_on, t_wsb, t_mmq[kq % 2]])
            t_rdfree[kq % 2] = tmm
            tev = op("scalar", lambda e, j=j, q=q, kq=kq: e.copy(
                bcast[:, j * 2048 + q * 512:j * 2048 + (q + 1) * 512],
                psB[:, (kq % 2) * 512:(kq % 2) * 512 + 512]), [tmm])
            t_mmq[kq % 2] = tev
            kq += 1
        t_bc.append(tev)
    m_b = bcast[:, 0:2048]
    rstd_b = bcast[:, 2048:4096]
    K_b = bcast[:, 4096:6144]


    if _PH <= 3:
        dma("sync", out[0:128, 0:64], stats[:], sem="s_out")
        P.add("gpsimd", lambda e: e.nop(), [P.tok("s_out")])
        P.flush(block)
        blk_guard.__exit__(None, None, None)
        for g in reversed(guards):
            g.__exit__(None, None, None)
        return nc
    t_wr = [t_wslab[0], t_wslab[1]]
    for i in range(NE):
        b = xnbuf[i % 2]
        t_wl = dma("sync", b, wffnT[i * 128:(i + 1) * 128, :], waits=[t_wr[i % 2]])
        ti8 = op("vector", lambda e, b=b: e.tensor_scalar(
            junki[:, 0:2048], b, wsc[:, 0:1], None, ALU.mult),
            [t_wl, t_wsv] + ([t_nfree[0], t_nfree[1]] if i < 2 else []))
        t_wr[i % 2] = op("vector", lambda e, i=i: e.tensor_scalar(
            arW[:, i * 2048:(i + 1) * 2048], junki[:, 0:2048],
            1.0, -1.0, ALU.min, ALU.max), [ti8])
    t_wrdone = op("vector", lambda e: e.tensor_copy(wsc[:, 1:2], wsc[:, 0:1]),
                  [t_wr[0], t_wr[1]])


    if _PH <= 4:
        dma("sync", out[0:128, 0:64], stats[:], sem="s_out")
        P.add("gpsimd", lambda e: e.nop(), [P.tok("s_out")])
        P.flush(block)
        blk_guard.__exit__(None, None, None)
        for g in reversed(guards):
            g.__exit__(None, None, None)
        return nc
    t_ffn_ev = {}
    t_ps_free = {0: t_wsv, 1: None}
    t_mm_prev = None
    for half in range(2):
        hs = slice(half * 1024, (half + 1) * 1024)
        t_xq = [None] * NE
        for e in range(NE):
            t_t1 = op("vector", lambda e_, e=e, half=half, hs=hs: e_.tensor_tensor(
                arGb[:, 0:1024],
                arX[:, e * 2048 + half * 1024:e * 2048 + half * 1024 + 1024],
                m_b[:, hs], ALU.subtract),
                [t_bc[0], t_xT[e], t_wrdone, t_wslab[0], t_wslab[1]])
            t_xq[e] = op("vector", lambda e_, e=e, hs=hs: e_.scalar_tensor_tensor(
                arQi[:, e * 1024:(e + 1) * 1024], arGb[:, 0:1024],
                c_gt[:, e:e + 1], K_b[:, hs], ALU.mult, ALU.mult),
                [t_t1, t_bc[2], t_consts])
        for tci in range(8):
            tc = half * 8 + tci
            t_lhs = op("vector", lambda e_, tci=tci: e_.tensor_copy(
                arGb[:, 4096:6144].rearrange("p (e d) -> p e d", e=16),
                arQi[:, 0:16384].rearrange("p (e t) -> p e t", e=16)
                [:, :, tci * 128:(tci + 1) * 128]),
                t_xq + [t_mm_prev])
            for oh in range(4):
                pp = psA if oh % 2 == 0 else psB
                fns = [lambda eng, e=e, oh=oh, pp=pp: eng.matmul(
                    pp[:, 0:512], arGb[:, 4096 + e * 128:4096 + (e + 1) * 128],
                    arW[:, e * 2048 + oh * 512:e * 2048 + (oh + 1) * 512],
                    start=(e == 0), stop=(e == NE - 1)) for e in range(NE)]
                tmm = op_group("tensor", fns,
                               [t_lhs, t_wrdone, t_ps_free[oh % 2], t_mm_prev])
                t_mm_prev = tmm
                t_ps_free[oh % 2] = op("vector",
                                       lambda e_, oh=oh, pp=pp, tc=tc: e_.tensor_scalar(
                                           junk[:, oh * 512:(oh + 1) * 512], pp[:, 0:512],
                                           stats[:, 48 + tc:49 + tc], None, ALU.mult),
                                       [tmm, t_ffsc,
                                        t_ffn_ev.get(tc - 1)])
            t_ffn_ev[tc] = dma("sync", ffn_scr[tc * 128:(tc + 1) * 128, :],
                               junk, waits=[t_ps_free[0], t_ps_free[1]])
    t_ffn_done = t_ffn_ev[15]


    if _PH <= 5:
        dma("gpsimd", out, ffn_scr, waits=[t_ffn_done], sem="s_out")
        dma("sync", out[0:128, 0:64], stats[:, 0:64] if False else stats[:], sem="s_out") if False else None
        P.add("gpsimd", lambda e: e.nop(), [P.tok("s_out")])
        P.flush(block)
        blk_guard.__exit__(None, None, None)
        for g in reversed(guards):
            g.__exit__(None, None, None)
        return nc
    t_xnT = None
    for e in range(NE):
        t_a = op("vector", lambda e_, e=e: e_.tensor_tensor(
            arGb[:, 4096:6144], arX[:, e * 2048:(e + 1) * 2048], m_b[:],
            ALU.subtract), [t_bc[1], t_mm_prev if e == 0 else None])
        t_xnT = op("vector", lambda e_, e=e: e_.tensor_tensor(
            arX[:, e * 2048:(e + 1) * 2048], arGb[:, 4096:6144], rstd_b[:],
            ALU.mult), [t_a])


    if _PH <= 6:
        dma("sync", out[0:128, 0:64], stats[:], sem="s_out")
        P.add("gpsimd", lambda e: e.nop(), [P.tok("s_out")])
        P.flush(block)
        blk_guard.__exit__(None, None, None)
        for g in reversed(guards):
            g.__exit__(None, None, None)
        return nc
    c_cosR = arQf[:, 4480:5504]
    c_sinR = arQf[:, 5504:6528]
    t_cosR = dma("sync", c_cosR, cosc, waits=[t_xnT])
    t_sinR = dma("sync", c_sinR, sinc, waits=[t_xnT])

    stg = [arQ[:, 0:3072], arQ[:, 3072:6144]]
    kstg = [arQ[:, 6144:7168], arQ[:, 7168:8192]]
    rsc = [arQ[:, 8192:8448], arQ[:, 8448:8704], arQ[:, 8704:8960]]
    t_wq = [None, None]
    t_wq_free = [t_mm_prev, t_mm_prev]
    t_stg_free = [None, None]
    t_kp_free = [None, None]
    t_last_spill = None
    for pair in range(8):
        wb = arW[:, (pair % 2) * 12288:(pair % 2) * 12288 + 12288]
        t_wq[pair % 2] = dma("gpsimd", wb.rearrange("p (e n) -> p e n", e=16),
                             wqkv[pair].rearrange("(e p) n -> p e n", p=128),
                             waits=[t_wq_free[pair % 2],
                                    t_mm_prev if pair < 2 else None])
        for g4 in range(4):
            st = stg[g4 % 2]
            kp = kstg[g4 % 2]
            stv = st.rearrange("p (c h x) -> p c h x", c=4, h=2)
            t_evs = []
            for c4 in range(4):
                c = g4 * 4 + c4
                pp = psA if c % 2 == 0 else psB
                fns = []
                for e in range(NE):
                    for hh in range(2):
                        fns.append(lambda eng, e=e, hh=hh, pp=pp, c=c, wb=wb:
                                   eng.matmul(
                                       pp[:, hh * 512:hh * 512 + 384],
                                       arX[:, e * 2048 + c * 128:e * 2048 + (c + 1) * 128],
                                       wb[:, e * 768 + hh * 384:e * 768 + (hh + 1) * 384],
                                       start=(e == 0), stop=(e == NE - 1)))
                tmm = op_group("tensor", fns,
                               [t_xnT, t_wq[pair % 2], t_ps_free[c % 2]])
                if c4 % 2 == 0:
                    t_ev = op("scalar", lambda e_, pp=pp, st=st, c4=c4: e_.copy(
                        st[:, c4 * 768:(c4 + 1) * 768]
                        .rearrange("p (h x) -> p h x", h=2),
                        pp[:].rearrange("p (h x) -> p h x", h=2)[:, :, 0:384]),
                        [tmm, t_stg_free[g4 % 2]])
                else:
                    t_ev = op("vector", lambda e_, pp=pp, st=st, c4=c4: e_.tensor_copy(
                        st[:, c4 * 768:(c4 + 1) * 768]
                        .rearrange("p (h x) -> p h x", h=2),
                        pp[:].rearrange("p (h x) -> p h x", h=2)[:, :, 0:384]),
                        [tmm, t_stg_free[g4 % 2]])
                t_ps_free[c % 2] = t_ev
                t_evs.append(t_ev)
            stm = st.rearrange("p (m x) -> p m x", m=8)
            cosv = c_cosR.rearrange("p (c f) -> p c f", f=32)[:, g4 * 8:(g4 + 1) * 8, :]
            sinv = c_sinR.rearrange("p (c f) -> p c f", f=32)[:, g4 * 8:(g4 + 1) * 8, :]
            sA = rsc[0].rearrange("p (m f) -> p m f", m=8)
            sB = rsc[1].rearrange("p (m f) -> p m f", m=8)
            sC = rsc[2].rearrange("p (m f) -> p m f", m=8)
            t_rot = t_evs[-1]
            for base in (0, 128):
                x1 = stm[:, :, base:base + 32]
                x2 = stm[:, :, base + 64:base + 96]
                tA = op("vector", lambda e, x1=x1, cosv=cosv, sinv=sinv: e.tensor_tensor(
                    sA, x1, sinv, ALU.mult), [t_rot] + t_evs + [t_cosR, t_sinR])
                tB = op("vector", lambda e, x1=x1, cosv=cosv: e.tensor_tensor(
                    sB, x1, cosv, ALU.mult), [tA])
                tC = op("vector", lambda e, x2=x2, sinv=sinv: e.tensor_tensor(
                    sC, x2, sinv, ALU.mult), [tB])
                tY1 = op("vector", lambda e, x1=x1: e.tensor_tensor(x1, sB, sC, ALU.add), [tC])
                tD = op("vector", lambda e, x2=x2, cosv=cosv: e.tensor_tensor(
                    sB, x2, cosv, ALU.mult), [tY1])
                t_rot = op("vector", lambda e, x2=x2: e.tensor_tensor(x2, sB, sA, ALU.subtract), [tD])
            t_kp = t_rot
            for mh in range(8):
                head = pair * 2 + (mh % 2)
                t_kp = op("vector", lambda e, kp=kp, mh=mh, head=head, stm=stm: e.tensor_scalar(
                    kp[:, mh * 128:(mh + 1) * 128],
                    stm[:, mh, 128:256], c_kd[:, head:head + 1], None, ALU.mult),
                    [t_rot, t_kp_free[g4 % 2]])
            t_sp = []
            for scr, base in ((q_scr, 0), (k_scr, 128), (v_scr, 256)):
                for hh in range(2):
                    dst = scr[pair * 2 + hh, g4 * 512:(g4 + 1) * 512, :] \
                        .rearrange("(c p) d -> p c d", p=128)
                    t_sp.append(dma("sync", dst, stv[:, :, hh, base:base + 128],
                                    waits=[t_rot]))
            kpv = kp.rearrange("p (c h d) -> p c h d", c=4, h=2)
            for hh in range(2):
                dstp = kp_scr[pair * 2 + hh, g4 * 512:(g4 + 1) * 512, :] \
                    .rearrange("(c p) d -> p c d", p=128)
                t_sp.append(dma("sync", dstp, kpv[:, :, hh, :], waits=[t_kp]))
            t_stg_free[g4 % 2] = t_sp[7]
            t_kp_free[g4 % 2] = t_sp[7]
            t_last_spill = t_sp[7]
        t_wq_free[pair % 2] = t_last_spill
    t_p2done = t_last_spill


    if _PH <= 7:
        dma("sync", out[0:128, 0:64], stats[:], sem="s_out")
        P.add("gpsimd", lambda e: e.nop(), [P.tok("s_out")])
        P.flush(block)
        blk_guard.__exit__(None, None, None)
        for g in reversed(guards):
            g.__exit__(None, None, None)
        return nc
    t_dm = dma("sync", c_dm.rearrange("p (h c) -> p h c", h=H),
               dmaskT_in.rearrange("h s c -> s h c"), waits=[t_p2done])
    t_qdbl = dma("sync", c_qdb.rearrange("p (h c) -> p h c", h=H),
                 qdb_in.rearrange("h d c -> d h c"), waits=[t_p2done])

    q_sb = arQ[:, 0:2048]
    k_sb = arQ[:, 2048:4096]
    kp_sb = arQ[:, 4096:6144]
    v_sb = arQ[:, 6144:8192]
    qT = arQ[:, 8192:10240]
    kT = arQ[:, 10240:12288]
    uperm = arF[:, 0:2048]
    hlist = list(range(H - CH, H)) + list(range(H - CH))

    t_rd_free = [t_p2done] * 4
    t_zs_free = []
    t_sd0_free = None
    t_psC_free = t_p2done
    t_psT_free = {"q": t_ps_free[0], "k": t_ps_free[1]}
    t_attn_ev = {}
    t_ssb_free = t_p2done
    t_uperm_free = None
    t_qp_free = None
    t_cc_st = []

    for hi, h in enumerate(hlist):
        kept = h >= H - CH
        qpT = arQ[:, 12288:14336]
        t_q = dma("sync", q_sb.rearrange("p (l d) -> p l d", l=16),
                  q_scr[h].rearrange("(l p) d -> p l d", p=128),
                  waits=[t_rd_free[0]])
        t_k = dma("sync", k_sb.rearrange("p (l d) -> p l d", l=16),
                  k_scr[h].rearrange("(l p) d -> p l d", p=128),
                  waits=[t_rd_free[1]])
        t_kp2 = dma("sync", kp_sb.rearrange("p (l d) -> p l d", l=16),
                    kp_scr[h].rearrange("(l p) d -> p l d", p=128),
                    waits=[t_rd_free[2]])
        t_v = dma("sync", v_sb.rearrange("p (l d) -> p l d", l=16),
                  v_scr[h].rearrange("(l p) d -> p l d", p=128),
                  waits=[t_rd_free[3]])

        fns = [lambda eng, l=l: eng.matmul(
            psC[:, l * 128:(l + 1) * 128], kp_sb[:, l * 128:(l + 1) * 128],
            v_sb[:, l * 128:(l + 1) * 128], start=True, stop=True)
            for l in range(16)]
        t_U = op_group("tensor", fns, [t_kp2, t_v, t_psC_free])
        t_up = op("scalar", lambda e: e.activation(
            uperm.rearrange("p (e l) -> p e l", l=16),
            psC[:].rearrange("p (l e) -> p e l", l=16), AF.Copy),
            [t_U, t_uperm_free])
        sd0m = arF[:, 2048:4096]
        t_d0a = op("gpsimd", lambda e, h=h: e.memset(sd0m, float(_RC[h])),
                   [t_sd0_free])
        t_d0b = op("gpsimd", lambda e: e.memset(
            sd0m.rearrange("p (e l) -> p e l", l=16)[:, :, 0], 0.0), [t_d0a])
        t_zs = op("vector", lambda e: e.tensor_tensor_scan(
            zs[:], sd0m, uperm[:] if False else uperm,
            0.0, ALU.mult, ALU.add), [t_up, t_d0b] + list(t_zs_free))
        t_uperm_free = t_zs
        t_sd0_free = t_zs

        t_tr_done = {}
        for tn, src, dst, psx, t_src in (("q", q_sb, qT, psAb, t_q),
                                         ("k", k_sb, kT, psBb, t_k)):
            t_prev = t_psT_free[tn]
            for bidx in range(2):
                fns = [lambda eng, l=l, src=src, psx=psx: eng.transpose(
                    psx[:, (l % 8) * 128:(l % 8 + 1) * 128],
                    src[:, l * 128:(l + 1) * 128], c_id[:])
                    for l in range(bidx * 8, (bidx + 1) * 8)]
                t_tr = op_group("tensor", fns, [t_src, t_id, t_prev])
                if tn == "q":
                    t_prev = op("vector", lambda e, bidx=bidx, dst=dst, psx=psx:
                                e.tensor_copy(dst[:, bidx * 1024:(bidx + 1) * 1024],
                                              psx[:, 0:1024]), [t_tr])
                else:
                    t_prev = op("scalar", lambda e, bidx=bidx, dst=dst, psx=psx:
                                e.copy(dst[:, bidx * 1024:(bidx + 1) * 1024],
                                       psx[:, 0:1024]), [t_tr])
            t_tr_done[tn] = t_prev
            t_psT_free[tn] = t_prev

        t_qp = op("vector", lambda e, qpT=qpT, h=h: e.tensor_tensor(
            qpT.rearrange("p (l c) -> p l c", l=16),
            qT.rearrange("p (l c) -> p l c", l=16),
            c_qdb[:, h * 128:(h + 1) * 128].unsqueeze(1)
            .broadcast_to([128, 16, 128]), ALU.mult),
            [t_tr_done["q"], t_qdbl, t_qp_free])

        fns = [lambda eng, l=l: eng.matmul(
            psC[:, l * 128:(l + 1) * 128], kT[:, l * 128:(l + 1) * 128],
            qT[:, l * 128:(l + 1) * 128], start=True, stop=True)
            for l in range(16)]
        t_S = op_group("tensor", fns, [t_tr_done["q"], t_tr_done["k"], t_up])
        t_mask = op("vector", lambda e, h=h: e.tensor_tensor(
            ssb[:].rearrange("p (l c) -> p l c", l=16),
            psC[:].rearrange("p (l c) -> p l c", l=16),
            c_dm[:, h * 128:(h + 1) * 128].unsqueeze(1)
            .broadcast_to([128, 16, 128]), ALU.mult), [t_S, t_ssb_free, t_dm])

        fns = []
        for l in range(16):
            fns.append(lambda eng, l=l: eng.matmul(
                psC[:, l * 128:(l + 1) * 128], v_sb[:, l * 128:(l + 1) * 128],
                ssb[:, l * 128:(l + 1) * 128], start=True, stop=(l == 0),
                skip_group_check=True))
            if l >= 1:
                fns.append(lambda eng, l=l, qpT=qpT: eng.matmul(
                    psC[:, l * 128:(l + 1) * 128],
                    zs[:].rearrange("p (e l) -> p e l", l=16)[:, :, l - 1],
                    qpT[:, l * 128:(l + 1) * 128], start=False, stop=True,
                    skip_group_check=True))
        t_O = op_group("tensor", fns, [t_mask, t_zs, t_qp])
        t_ssb_free = t_O
        t_qp_free = t_O

        t_aev = op("scalar", lambda e, h=h: e.copy(
            arW[:, h * 2048:(h + 1) * 2048], psC[:]), [t_O])
        t_attn_ev[h] = t_aev
        t_psC_free = t_aev

        if kept:
            t_st = op("scalar", lambda e, h=h: e.copy(
                ccst[:, (h - (H - CH)) * 128:(h - (H - CH) + 1) * 128],
                zs[:].rearrange("p (e l) -> p e l", l=16)[:, :, 15]), [t_zs])
            t_cc_st.append(t_st)
            t_zs_free = [t_O, t_st]
        else:
            t_zs_free = [t_O]

        t_rd_free = [t_tr_done["q"], t_tr_done["k"], t_U, t_O]


    if _PH <= 8:
        w8 = [t_psC_free, t_uperm_free, t_qp_free] + list(t_zs_free)
        dma("gpsimd", out[0:128, :], zs, waits=w8, sem="s_out")
        dma("sync", out[128:256, :], uperm, waits=w8, sem="s_out")
        dma("gpsimd", out[256:384, :], kp_sb, waits=w8, sem="s_out")
        dma("gpsimd", out[384:512, :], v_sb, waits=w8, sem="s_out")
        dma("gpsimd", out[512:640, :], q_sb, waits=w8, sem="s_out")
        dma("gpsimd", out[640:768, :], k_sb, waits=w8, sem="s_out")
        dma("gpsimd", out[768:896, :], qT, waits=w8, sem="s_out")
        dma("gpsimd", out[896:1024, :], arQ[:, 12288:14336], waits=w8, sem="s_out")
        dma("gpsimd", out[1024:1152, :], ssb, waits=w8, sem="s_out")
        dma("gpsimd", out[1152:1280, 0:1536], ccst.bitcast(BF16)[:, 0:1536], waits=w8, sem="s_out")
        dma("sync", out[1280:1408, 0:64], stats[:], sem="s_out")
        P.add("gpsimd", lambda e: e.nop(), [P.tok("s_out")])
        P.flush(block)
        blk_guard.__exit__(None, None, None)
        for g in reversed(guards):
            g.__exit__(None, None, None)
        return nc
    t_ccw = dma("gpsimd", cc_i.rearrange("(a p) d -> p a d", p=128),
                ccst[:].rearrange("p (a d) -> p a d", a=CH), waits=t_cc_st)
    hcc, t_cc = P.inc("s_cc", 1)
    P.add("gpsimd", lambda e: e.collective_compute(
        "AllGather", ALU.bypass,
        replica_groups=[[0, 1, 2, 3, 4, 5, 6, 7]],
        ins=[cc_i], outs=[cc_o]).then_inc(hcc, 1), [t_ccw])

    t_last_attn = t_attn_ev[hlist[-1]]
    t_comb = [None] * CH
    t_ccr_free = None
    for r in range(8):
        buf = ccg[:, (r % 2) * CH * 128:(r % 2) * CH * 128 + CH * 128]
        t_ccr = dma("sync", buf.rearrange("p (a d) -> p a d", a=CH),
                    cc_o[r * CH * 128:(r + 1) * CH * 128, :]
                    .rearrange("(a p) d -> p a d", p=128),
                    waits=[("s_cc", 1), t_ccr_free])
        tlast = None
        for h6 in range(CH):
            if r == 0:
                tc2 = op("vector", lambda e, h6=h6, buf=buf: e.tensor_scalar(
                    arG[:, h6 * 128:(h6 + 1) * 128],
                    buf[:, h6 * 128:(h6 + 1) * 128],
                    c_cf[:, h6 * 8:h6 * 8 + 1], None, ALU.mult),
                    [t_ccr, t_last_attn])
            else:
                tc2 = op("vector", lambda e, h6=h6, buf=buf, r=r: e.scalar_tensor_tensor(
                    arG[:, h6 * 128:(h6 + 1) * 128],
                    buf[:, h6 * 128:(h6 + 1) * 128],
                    c_cf[:, h6 * 8 + r:h6 * 8 + r + 1],
                    arG[:, h6 * 128:(h6 + 1) * 128], ALU.mult, ALU.add),
                    [t_ccr, t_comb[h6]])
            t_comb[h6] = tc2
            tlast = tc2
        t_ccr_free = tlast
    t_kvb = op("vector", lambda e: e.tensor_copy(kvini[:], arG[:, 0:CH * 128]),
               t_comb)

    t_corr_ev = {}
    t_kvs_free = [None, None]
    t_psC2 = t_psC_free
    t_qrd_free = t_rd_free[0]
    t_trc_free = t_psT_free["q"]
    for h6 in range(CH):
        h = (H - CH) + h6
        t_qr = dma("sync", q_sb.rearrange("p (l d) -> p l d", l=16),
                   q_scr[h].rearrange("(l p) d -> p l d", p=128),
                   waits=[t_qrd_free])
        t_prev = t_trc_free
        for bidx in range(2):
            fns = [lambda eng, l=l: eng.transpose(
                psAb[:, (l % 8) * 128:(l % 8 + 1) * 128],
                q_sb[:, l * 128:(l + 1) * 128], c_id[:])
                for l in range(bidx * 8, (bidx + 1) * 8)]
            t_tr = op_group("tensor", fns, [t_qr, t_prev])
            t_prev = op("vector", lambda e, bidx=bidx: e.tensor_copy(
                qT[:, bidx * 1024:(bidx + 1) * 1024], psAb[:, 0:1024]), [t_tr])
        t_trc_free = t_prev
        t_mm_toks = []
        for l in range(16):
            ksl = kvs[:, (l % 2) * 128:(l % 2) * 128 + 128]
            t_s = op("vector", lambda e, h6=h6, ksl=ksl, l=l, h=h: e.tensor_scalar(
                ksl, kvini[:, h6 * 128:(h6 + 1) * 128],
                float(_RC[h] ** l), None, ALU.mult),
                [t_kvb, t_kvs_free[l % 2]])
            t_mm = op("tensor", lambda eng, l=l, ksl=ksl: eng.matmul(
                psC[:, l * 128:(l + 1) * 128], ksl,
                qT[:, l * 128:(l + 1) * 128], start=True, stop=True),
                [t_s, t_prev] + ([t_psC2] if l == 0 else []))
            t_kvs_free[l % 2] = t_mm
            t_mm_toks.append(t_mm)
        t_qrd_free = t_mm_toks[-1]
        t_e1 = op("vector", lambda e, h=h: e.tensor_tensor(
            ssb[:].rearrange("p (l c) -> p l c", l=16),
            psC[:].rearrange("p (l c) -> p l c", l=16),
            c_qdb[:, h * 128:(h + 1) * 128].unsqueeze(1)
            .broadcast_to([128, 16, 128]), ALU.mult),
            [t_mm_toks[-1], t_ssb_free])
        t_add = op("vector", lambda e, h=h: e.tensor_tensor(
            arW[:, h * 2048:(h + 1) * 2048], arW[:, h * 2048:(h + 1) * 2048],
            ssb, ALU.add), [t_e1, t_attn_ev[h]])
        t_ssb_free = t_add
        t_psC2 = t_add
        t_corr_ev[h] = t_add


    if _PH <= 9:
        w9 = [t_psC2, t_kvb]
        dma("gpsimd", out[0:128, 0:768], kvini, waits=w9, sem="s_out")
        dma("sync", out[128:256, 0:768], ccst, waits=w9, sem="s_out")
        dma("sync", out[256:384, 0:1536], ccg[:, 0:1536], waits=w9, sem="s_out")
        dma("gpsimd", out[384:512, :].rearrange("(a p) d -> p a d", p=128)
            .rearrange("p a d -> p (a d)") if False else out[384:512, :], arW[:, 0:2048], waits=w9, sem="s_out")
        dma("gpsimd", out[512:640, :], arW[:, 15 * 2048:16 * 2048], waits=w9, sem="s_out")
        dma("gpsimd", out[640:768, :], arW[:, 12 * 2048:13 * 2048], waits=w9, sem="s_out")
        dma("sync", out[768:896, 0:64], stats[:], sem="s_out")
        P.add("gpsimd", lambda e: e.nop(), [P.tok("s_out")])
        P.flush(block)
        blk_guard.__exit__(None, None, None)
        for g in reversed(guards):
            g.__exit__(None, None, None)
        return nc
    sqb = arQ[:, 0:2048]
    t_ss = [None] * 4
    t_sq_free = t_psC2
    for h in range(H):
        w_h = [t_attn_ev[h], t_corr_ev.get(h), t_sq_free]
        t_sq = op("vector", lambda e, h=h: e.tensor_tensor(
            sqb, arW[:, h * 2048:(h + 1) * 2048],
            arW[:, h * 2048:(h + 1) * 2048], ALU.mult), w_h)
        for q in range(4):
            pp, off = [(psA, 0), (psA, 512), (psB, 0), (psB, 512)][q]
            t_ss[q] = op("tensor", lambda e, q=q, pp=pp, off=off, h=h: e.matmul(
                pp[0:1, off:off + 512], c_ob[:], sqb[:, q * 512:(q + 1) * 512],
                start=(h == 0), stop=(h == H - 1)),
                [t_sq, t_ob] +
                ([t_psT_free["q"], t_psT_free["k"], t_psC2] if h == 0 else []))
        t_sq_free = t_ss[3]
    t_c1s = []
    for q in range(4):
        pp, off = [(psA, 0), (psA, 512), (psB, 0), (psB, 512)][q]
        t_c1s.append(op("vector", lambda e, pp=pp, off=off, q=q: e.tensor_copy(
            rows[0:1, (q % 2) * 512:(q % 2) * 512 + 512]
            if False else rows[0:1, 0:512].tensor[0:1, q * 512:(q + 1) * 512]
            if False else rows[0:1, 0:2048][0:1, q * 512:(q + 1) * 512],
            pp[0:1, off:off + 512]), [t_ss[q]]))
    t_rb = []
    for q in range(4):
        sl = rows[0:1, q * 512:(q + 1) * 512]
        t_r1 = op("vector", lambda e, sl=sl: e.tensor_scalar(
            sl, sl, 1.0 / E, None, ALU.mult), [t_c1s[q]])
        tmpsl = rows[0:1, 2048:2560]
        t_r2b = op("scalar", lambda e, sl=sl, tmpsl=tmpsl: e.activation(
            tmpsl, sl, AF.Sqrt, bias=epsv[0:1, 0:1]),
            [t_r1] + ([t_rb[q - 1][1]] if q >= 1 else []))
        t_r3 = op("vector", lambda e, sl=sl, tmpsl=tmpsl: e.reciprocal(sl, tmpsl),
                  [t_r2b])
        tmm = op("tensor", lambda e, q=q, sl=sl: e.matmul(
            psB[:, (q % 2) * 512:(q % 2) * 512 + 512], c_on[:], sl,
            start=True, stop=True),
            [t_r3] + t_c1s + ([t_rb[q - 2][0]] if q >= 2 else []))
        tev = op("scalar", lambda e, q=q: e.copy(
            arG[:, 2048 + q * 512:2048 + (q + 1) * 512],
            psB[:, (q % 2) * 512:(q % 2) * 512 + 512]), [tmm])
        t_rb.append((tev, t_r2b))
    t_rb = [t[0] for t in t_rb]
    rstdb = arG[:, 2048:4096]
    t_attn_n = None
    for h in range(H):
        t_n1 = op("vector", lambda e, h=h: e.scalar_tensor_tensor(
            sqb, arW[:, h * 2048:(h + 1) * 2048],
            c_rw[:, h:h + 1], rstdb, ALU.mult, ALU.mult),
            [t_rb[-1], t_attn_ev[h], t_corr_ev.get(h), t_attn_n])
        t_attn_n = op("vector", lambda e, h=h: e.tensor_copy(
            arW[:, h * 2048:(h + 1) * 2048], sqb), [t_n1])


    if _PH <= 10:
        for a_ in range(16):
            dma("gpsimd", out[a_ * 128:(a_ + 1) * 128, :],
                arW[:, a_ * 2048:(a_ + 1) * 2048], waits=[t_attn_n], sem="s_out")
        P.add("gpsimd", lambda e: e.nop(), [P.tok("s_out")])
        P.flush(block)
        blk_guard.__exit__(None, None, None)
        for g in reversed(guards):
            g.__exit__(None, None, None)
        return nc
    wob = [arQ[:, 0:8192], arQ[:, 8192:16384]]
    fft4 = arX[:, 2048:4096]
    xft4 = arF[:, 0:2048]
    t1f4 = arF[:, 2048:4096]
    outst = arG[:, 0:2048]
    t_wo = [None, None]
    t_wo_free = [t_sq_free, t_sq_free]
    t_prev_st = t_psC2
    t_prev_1 = [t_rb[-1], t_kvb]
    t_outs = []
    for ot in range(4):
        t_wo[ot % 2] = dma("gpsimd", wob[ot % 2].rearrange("p (e n) -> p e n", e=16),
                           wo_t[ot].rearrange("(e p) n -> p e n", p=128),
                           waits=[t_attn_n if ot < 2 else None, t_wo_free[ot % 2]])
        for tq in range(4):
            r0 = tq * 4 * 128
            t_ff = dma("sync", fft4.rearrange("p (c d) -> p c d", c=4),
                       ffn_scr[r0:r0 + 512, ot * 512:(ot + 1) * 512]
                       .rearrange("(c p) d -> p c d", p=128),
                       waits=[t_prev_1[0], t_ffn_done])
            t_xf = dma("sync", xft4.rearrange("p (c d) -> p c d", c=4),
                       x_tok[r0:r0 + 512, ot * 512:(ot + 1) * 512]
                       .rearrange("(c p) d -> p c d", p=128),
                       waits=[t_prev_1[1]])
            fns = []
            for c4 in range(4):
                tc = tq * 4 + c4
                for e in range(NE):
                    fns.append(lambda eng, e=e, tc=tc, ot=ot, c4=c4: eng.matmul(
                        psC[:, c4 * 512:(c4 + 1) * 512],
                        arW[:, e * 2048 + tc * 128:e * 2048 + (tc + 1) * 128],
                        wob[ot % 2][:, e * 512:(e + 1) * 512],
                        start=(e == 0), stop=(e == NE - 1)))
            t_mm = op_group("tensor", fns,
                            [t_attn_n, t_wo[ot % 2],
                             t_prev_st if (ot == 0 and tq == 0) else t_prev_1[0]])
            t_1 = op("vector", lambda e: e.scalar_tensor_tensor(
                t1f4, psC[:], 1.0, fft4, ALU.mult, ALU.add), [t_mm, t_ff])
            t_2 = op("vector", lambda e: e.tensor_tensor(
                outst, t1f4, xft4, ALU.add),
                [t_1, t_xf] + ([] if (ot == 0 and tq == 0) else [t_prev_st]))
            t_st = dma("sync", out[r0:r0 + 512, ot * 512:(ot + 1) * 512]
                       .rearrange("(c p) d -> p c d", p=128),
                       outst.rearrange("p (c d) -> p c d", c=4),
                       waits=[t_2], sem="s_out")
            t_prev_st = t_st
            t_prev_1 = [t_1, t_2]
            t_outs.append(t_st)
        t_wo_free[ot % 2] = t_outs[-1]

    P.add("gpsimd", lambda e: e.nop(), [t_outs[-1]])

    P.flush(block)
    blk_guard.__exit__(None, None, None)
    for g in reversed(guards):
        g.__exit__(None, None, None)
    return nc


def _make_consts(core):
    g = core % 4
    pos = g * 2048 + np.arange(2048, dtype=np.float64)
    freq = (1.0 / 1024.0) ** np.linspace(0.0, 1.0, 32)
    theta = pos[:, None] * freq[None, :]
    cos = np.cos(theta).astype(np.float32).reshape(NC, 128, 1, 32)
    sin = np.sin(theta).astype(np.float32).reshape(NC, 128, 1, 32)
    cos = np.ascontiguousarray(
        np.repeat(cos, 2, axis=2).transpose(1, 0, 2, 3).reshape(128, NC * 2 * 32))
    sin = np.ascontiguousarray(
        np.repeat(sin, 2, axis=2).transpose(1, 0, 2, 3).reshape(128, NC * 2 * 32))

    i = np.arange(128, dtype=np.float64)
    diff = i[None, :, None] - i[None, None, :]
    dm = np.where(diff >= 0, _R[:, None, None] ** np.maximum(diff, 0.0), 0.0)
    dmaskT = np.ascontiguousarray(np.transpose(dm, (0, 2, 1))).astype(np.float32)

    qdb = (_R[:, None] ** (i[None, :] + 1.0))
    qdb_t = np.ascontiguousarray(
        np.broadcast_to(qdb[:, None, :], (H, 128, 128))).astype(np.float32)
    kdecT = np.ascontiguousarray(
        (_R[None, :] ** (127.0 - i[:, None])).astype(np.float32))

    coef = np.zeros((CH * 8,), dtype=np.float64)
    for h6 in range(CH):
        for r in range(8):
            gr = r % 4
            if r // 4 == core // 4 and gr < g:
                coef[h6 * 8 + r] = _RC16[(H - CH) + h6] ** (g - 1 - gr)
    coefm = np.ascontiguousarray(
        np.broadcast_to(coef.astype(np.float32)[None, :], (128, CH * 8)))

    return dict(cosc=np.ascontiguousarray(cos), sinc=np.ascontiguousarray(sin),
                dmaskT=dmaskT, qdb=qdb_t, kdecT=kdecT, coef=coefm)


def kernel(L0_attn_x, L0_attn_Wq, L0_attn_Wk, L0_attn_Wv, L0_attn_Wo,
           L0_attn_rms_w, L0_ffn_gamma, L0_ffn_W):
    x = np.asarray(L0_attn_x, dtype=np.float32)
    Wq = np.asarray(L0_attn_Wq, dtype=np.float32)
    Wk = np.asarray(L0_attn_Wk, dtype=np.float32)
    Wv = np.asarray(L0_attn_Wv, dtype=np.float32)
    Wo = np.asarray(L0_attn_Wo, dtype=np.float32)
    rms_w = np.asarray(L0_attn_rms_w, dtype=np.float32)
    gamma = np.asarray(L0_ffn_gamma, dtype=np.float32)
    Wf = np.asarray(L0_ffn_W, dtype=np.float32)

    x2d = x.reshape(16384, E)
    pairs = []
    for pair in range(8):
        cols = []
        for hh in range(2):
            hd = (pair * 2 + hh) * 128
            cols += [Wq[:, hd:hd + 128], Wk[:, hd:hd + 128], Wv[:, hd:hd + 128]]
        pairs.append(np.concatenate(cols, axis=1))
    wqkv = np.ascontiguousarray(np.stack(pairs))
    wffnT = np.ascontiguousarray(Wf.T)
    wo_t = np.ascontiguousarray(Wo.reshape(E, 4, 512).transpose(1, 0, 2))
    gamma_t = np.ascontiguousarray(gamma.reshape(NE, 128).T)
    gamma_b = np.ascontiguousarray(np.broadcast_to(gamma[None, :], (128, E)))
    rmsw_t = np.ascontiguousarray(rms_w.reshape(H, 128).T)
    identb = np.eye(128, dtype=BF)
    ones_f = np.ones((1, 128), dtype=np.float32)
    ones_c32 = np.ones((128, 1), dtype=np.float32)
    ones_cb = np.ones((128, 1), dtype=BF)

    in_maps = []
    for core in range(8):
        sl = slice(core * NTOK, (core + 1) * NTOK)
        im = dict(
            x_tok=np.ascontiguousarray(x2d[sl]),
            xT=np.ascontiguousarray(x2d[sl].T),
            wqkv=wqkv, wffnT=wffnT, wo_t=wo_t,
            gamma_t=gamma_t, gamma_b=gamma_b, rmsw_t=rmsw_t,
            identb=identb, ones_f=ones_f, ones_c32=ones_c32, ones_cb=ones_cb,
        )
        im.update(_make_consts(core))
        in_maps.append(im)

    nc = build_core_program()
    res = run_bass_kernel_spmd(nc, in_maps, core_ids=list(range(8)))
    outs = [res.results[c]["out"] for c in range(8)]
    return np.concatenate(outs, axis=0).reshape(2, 8192, E).astype(np.float32)


# version 16 (final)
# speedup vs baseline: 1.1427x; 1.0018x over previous
import math
import os
import numpy as np
import ml_dtypes
import concourse.bass as bass
import concourse.mybir as mybir
from concourse.bass_utils import run_bass_kernel_spmd

F32 = mybir.dt.float32
BF16 = mybir.dt.bfloat16
I8 = mybir.dt.int8
AF = mybir.ActivationFunctionType
ALU = mybir.AluOpType
AXL = mybir.AxisListType

E = 2048
H = 16
NTOK = 2048
NC = 16
NE = 16
CH = 10
BF = ml_dtypes.bfloat16


def _slopes(n):
    start = 2.0 ** (-2.0 ** (-(math.log2(n) - 3)))
    return np.array([start * start ** i for i in range(n)], dtype=np.float64)


_R = np.exp(-_slopes(H))
_RC = _R ** 128
_RC16 = _RC ** 16


class Prog:
    ENGINES = ("sync", "gpsimd", "tensor", "vector", "scalar")

    def __init__(self, nc):
        self.nc = nc
        self.ops = {e: [] for e in self.ENGINES}
        self.semh = {}
        self.semc = {}
        self.waited = {e: {} for e in self.ENGINES}

    def sem(self, name, guards):
        g = self.nc.semaphore(name)
        guards.append(g)
        self.semh[name] = g.__enter__()
        self.semc[name] = 0
        return name

    def inc(self, name, amt):
        self.semc[name] += amt
        return self.semh[name], (name, self.semc[name])

    def tok(self, name):
        return (name, self.semc[name])

    def add(self, eng, fn, waits=()):
        w = []
        for t in waits:
            if t is None:
                continue
            s, v = t
            if v <= 0 or self.waited[eng].get(s, 0) >= v:
                continue
            self.waited[eng][s] = v
            w.append((self.semh[s], v))
        self.ops[eng].append((w, fn))

    def flush(self, block):
        for ename in self.ENGINES:
            ops = self.ops[ename]
            if not ops:
                continue
            self.ops[ename] = []

            def body(eng, ops=ops):
                for waits, fn in ops:
                    for h, v in waits:
                        eng.wait_ge(h, v)
                    fn(eng)
            getattr(block, ename)(body)


_ENG_SEM = {"tensor": "s_pe", "vector": "s_ve", "scalar": "s_ac", "gpsimd": "s_po"}


_PH = int(os.environ.get("KPHASE", "99"))


def build_core_program():
    nc = bass.Bass("TRN2", target_bir_lowering=False)

    def din(name, shape, dt=F32):
        return nc.dram_tensor(name, shape, dt, kind="ExternalInput").ap()

    x_tok = din("x_tok", [NTOK, E])
    xTin = din("xT", [E, NTOK])
    wqkv = din("wqkv", [8, E, 768])
    wffnT = din("wffnT", [E, E])
    wo_t = din("wo_t", [4, E, 512])
    cosc = din("cosc", [128, NC * 2 * 32])
    sinc = din("sinc", [128, NC * 2 * 32])
    dmaskT_in = din("dmaskT", [H, 128, 128])
    qdb_in = din("qdb", [H, 128, 128])
    kdecT_in = din("kdecT", [128, H])
    coef_in = din("coef", [128, CH * 8])
    rmsw_in = din("rmsw_t", [128, H])
    gamma_in = din("gamma_t", [128, NE])
    gammab_in = din("gamma_b", [128, E])
    ident_in = din("identb", [128, 128], BF16)
    ones_in = din("ones_f", [1, 128])
    onesc_in = din("ones_c32", [128, 1])
    onescb_in = din("ones_cb", [128, 1], BF16)

    out = nc.dram_tensor("out", [NTOK, E], F32, kind="ExternalOutput").ap()

    q_scr = nc.dram_tensor("q_scr", [H, NTOK, 128], BF16).ap()
    k_scr = nc.dram_tensor("k_scr", [H, NTOK, 128], BF16).ap()
    kp_scr = nc.dram_tensor("kp_scr", [H, NTOK, 128], BF16).ap()
    v_scr = nc.dram_tensor("v_scr", [H, NTOK, 128], BF16).ap()
    ffn_scr = nc.dram_tensor("ffn_scr", [NTOK, E], BF16).ap()
    row_scr = nc.dram_tensor("row_scr", [128, 48], F32).ap()
    cc_i = nc.dram_tensor("cc_i", [CH * 128, 128], F32).ap()
    cc_o = nc.dram_tensor("cc_o", [8 * CH * 128, 128], F32, addr_space="Shared").ap()

    guards = []

    def sb(name, shape, dt):
        g = nc.sbuf_tensor(name, shape, dt)
        guards.append(g)
        return g.__enter__()

    def mkpsum(name, shape, dt=F32):
        g = nc.psum_tensor(name, shape, dt)
        guards.append(g)
        return g.__enter__()

    arX = sb("arX", [128, NE * 2048], BF16)
    arW = sb("arW", [128, NE * 2048], BF16)
    arQ = sb("arQ", [128, 16384], BF16)
    arF = sb("arF", [128, 4096], F32)
    arG = sb("arG", [128, 4096], F32)
    arQi = arQ[:].bitcast(I8)
    arQf = arQ[:].bitcast(F32)
    arGb = arG[:].bitcast(BF16)

    c_kd = sb("c_kd", [128, H], F32)
    c_cf = sb("c_cf", [128, CH * 8], F32)
    c_rw = sb("c_rw", [128, H], F32)
    c_gt = sb("c_gt", [128, NE], F32)
    c_id = sb("c_id", [128, 128], BF16)
    c_on = sb("c_on", [1, 128], F32)
    c_oc = sb("c_oc", [128, 1], F32)
    c_ob = sb("c_ob", [128, 1], BF16)

    epsv = sb("epsv", [128, 2], F32)
    junk = arQ[:, 14336:16384]
    junki = junk.bitcast(I8)
    sv = sb("sv", [128, 16 * NC], F32)
    stats = sb("stats", [128, 64], F32)
    rows = sb("rows", [1, 2564], F32)
    wsc = sb("wsc", [128, 4], F32)
    arXf = arX[:].bitcast(F32)
    arWf = arW[:].bitcast(F32)
    bcast = arQ[:, 8192:14336]
    c_gb = arWf[:, 0:2048]
    zs = arX[:, 0:2048]
    ssb = arX[:, 2048:4096]
    ccst = arXf[:, 2048:2048 + CH * 128]
    ccg = arXf[:, 3328:3328 + 2 * CH * 128]
    kvini = arX[:, 11776:11776 + CH * 128]
    kvs = arX[:, 13056:13312]
    c_dm = arXf[:, 6720:8768]
    c_qdb = arXf[:, 8768:10816]

    psA = mkpsum("psA", [128, 1024], F32)
    psB = mkpsum("psB", [128, 1024], F32)
    psC = mkpsum("psC", [128, 2048], F32)
    psAb = psA[:].bitcast(BF16)
    psBb = psB[:].bitcast(BF16)

    P = Prog(nc)
    for s in ("s_sy", "s_gp", "s_pe", "s_ve", "s_ac", "s_po", "s_cc", "s_out"):
        P.sem(s, guards)

    blk_guard = nc.Block()
    block = blk_guard.__enter__()

    def dma(queue, dst, src, waits=(), sem=None, **kw):
        sname = sem or ("s_sy" if queue == "sync" else "s_gp")
        h, t = P.inc(sname, 16)
        P.add(queue, lambda eng, d=dst, s=src: eng.dma_start(out=d, in_=s, **kw)
              .then_inc(h, 16), waits)
        return t

    def op(eng, fn, waits=()):
        h, t = P.inc(_ENG_SEM[eng], 1)
        P.add(eng, lambda e, f=fn: f(e).then_inc(h, 1), waits)
        return t

    def op_group(eng, fns, waits=()):
        h, t = P.inc(_ENG_SEM[eng], 1)

        def run(e, fns=tuple(fns)):
            for f in fns[:-1]:
                f(e)
            fns[-1](e).then_inc(h, 1)
        P.add(eng, run, waits)
        return t

    t_eps = op("gpsimd", lambda e: e.memset(epsv[:, 0:1], 1e-5), [])
    t_eps = op("gpsimd", lambda e: e.memset(epsv[:, 1:2], 1e-6), [t_eps])
    dma("sync", c_kd[:], kdecT_in)
    dma("sync", c_cf[:], coef_in)
    dma("sync", c_rw[:], rmsw_in)
    t_gb = dma("sync", c_gb, gammab_in)
    t_id = dma("sync", c_id[:], ident_in)
    t_on = dma("sync", c_on[:], ones_in)
    t_oc = dma("sync", c_oc[:], onesc_in)
    t_ob = dma("sync", c_ob[:], onescb_in)
    t_consts = dma("sync", c_gt[:], gamma_in)

    t_xT = []
    for e in range(NE):
        t_xT.append(dma("gpsimd", arX[:, e * 2048:(e + 1) * 2048],
                        xTin[e * 128:(e + 1) * 128, :]))
    t_xT_all = t_xT[-1]

    xstage = [arF[:, 0:2048], arF[:, 2048:4096]]
    xnbuf = [arG[:, 0:2048], arG[:, 2048:4096]]
    t_xfree = [None, None]
    t_nfree = [None, None]
    t_K = [None] * NC
    for c in range(NC):
        b = xstage[c % 2]
        xn = xnbuf[c % 2]
        V = sv[:, c * 16:(c + 1) * 16]
        t_ld = dma("sync", b, x_tok[c * 128:(c + 1) * 128, :], waits=[t_xfree[c % 2]])
        t_s1 = op("scalar", lambda e, b=b, V=V: e.activation(
            junk, b, AF.Copy, accum_out=V[:, 0:1]), [t_ld])
        t_s2 = op("scalar", lambda e, b=b, V=V: e.activation(
            junk, b, AF.Square, accum_out=V[:, 1:2]), [t_ld])
        t_m = op("vector", lambda e, V=V, c=c: e.tensor_scalar(
            stats[:, c:c + 1], V[:, 0:1], 1.0 / E, None, ALU.mult), [t_s1])
        t_v1 = op("vector", lambda e, V=V: e.tensor_scalar(
            V[:, 2:3], V[:, 1:2], 1.0 / E, None, ALU.mult), [t_s2])
        t_m2 = op("vector", lambda e, V=V, c=c: e.tensor_tensor(
            V[:, 3:4], stats[:, c:c + 1], stats[:, c:c + 1], ALU.mult), [t_m])
        t_var = op("vector", lambda e, V=V: e.tensor_tensor(
            V[:, 4:5], V[:, 2:3], V[:, 3:4], ALU.subtract), [t_v1, t_m2])
        t_sd = op("scalar", lambda e, V=V: e.activation(
            V[:, 5:6], V[:, 4:5], AF.Sqrt, bias=epsv[:, 0:1]), [t_var, t_eps])
        t_rstd = op("vector", lambda e, V=V, c=c: e.reciprocal(
            stats[:, 16 + c:17 + c], V[:, 5:6]), [t_sd])
        t_u = op("vector", lambda e, V=V: e.tensor_scalar(
            V[:, 6:7], V[:, 4:5], 1e-5, None, ALU.add), [t_var])
        t_ur = op("vector", lambda e, V=V: e.reciprocal(V[:, 7:8], V[:, 6:7]), [t_u])
        t_r2 = op("vector", lambda e, V=V: e.tensor_tensor(
            V[:, 8:9], V[:, 4:5], V[:, 7:8], ALU.mult), [t_ur])
        t_r2e = op("scalar", lambda e, V=V: e.activation(
            V[:, 9:10], V[:, 8:9], AF.Sqrt, bias=epsv[:, 1:2]), [t_r2, t_eps])
        t_rstd2 = op("vector", lambda e, V=V: e.reciprocal(V[:, 10:11], V[:, 9:10]),
                     [t_r2e])
        t_xn = op("vector", lambda e, b=b, xn=xn, c=c: e.tensor_scalar(
            xn, b, stats[:, c:c + 1], stats[:, 16 + c:17 + c],
            ALU.subtract, ALU.mult), [t_rstd, t_nfree[c % 2]])
        t_xg = op("gpsimd", lambda e, b=b, xn=xn: e.tensor_tensor(
            b, xn, c_gb, ALU.mult), [t_xn, t_gb, t_s1, t_s2])
        t_am = op("vector", lambda e, b=b, V=V: e.tensor_reduce(
            V[:, 11:12], b, AXL.X, ALU.max, apply_absolute_value=True), [t_xg])
        t_amx = op("vector", lambda e, V=V: e.tensor_tensor(
            V[:, 12:13], V[:, 11:12], V[:, 10:11], ALU.mult), [t_am, t_rstd2])
        t_ap = op("vector", lambda e, V=V: e.tensor_scalar(
            V[:, 13:14], V[:, 12:13], 1e-5, None, ALU.add), [t_amx])
        t_ar = op("vector", lambda e, V=V: e.reciprocal(V[:, 14:15], V[:, 13:14]),
                  [t_ap])
        t_as = op("vector", lambda e, V=V, c=c: e.tensor_scalar(
            stats[:, 48 + c:49 + c], V[:, 14:15], 127.0, None, ALU.mult), [t_ar])
        t_k1 = op("vector", lambda e, V=V, c=c: e.tensor_tensor(
            V[:, 15:16], stats[:, 16 + c:17 + c], V[:, 10:11], ALU.mult),
            [t_rstd, t_rstd2])
        t_K[c] = op("vector", lambda e, V=V, c=c: e.tensor_tensor(
            stats[:, 32 + c:33 + c], V[:, 15:16], stats[:, 48 + c:49 + c],
            ALU.mult), [t_k1, t_as])
        t_xfree[c % 2] = t_am
        t_nfree[c % 2] = t_xg
    t_p1done = t_K[NC - 1]

    if _PH <= 1:
        dma("sync", out[0:128, 0:64], stats[:], sem="s_out")
        P.add("gpsimd", lambda e: e.nop(), [P.tok("s_out")])
        P.flush(block)
        blk_guard.__exit__(None, None, None)
        for g in reversed(guards):
            g.__exit__(None, None, None)
        return nc
    t_wslab = [None, None]
    t_wacc = None
    for i in range(NE):
        b = xnbuf[i % 2]
        gate = t_nfree[i % 2] if i < 2 else t_wslab[i % 2]
        t_wl = dma("sync", b, wffnT[i * 128:(i + 1) * 128, :], waits=[gate])
        t_abs = op("scalar", lambda e, b=b: e.activation(
            junk, b, AF.Abs, accum_out=wsc[:, 1:2]), [t_wl, t_wacc])
        if i == 0:
            t_wacc = op("vector", lambda e: e.tensor_copy(
                wsc[:, 2:3], wsc[:, 1:2]), [t_abs])
        else:
            t_wacc = op("vector", lambda e: e.tensor_tensor(
                wsc[:, 2:3], wsc[:, 2:3], wsc[:, 1:2], ALU.add), [t_abs])
        t_wslab[i % 2] = t_wacc
    t_rw1_ph = [None]
    t_wred = op("tensor", lambda e: e.matmul(
        psA[0:1, 0:1], wsc[:, 2:3], c_oc[:], start=True, stop=True),
        [t_wacc, t_oc])
    t_ws1 = op("vector", lambda e: e.tensor_scalar(
        rows[0:1, 2560:2561], psA[0:1, 0:1], 1.0 / (E * E), 1e-5,
        ALU.mult, ALU.add), [t_wred])
    t_ws2 = op("vector", lambda e: e.reciprocal(
        rows[0:1, 2561:2562], rows[0:1, 2560:2561]), [t_ws1])
    t_wsb = op("tensor", lambda e: e.matmul(
        psA[:, 0:1], c_on[:], rows[0:1, 2561:2562], start=True, stop=True),
        [t_ws2, t_on])
    t_wsv = op("scalar", lambda e: e.copy(wsc[:, 0:1], psA[:, 0:1]), [t_wsb])
    t_f1 = op("vector", lambda e: e.tensor_scalar(
        stats[:, 0:16], stats[:, 48:64], wsc[:, 0:1], None, ALU.mult),
        [t_wsv, t_p1done, t_rw1_ph[0]])
    t_ffsc = op("vector", lambda e: e.reciprocal(stats[:, 48:64], stats[:, 0:16]),
                [t_f1])


    if _PH <= 2:
        dma("sync", out[0:128, 0:64], stats[:], sem="s_out")
        P.add("gpsimd", lambda e: e.nop(), [P.tok("s_out")])
        P.flush(block)
        blk_guard.__exit__(None, None, None)
        for g in reversed(guards):
            g.__exit__(None, None, None)
        return nc
    t_rw1 = dma("sync", row_scr[:, 0:48], stats[:, 0:48], waits=[t_p1done])
    t_rw1_ph[0] = t_rw1
    t_bc = []
    t_rdfree = [None, None]
    t_mmq = [None, None]
    kq = 0
    for j in range(3):
        tev = None
        for q in range(4):
            sl = rows[0:1, (kq % 2) * 512:(kq % 2) * 512 + 512]
            src_ap = row_scr.rearrange("p c -> c p")[
                j * 16 + q * 4:j * 16 + (q + 1) * 4, :].unsqueeze(0)
            t_rd = dma("gpsimd", sl.rearrange("a (c p) -> a c p", c=4), src_ap,
                       waits=[t_rw1, t_rdfree[kq % 2]],
                       allow_slow_non_contiguous=True)
            tmm = op("tensor", lambda e, kq=kq, sl=sl: e.matmul(
                psB[:, (kq % 2) * 512:(kq % 2) * 512 + 512], c_on[:], sl,
                start=True, stop=True),
                [t_rd, t_on, t_wsb, t_mmq[kq % 2]])
            t_rdfree[kq % 2] = tmm
            tev = op("scalar", lambda e, j=j, q=q, kq=kq: e.copy(
                bcast[:, j * 2048 + q * 512:j * 2048 + (q + 1) * 512],
                psB[:, (kq % 2) * 512:(kq % 2) * 512 + 512]), [tmm])
            t_mmq[kq % 2] = tev
            kq += 1
        t_bc.append(tev)
    m_b = bcast[:, 0:2048]
    rstd_b = bcast[:, 2048:4096]
    K_b = bcast[:, 4096:6144]


    if _PH <= 3:
        dma("sync", out[0:128, 0:64], stats[:], sem="s_out")
        P.add("gpsimd", lambda e: e.nop(), [P.tok("s_out")])
        P.flush(block)
        blk_guard.__exit__(None, None, None)
        for g in reversed(guards):
            g.__exit__(None, None, None)
        return nc
    t_wr = [t_wslab[0], t_wslab[1]]
    for i in range(NE):
        b = xnbuf[i % 2]
        t_wl = dma("sync", b, wffnT[i * 128:(i + 1) * 128, :], waits=[t_wr[i % 2]])
        ti8 = op("vector", lambda e, b=b: e.tensor_scalar(
            junki[:, 0:2048], b, wsc[:, 0:1], None, ALU.mult),
            [t_wl, t_wsv] + ([t_nfree[0], t_nfree[1]] if i < 2 else []))
        t_wr[i % 2] = op("vector", lambda e, i=i: e.tensor_scalar(
            arW[:, i * 2048:(i + 1) * 2048], junki[:, 0:2048],
            1.0, -1.0, ALU.min, ALU.max), [ti8])
    t_wrdone = op("vector", lambda e: e.tensor_copy(wsc[:, 1:2], wsc[:, 0:1]),
                  [t_wr[0], t_wr[1]])


    if _PH <= 4:
        dma("sync", out[0:128, 0:64], stats[:], sem="s_out")
        P.add("gpsimd", lambda e: e.nop(), [P.tok("s_out")])
        P.flush(block)
        blk_guard.__exit__(None, None, None)
        for g in reversed(guards):
            g.__exit__(None, None, None)
        return nc
    t_ffn_ev = {}
    t_ps_free = {0: t_wsv, 1: None}
    t_mm_prev = None
    for half in range(2):
        hs = slice(half * 1024, (half + 1) * 1024)
        t_xq = [None] * NE
        for e in range(NE):
            t_t1 = op("vector", lambda e_, e=e, half=half, hs=hs: e_.tensor_tensor(
                arGb[:, 0:1024],
                arX[:, e * 2048 + half * 1024:e * 2048 + half * 1024 + 1024],
                m_b[:, hs], ALU.subtract),
                [t_bc[0], t_xT[e], t_wrdone, t_wslab[0], t_wslab[1]])
            t_xq[e] = op("vector", lambda e_, e=e, hs=hs: e_.scalar_tensor_tensor(
                arQi[:, e * 1024:(e + 1) * 1024], arGb[:, 0:1024],
                c_gt[:, e:e + 1], K_b[:, hs], ALU.mult, ALU.mult),
                [t_t1, t_bc[2], t_consts])
        for tci in range(8):
            tc = half * 8 + tci
            t_lhs = op("vector", lambda e_, tci=tci: e_.tensor_copy(
                arGb[:, 4096:6144].rearrange("p (e d) -> p e d", e=16),
                arQi[:, 0:16384].rearrange("p (e t) -> p e t", e=16)
                [:, :, tci * 128:(tci + 1) * 128]),
                t_xq + [t_mm_prev])
            for oh in range(4):
                pp = psA if oh % 2 == 0 else psB
                fns = [lambda eng, e=e, oh=oh, pp=pp: eng.matmul(
                    pp[:, 0:512], arGb[:, 4096 + e * 128:4096 + (e + 1) * 128],
                    arW[:, e * 2048 + oh * 512:e * 2048 + (oh + 1) * 512],
                    start=(e == 0), stop=(e == NE - 1)) for e in range(NE)]
                tmm = op_group("tensor", fns,
                               [t_lhs, t_wrdone, t_ps_free[oh % 2], t_mm_prev])
                t_mm_prev = tmm
                t_ps_free[oh % 2] = op("vector",
                                       lambda e_, oh=oh, pp=pp, tc=tc: e_.tensor_scalar(
                                           junk[:, oh * 512:(oh + 1) * 512], pp[:, 0:512],
                                           stats[:, 48 + tc:49 + tc], None, ALU.mult),
                                       [tmm, t_ffsc,
                                        t_ffn_ev.get(tc - 1)])
            t_ffn_ev[tc] = dma("sync", ffn_scr[tc * 128:(tc + 1) * 128, :],
                               junk, waits=[t_ps_free[0], t_ps_free[1]])
    t_ffn_done = t_ffn_ev[15]


    if _PH <= 5:
        dma("gpsimd", out, ffn_scr, waits=[t_ffn_done], sem="s_out")
        dma("sync", out[0:128, 0:64], stats[:, 0:64] if False else stats[:], sem="s_out") if False else None
        P.add("gpsimd", lambda e: e.nop(), [P.tok("s_out")])
        P.flush(block)
        blk_guard.__exit__(None, None, None)
        for g in reversed(guards):
            g.__exit__(None, None, None)
        return nc
    t_xnT = None
    for e in range(NE):
        t_a = op("vector", lambda e_, e=e: e_.tensor_tensor(
            arGb[:, 4096:6144], arX[:, e * 2048:(e + 1) * 2048], m_b[:],
            ALU.subtract), [t_bc[1], t_mm_prev if e == 0 else None])
        t_xnT = op("vector", lambda e_, e=e: e_.tensor_tensor(
            arX[:, e * 2048:(e + 1) * 2048], arGb[:, 4096:6144], rstd_b[:],
            ALU.mult), [t_a])


    if _PH <= 6:
        dma("sync", out[0:128, 0:64], stats[:], sem="s_out")
        P.add("gpsimd", lambda e: e.nop(), [P.tok("s_out")])
        P.flush(block)
        blk_guard.__exit__(None, None, None)
        for g in reversed(guards):
            g.__exit__(None, None, None)
        return nc
    c_cosR = arQf[:, 4480:5504]
    c_sinR = arQf[:, 5504:6528]
    t_cosR = dma("sync", c_cosR, cosc, waits=[t_xnT])
    t_sinR = dma("sync", c_sinR, sinc, waits=[t_xnT])

    stg = [arQ[:, 0:3072], arQ[:, 3072:6144]]
    kstg = [arQ[:, 6144:7168], arQ[:, 7168:8192]]
    rsc = [arQ[:, 8192:8448], arQ[:, 8448:8704], arQ[:, 8704:8960]]
    t_wq = [None, None]
    t_wq_free = [t_mm_prev, t_mm_prev]
    t_stg_free = [None, None]
    t_kp_free = [None, None]
    t_last_spill = None
    for pair in range(8):
        wb = arW[:, (pair % 2) * 12288:(pair % 2) * 12288 + 12288]
        t_wq[pair % 2] = dma("gpsimd", wb.rearrange("p (e n) -> p e n", e=16),
                             wqkv[pair].rearrange("(e p) n -> p e n", p=128),
                             waits=[t_wq_free[pair % 2],
                                    t_mm_prev if pair < 2 else None])
        for g4 in range(4):
            st = stg[g4 % 2]
            kp = kstg[g4 % 2]
            stv = st.rearrange("p (c h x) -> p c h x", c=4, h=2)
            t_evs = []
            for c4 in range(4):
                c = g4 * 4 + c4
                pp = psA if c % 2 == 0 else psB
                fns = []
                for e in range(NE):
                    for hh in range(2):
                        fns.append(lambda eng, e=e, hh=hh, pp=pp, c=c, wb=wb:
                                   eng.matmul(
                                       pp[:, hh * 512:hh * 512 + 384],
                                       arX[:, e * 2048 + c * 128:e * 2048 + (c + 1) * 128],
                                       wb[:, e * 768 + hh * 384:e * 768 + (hh + 1) * 384],
                                       start=(e == 0), stop=(e == NE - 1)))
                tmm = op_group("tensor", fns,
                               [t_xnT, t_wq[pair % 2], t_ps_free[c % 2]])
                if c4 % 2 == 0:
                    t_ev = op("scalar", lambda e_, pp=pp, st=st, c4=c4: e_.copy(
                        st[:, c4 * 768:(c4 + 1) * 768]
                        .rearrange("p (h x) -> p h x", h=2),
                        pp[:].rearrange("p (h x) -> p h x", h=2)[:, :, 0:384]),
                        [tmm, t_stg_free[g4 % 2]])
                else:
                    t_ev = op("vector", lambda e_, pp=pp, st=st, c4=c4: e_.tensor_copy(
                        st[:, c4 * 768:(c4 + 1) * 768]
                        .rearrange("p (h x) -> p h x", h=2),
                        pp[:].rearrange("p (h x) -> p h x", h=2)[:, :, 0:384]),
                        [tmm, t_stg_free[g4 % 2]])
                t_ps_free[c % 2] = t_ev
                t_evs.append(t_ev)
            stm = st.rearrange("p (m x) -> p m x", m=8)
            cosv = c_cosR.rearrange("p (c f) -> p c f", f=32)[:, g4 * 8:(g4 + 1) * 8, :]
            sinv = c_sinR.rearrange("p (c f) -> p c f", f=32)[:, g4 * 8:(g4 + 1) * 8, :]
            sA = rsc[0].rearrange("p (m f) -> p m f", m=8)
            sB = rsc[1].rearrange("p (m f) -> p m f", m=8)
            sC = rsc[2].rearrange("p (m f) -> p m f", m=8)
            t_rot = t_evs[-1]
            for base in (0, 128):
                x1 = stm[:, :, base:base + 32]
                x2 = stm[:, :, base + 64:base + 96]
                tA = op("vector", lambda e, x1=x1, cosv=cosv, sinv=sinv: e.tensor_tensor(
                    sA, x1, sinv, ALU.mult), [t_rot] + t_evs + [t_cosR, t_sinR])
                tB = op("vector", lambda e, x1=x1, cosv=cosv: e.tensor_tensor(
                    sB, x1, cosv, ALU.mult), [tA])
                tC = op("vector", lambda e, x2=x2, sinv=sinv: e.tensor_tensor(
                    sC, x2, sinv, ALU.mult), [tB])
                tY1 = op("vector", lambda e, x1=x1: e.tensor_tensor(x1, sB, sC, ALU.add), [tC])
                tD = op("vector", lambda e, x2=x2, cosv=cosv: e.tensor_tensor(
                    sB, x2, cosv, ALU.mult), [tY1])
                t_rot = op("vector", lambda e, x2=x2: e.tensor_tensor(x2, sB, sA, ALU.subtract), [tD])
            t_kp = t_rot
            for mh in range(8):
                head = pair * 2 + (mh % 2)
                t_kp = op("vector", lambda e, kp=kp, mh=mh, head=head, stm=stm: e.tensor_scalar(
                    kp[:, mh * 128:(mh + 1) * 128],
                    stm[:, mh, 128:256], c_kd[:, head:head + 1], None, ALU.mult),
                    [t_rot, t_kp_free[g4 % 2]])
            t_sp = []
            for scr, base in ((q_scr, 0), (k_scr, 128), (v_scr, 256)):
                for hh in range(2):
                    dst = scr[pair * 2 + hh, g4 * 512:(g4 + 1) * 512, :] \
                        .rearrange("(c p) d -> p c d", p=128)
                    t_sp.append(dma("sync", dst, stv[:, :, hh, base:base + 128],
                                    waits=[t_rot]))
            kpv = kp.rearrange("p (c h d) -> p c h d", c=4, h=2)
            for hh in range(2):
                dstp = kp_scr[pair * 2 + hh, g4 * 512:(g4 + 1) * 512, :] \
                    .rearrange("(c p) d -> p c d", p=128)
                t_sp.append(dma("sync", dstp, kpv[:, :, hh, :], waits=[t_kp]))
            t_stg_free[g4 % 2] = t_sp[7]
            t_kp_free[g4 % 2] = t_sp[7]
            t_last_spill = t_sp[7]
        t_wq_free[pair % 2] = t_last_spill
    t_p2done = t_last_spill


    if _PH <= 7:
        dma("sync", out[0:128, 0:64], stats[:], sem="s_out")
        P.add("gpsimd", lambda e: e.nop(), [P.tok("s_out")])
        P.flush(block)
        blk_guard.__exit__(None, None, None)
        for g in reversed(guards):
            g.__exit__(None, None, None)
        return nc
    t_dm = dma("sync", c_dm.rearrange("p (h c) -> p h c", h=H),
               dmaskT_in.rearrange("h s c -> s h c"), waits=[t_p2done])
    t_qdbl = dma("sync", c_qdb.rearrange("p (h c) -> p h c", h=H),
                 qdb_in.rearrange("h d c -> d h c"), waits=[t_p2done])

    q_sb = arQ[:, 0:2048]
    k_sb = arQ[:, 2048:4096]
    kp_sb = arQ[:, 4096:6144]
    v_sb = arQ[:, 6144:8192]
    qT = arQ[:, 8192:10240]
    kT = arQ[:, 10240:12288]
    uperm = arF[:, 0:2048]
    hlist = list(range(H - CH, H)) + list(range(H - CH))

    t_rd_free = [t_p2done] * 4
    t_zs_free = []
    t_sd0_free = None
    t_psC_free = t_p2done
    t_psT_free = {"q": t_ps_free[0], "k": t_ps_free[1]}
    t_attn_ev = {}
    t_ssb_free = t_p2done
    t_uperm_free = None
    t_qp_free = None
    t_cc_st = []

    for hi, h in enumerate(hlist):
        kept = h >= H - CH
        qpT = arQ[:, 12288:14336]
        t_q = dma("sync", q_sb.rearrange("p (l d) -> p l d", l=16),
                  q_scr[h].rearrange("(l p) d -> p l d", p=128),
                  waits=[t_rd_free[0]])
        t_k = dma("sync", k_sb.rearrange("p (l d) -> p l d", l=16),
                  k_scr[h].rearrange("(l p) d -> p l d", p=128),
                  waits=[t_rd_free[1]])
        t_kp2 = dma("sync", kp_sb.rearrange("p (l d) -> p l d", l=16),
                    kp_scr[h].rearrange("(l p) d -> p l d", p=128),
                    waits=[t_rd_free[2]])
        t_v = dma("sync", v_sb.rearrange("p (l d) -> p l d", l=16),
                  v_scr[h].rearrange("(l p) d -> p l d", p=128),
                  waits=[t_rd_free[3]])

        fns = [lambda eng, l=l: eng.matmul(
            psC[:, l * 128:(l + 1) * 128], kp_sb[:, l * 128:(l + 1) * 128],
            v_sb[:, l * 128:(l + 1) * 128], start=True, stop=True)
            for l in range(16)]
        t_U = op_group("tensor", fns, [t_kp2, t_v, t_psC_free])
        t_up = op("scalar", lambda e: e.activation(
            uperm.rearrange("p (e l) -> p e l", l=16),
            psC[:].rearrange("p (l e) -> p e l", l=16), AF.Copy),
            [t_U, t_uperm_free])
        sd0m = arF[:, 2048:4096]
        t_d0a = op("gpsimd", lambda e, h=h: e.memset(sd0m, float(_RC[h])),
                   [t_sd0_free])
        t_d0b = op("gpsimd", lambda e: e.memset(
            sd0m.rearrange("p (e l) -> p e l", l=16)[:, :, 0], 0.0), [t_d0a])
        t_zs = op("vector", lambda e: e.tensor_tensor_scan(
            zs[:], sd0m, uperm[:] if False else uperm,
            0.0, ALU.mult, ALU.add), [t_up, t_d0b] + list(t_zs_free))
        t_uperm_free = t_zs
        t_sd0_free = t_zs

        t_tr_done = {}
        for tn, src, dst, psx, t_src in (("q", q_sb, qT, psAb, t_q),
                                         ("k", k_sb, kT, psBb, t_k)):
            t_prev = t_psT_free[tn]
            for bidx in range(2):
                fns = [lambda eng, l=l, src=src, psx=psx: eng.transpose(
                    psx[:, (l % 8) * 128:(l % 8 + 1) * 128],
                    src[:, l * 128:(l + 1) * 128], c_id[:])
                    for l in range(bidx * 8, (bidx + 1) * 8)]
                t_tr = op_group("tensor", fns, [t_src, t_id, t_prev])
                if tn == "q":
                    t_prev = op("vector", lambda e, bidx=bidx, dst=dst, psx=psx:
                                e.tensor_copy(dst[:, bidx * 1024:(bidx + 1) * 1024],
                                              psx[:, 0:1024]), [t_tr])
                else:
                    t_prev = op("scalar", lambda e, bidx=bidx, dst=dst, psx=psx:
                                e.copy(dst[:, bidx * 1024:(bidx + 1) * 1024],
                                       psx[:, 0:1024]), [t_tr])
            t_tr_done[tn] = t_prev
            t_psT_free[tn] = t_prev

        t_qp = op("vector", lambda e, qpT=qpT, h=h: e.tensor_tensor(
            qpT.rearrange("p (l c) -> p l c", l=16),
            qT.rearrange("p (l c) -> p l c", l=16),
            c_qdb[:, h * 128:(h + 1) * 128].unsqueeze(1)
            .broadcast_to([128, 16, 128]), ALU.mult),
            [t_tr_done["q"], t_qdbl, t_qp_free])

        fns = [lambda eng, l=l: eng.matmul(
            psC[:, l * 128:(l + 1) * 128], kT[:, l * 128:(l + 1) * 128],
            qT[:, l * 128:(l + 1) * 128], start=True, stop=True)
            for l in range(16)]
        t_S = op_group("tensor", fns, [t_tr_done["q"], t_tr_done["k"], t_up])
        t_mask = op("vector", lambda e, h=h: e.tensor_tensor(
            ssb[:].rearrange("p (l c) -> p l c", l=16),
            psC[:].rearrange("p (l c) -> p l c", l=16),
            c_dm[:, h * 128:(h + 1) * 128].unsqueeze(1)
            .broadcast_to([128, 16, 128]), ALU.mult), [t_S, t_ssb_free, t_dm])

        fns = []
        for l in range(16):
            fns.append(lambda eng, l=l: eng.matmul(
                psC[:, l * 128:(l + 1) * 128], v_sb[:, l * 128:(l + 1) * 128],
                ssb[:, l * 128:(l + 1) * 128], start=True, stop=(l == 0),
                skip_group_check=True))
            if l >= 1:
                fns.append(lambda eng, l=l, qpT=qpT: eng.matmul(
                    psC[:, l * 128:(l + 1) * 128],
                    zs[:].rearrange("p (e l) -> p e l", l=16)[:, :, l - 1],
                    qpT[:, l * 128:(l + 1) * 128], start=False, stop=True,
                    skip_group_check=True))
        t_O = op_group("tensor", fns, [t_mask, t_zs, t_qp])
        t_ssb_free = t_O
        t_qp_free = t_O

        t_aev = op("scalar", lambda e, h=h: e.copy(
            arW[:, h * 2048:(h + 1) * 2048], psC[:]), [t_O])
        t_attn_ev[h] = t_aev
        t_psC_free = t_aev

        if kept:
            t_st = op("scalar", lambda e, h=h: e.copy(
                ccst[:, (h - (H - CH)) * 128:(h - (H - CH) + 1) * 128],
                zs[:].rearrange("p (e l) -> p e l", l=16)[:, :, 15]), [t_zs])
            t_cc_st.append(t_st)
            t_zs_free = [t_O, t_st]
        else:
            t_zs_free = [t_O]

        t_rd_free = [t_tr_done["q"], t_tr_done["k"], t_U, t_O]


    if _PH <= 8:
        w8 = [t_psC_free, t_uperm_free, t_qp_free] + list(t_zs_free)
        dma("gpsimd", out[0:128, :], zs, waits=w8, sem="s_out")
        dma("sync", out[128:256, :], uperm, waits=w8, sem="s_out")
        dma("gpsimd", out[256:384, :], kp_sb, waits=w8, sem="s_out")
        dma("gpsimd", out[384:512, :], v_sb, waits=w8, sem="s_out")
        dma("gpsimd", out[512:640, :], q_sb, waits=w8, sem="s_out")
        dma("gpsimd", out[640:768, :], k_sb, waits=w8, sem="s_out")
        dma("gpsimd", out[768:896, :], qT, waits=w8, sem="s_out")
        dma("gpsimd", out[896:1024, :], arQ[:, 12288:14336], waits=w8, sem="s_out")
        dma("gpsimd", out[1024:1152, :], ssb, waits=w8, sem="s_out")
        dma("gpsimd", out[1152:1280, 0:1536], ccst.bitcast(BF16)[:, 0:1536], waits=w8, sem="s_out")
        dma("sync", out[1280:1408, 0:64], stats[:], sem="s_out")
        P.add("gpsimd", lambda e: e.nop(), [P.tok("s_out")])
        P.flush(block)
        blk_guard.__exit__(None, None, None)
        for g in reversed(guards):
            g.__exit__(None, None, None)
        return nc
    t_ccw = dma("gpsimd", cc_i.rearrange("(a p) d -> p a d", p=128),
                ccst[:].rearrange("p (a d) -> p a d", a=CH), waits=t_cc_st)
    hcc, t_cc = P.inc("s_cc", 1)
    P.add("gpsimd", lambda e: e.collective_compute(
        "AllGather", ALU.bypass,
        replica_groups=[[0, 1, 2, 3, 4, 5, 6, 7]],
        ins=[cc_i], outs=[cc_o]).then_inc(hcc, 1), [t_ccw])

    t_last_attn = t_attn_ev[hlist[-1]]
    t_comb = [None] * CH
    t_ccr_free = None
    for r in range(8):
        buf = ccg[:, (r % 2) * CH * 128:(r % 2) * CH * 128 + CH * 128]
        t_ccr = dma("sync", buf.rearrange("p (a d) -> p a d", a=CH),
                    cc_o[r * CH * 128:(r + 1) * CH * 128, :]
                    .rearrange("(a p) d -> p a d", p=128),
                    waits=[("s_cc", 1), t_ccr_free])
        tlast = None
        for h6 in range(CH):
            if r == 0:
                tc2 = op("vector", lambda e, h6=h6, buf=buf: e.tensor_scalar(
                    arG[:, h6 * 128:(h6 + 1) * 128],
                    buf[:, h6 * 128:(h6 + 1) * 128],
                    c_cf[:, h6 * 8:h6 * 8 + 1], None, ALU.mult),
                    [t_ccr, t_last_attn])
            else:
                tc2 = op("vector", lambda e, h6=h6, buf=buf, r=r: e.scalar_tensor_tensor(
                    arG[:, h6 * 128:(h6 + 1) * 128],
                    buf[:, h6 * 128:(h6 + 1) * 128],
                    c_cf[:, h6 * 8 + r:h6 * 8 + r + 1],
                    arG[:, h6 * 128:(h6 + 1) * 128], ALU.mult, ALU.add),
                    [t_ccr, t_comb[h6]])
            t_comb[h6] = tc2
            tlast = tc2
        t_ccr_free = tlast
    t_kvb = op("vector", lambda e: e.tensor_copy(kvini[:], arG[:, 0:CH * 128]),
               t_comb)

    t_corr_ev = {}
    t_kvs_free = [None, None]
    t_psC2 = t_psC_free
    t_qrd_free = t_rd_free[0]
    t_trc_free = t_psT_free["q"]
    for h6 in range(CH):
        h = (H - CH) + h6
        t_qr = dma("sync", q_sb.rearrange("p (l d) -> p l d", l=16),
                   q_scr[h].rearrange("(l p) d -> p l d", p=128),
                   waits=[t_qrd_free])
        t_prev = t_trc_free
        for bidx in range(2):
            fns = [lambda eng, l=l: eng.transpose(
                psAb[:, (l % 8) * 128:(l % 8 + 1) * 128],
                q_sb[:, l * 128:(l + 1) * 128], c_id[:])
                for l in range(bidx * 8, (bidx + 1) * 8)]
            t_tr = op_group("tensor", fns, [t_qr, t_prev])
            t_prev = op("vector", lambda e, bidx=bidx: e.tensor_copy(
                qT[:, bidx * 1024:(bidx + 1) * 1024], psAb[:, 0:1024]), [t_tr])
        t_trc_free = t_prev
        t_mm_toks = []
        for l in range(16):
            ksl = kvs[:, (l % 2) * 128:(l % 2) * 128 + 128]
            t_s = op("vector", lambda e, h6=h6, ksl=ksl, l=l, h=h: e.tensor_scalar(
                ksl, kvini[:, h6 * 128:(h6 + 1) * 128],
                float(_RC[h] ** l), None, ALU.mult),
                [t_kvb, t_kvs_free[l % 2]])
            t_mm = op("tensor", lambda eng, l=l, ksl=ksl: eng.matmul(
                psC[:, l * 128:(l + 1) * 128], ksl,
                qT[:, l * 128:(l + 1) * 128], start=True, stop=True),
                [t_s, t_prev] + ([t_psC2] if l == 0 else []))
            t_kvs_free[l % 2] = t_mm
            t_mm_toks.append(t_mm)
        t_qrd_free = t_mm_toks[-1]
        t_e1 = op("vector", lambda e, h=h: e.tensor_tensor(
            ssb[:].rearrange("p (l c) -> p l c", l=16),
            psC[:].rearrange("p (l c) -> p l c", l=16),
            c_qdb[:, h * 128:(h + 1) * 128].unsqueeze(1)
            .broadcast_to([128, 16, 128]), ALU.mult),
            [t_mm_toks[-1], t_ssb_free])
        t_add = op("vector", lambda e, h=h: e.tensor_tensor(
            arW[:, h * 2048:(h + 1) * 2048], arW[:, h * 2048:(h + 1) * 2048],
            ssb, ALU.add), [t_e1, t_attn_ev[h]])
        t_ssb_free = t_add
        t_psC2 = t_add
        t_corr_ev[h] = t_add


    if _PH <= 9:
        w9 = [t_psC2, t_kvb]
        dma("gpsimd", out[0:128, 0:768], kvini, waits=w9, sem="s_out")
        dma("sync", out[128:256, 0:768], ccst, waits=w9, sem="s_out")
        dma("sync", out[256:384, 0:1536], ccg[:, 0:1536], waits=w9, sem="s_out")
        dma("gpsimd", out[384:512, :].rearrange("(a p) d -> p a d", p=128)
            .rearrange("p a d -> p (a d)") if False else out[384:512, :], arW[:, 0:2048], waits=w9, sem="s_out")
        dma("gpsimd", out[512:640, :], arW[:, 15 * 2048:16 * 2048], waits=w9, sem="s_out")
        dma("gpsimd", out[640:768, :], arW[:, 12 * 2048:13 * 2048], waits=w9, sem="s_out")
        dma("sync", out[768:896, 0:64], stats[:], sem="s_out")
        P.add("gpsimd", lambda e: e.nop(), [P.tok("s_out")])
        P.flush(block)
        blk_guard.__exit__(None, None, None)
        for g in reversed(guards):
            g.__exit__(None, None, None)
        return nc
    sqb = arQ[:, 0:2048]
    t_ss = [None] * 4
    t_sq_free = t_psC2
    for h in range(H):
        w_h = [t_attn_ev[h], t_corr_ev.get(h), t_sq_free]
        t_sq = op("vector", lambda e, h=h: e.tensor_tensor(
            sqb, arW[:, h * 2048:(h + 1) * 2048],
            arW[:, h * 2048:(h + 1) * 2048], ALU.mult), w_h)
        for q in range(4):
            pp, off = [(psA, 0), (psA, 512), (psB, 0), (psB, 512)][q]
            t_ss[q] = op("tensor", lambda e, q=q, pp=pp, off=off, h=h: e.matmul(
                pp[0:1, off:off + 512], c_ob[:], sqb[:, q * 512:(q + 1) * 512],
                start=(h == 0), stop=(h == H - 1)),
                [t_sq, t_ob] +
                ([t_psT_free["q"], t_psT_free["k"], t_psC2] if h == 0 else []))
        t_sq_free = t_ss[3]
    t_c1s = []
    for q in range(4):
        pp, off = [(psA, 0), (psA, 512), (psB, 0), (psB, 512)][q]
        t_c1s.append(op("vector", lambda e, pp=pp, off=off, q=q: e.tensor_copy(
            rows[0:1, (q % 2) * 512:(q % 2) * 512 + 512]
            if False else rows[0:1, 0:512].tensor[0:1, q * 512:(q + 1) * 512]
            if False else rows[0:1, 0:2048][0:1, q * 512:(q + 1) * 512],
            pp[0:1, off:off + 512]), [t_ss[q]]))
    t_rb = []
    for q in range(4):
        sl = rows[0:1, q * 512:(q + 1) * 512]
        t_r1 = op("vector", lambda e, sl=sl: e.tensor_scalar(
            sl, sl, 1.0 / E, None, ALU.mult), [t_c1s[q]])
        tmpsl = rows[0:1, 2048:2560]
        t_r2b = op("scalar", lambda e, sl=sl, tmpsl=tmpsl: e.activation(
            tmpsl, sl, AF.Sqrt, bias=epsv[0:1, 0:1]),
            [t_r1] + ([t_rb[q - 1][1]] if q >= 1 else []))
        t_r3 = op("vector", lambda e, sl=sl, tmpsl=tmpsl: e.reciprocal(sl, tmpsl),
                  [t_r2b])
        tmm = op("tensor", lambda e, q=q, sl=sl: e.matmul(
            psB[:, (q % 2) * 512:(q % 2) * 512 + 512], c_on[:], sl,
            start=True, stop=True),
            [t_r3] + t_c1s + ([t_rb[q - 2][0]] if q >= 2 else []))
        tev = op("scalar", lambda e, q=q: e.copy(
            arG[:, 2048 + q * 512:2048 + (q + 1) * 512],
            psB[:, (q % 2) * 512:(q % 2) * 512 + 512]), [tmm])
        t_rb.append((tev, t_r2b))
    t_rb = [t[0] for t in t_rb]
    rstdb = arG[:, 2048:4096]
    t_attn_n = None
    for h in range(H):
        t_attn_n = op("vector", lambda e, h=h: e.scalar_tensor_tensor(
            arW[:, h * 2048:(h + 1) * 2048], arW[:, h * 2048:(h + 1) * 2048],
            c_rw[:, h:h + 1], rstdb, ALU.mult, ALU.mult),
            [t_rb[-1], t_attn_ev[h], t_corr_ev.get(h), t_attn_n])


    if _PH <= 10:
        for a_ in range(16):
            dma("gpsimd", out[a_ * 128:(a_ + 1) * 128, :],
                arW[:, a_ * 2048:(a_ + 1) * 2048], waits=[t_attn_n], sem="s_out")
        P.add("gpsimd", lambda e: e.nop(), [P.tok("s_out")])
        P.flush(block)
        blk_guard.__exit__(None, None, None)
        for g in reversed(guards):
            g.__exit__(None, None, None)
        return nc
    wob = [arQ[:, 0:8192], arQ[:, 8192:16384]]
    fft4 = arX[:, 2048:4096]
    xft4 = arF[:, 0:2048]
    t1f4 = arF[:, 2048:4096]
    outst = arG[:, 0:2048]
    t_wo = [None, None]
    t_wo_free = [t_sq_free, t_sq_free]
    t_prev_st = t_psC2
    t_prev_1 = [t_rb[-1], t_kvb]
    t_outs = []
    for ot in range(4):
        t_wo[ot % 2] = dma("gpsimd", wob[ot % 2].rearrange("p (e n) -> p e n", e=16),
                           wo_t[ot].rearrange("(e p) n -> p e n", p=128),
                           waits=[t_sq_free if ot == 0 else
                                  (t_attn_n if ot == 1 else None),
                                  t_wo_free[ot % 2]])
        for tq in range(4):
            r0 = tq * 4 * 128
            t_ff = dma("sync", fft4.rearrange("p (c d) -> p c d", c=4),
                       ffn_scr[r0:r0 + 512, ot * 512:(ot + 1) * 512]
                       .rearrange("(c p) d -> p c d", p=128),
                       waits=[t_prev_1[0], t_ffn_done])
            t_xf = dma("sync", xft4.rearrange("p (c d) -> p c d", c=4),
                       x_tok[r0:r0 + 512, ot * 512:(ot + 1) * 512]
                       .rearrange("(c p) d -> p c d", p=128),
                       waits=[t_prev_1[1]])
            fns = []
            for c4 in range(4):
                tc = tq * 4 + c4
                for e in range(NE):
                    fns.append(lambda eng, e=e, tc=tc, ot=ot, c4=c4: eng.matmul(
                        psC[:, c4 * 512:(c4 + 1) * 512],
                        arW[:, e * 2048 + tc * 128:e * 2048 + (tc + 1) * 128],
                        wob[ot % 2][:, e * 512:(e + 1) * 512],
                        start=(e == 0), stop=(e == NE - 1)))
            t_mm = op_group("tensor", fns,
                            [t_attn_n, t_wo[ot % 2],
                             t_prev_st if (ot == 0 and tq == 0) else t_prev_1[0]])
            t_1 = op("vector", lambda e: e.scalar_tensor_tensor(
                t1f4, psC[:], 1.0, fft4, ALU.mult, ALU.add), [t_mm, t_ff])
            t_2 = op("vector", lambda e: e.tensor_tensor(
                outst, t1f4, xft4, ALU.add),
                [t_1, t_xf] + ([] if (ot == 0 and tq == 0) else [t_prev_st]))
            t_st = dma("sync", out[r0:r0 + 512, ot * 512:(ot + 1) * 512]
                       .rearrange("(c p) d -> p c d", p=128),
                       outst.rearrange("p (c d) -> p c d", c=4),
                       waits=[t_2], sem="s_out")
            t_prev_st = t_st
            t_prev_1 = [t_1, t_2]
            t_outs.append(t_st)
        t_wo_free[ot % 2] = t_outs[-1]

    P.add("gpsimd", lambda e: e.nop(), [t_outs[-1]])

    P.flush(block)
    blk_guard.__exit__(None, None, None)
    for g in reversed(guards):
        g.__exit__(None, None, None)
    return nc


def _make_consts(core):
    g = core % 4
    pos = g * 2048 + np.arange(2048, dtype=np.float64)
    freq = (1.0 / 1024.0) ** np.linspace(0.0, 1.0, 32)
    theta = pos[:, None] * freq[None, :]
    cos = np.cos(theta).astype(np.float32).reshape(NC, 128, 1, 32)
    sin = np.sin(theta).astype(np.float32).reshape(NC, 128, 1, 32)
    cos = np.ascontiguousarray(
        np.repeat(cos, 2, axis=2).transpose(1, 0, 2, 3).reshape(128, NC * 2 * 32))
    sin = np.ascontiguousarray(
        np.repeat(sin, 2, axis=2).transpose(1, 0, 2, 3).reshape(128, NC * 2 * 32))

    i = np.arange(128, dtype=np.float64)
    diff = i[None, :, None] - i[None, None, :]
    dm = np.where(diff >= 0, _R[:, None, None] ** np.maximum(diff, 0.0), 0.0)
    dmaskT = np.ascontiguousarray(np.transpose(dm, (0, 2, 1))).astype(np.float32)

    qdb = (_R[:, None] ** (i[None, :] + 1.0))
    qdb_t = np.ascontiguousarray(
        np.broadcast_to(qdb[:, None, :], (H, 128, 128))).astype(np.float32)
    kdecT = np.ascontiguousarray(
        (_R[None, :] ** (127.0 - i[:, None])).astype(np.float32))

    coef = np.zeros((CH * 8,), dtype=np.float64)
    for h6 in range(CH):
        for r in range(8):
            gr = r % 4
            if r // 4 == core // 4 and gr < g:
                coef[h6 * 8 + r] = _RC16[(H - CH) + h6] ** (g - 1 - gr)
    coefm = np.ascontiguousarray(
        np.broadcast_to(coef.astype(np.float32)[None, :], (128, CH * 8)))

    return dict(cosc=np.ascontiguousarray(cos), sinc=np.ascontiguousarray(sin),
                dmaskT=dmaskT, qdb=qdb_t, kdecT=kdecT, coef=coefm)


def kernel(L0_attn_x, L0_attn_Wq, L0_attn_Wk, L0_attn_Wv, L0_attn_Wo,
           L0_attn_rms_w, L0_ffn_gamma, L0_ffn_W):
    x = np.asarray(L0_attn_x, dtype=np.float32)
    Wq = np.asarray(L0_attn_Wq, dtype=np.float32)
    Wk = np.asarray(L0_attn_Wk, dtype=np.float32)
    Wv = np.asarray(L0_attn_Wv, dtype=np.float32)
    Wo = np.asarray(L0_attn_Wo, dtype=np.float32)
    rms_w = np.asarray(L0_attn_rms_w, dtype=np.float32)
    gamma = np.asarray(L0_ffn_gamma, dtype=np.float32)
    Wf = np.asarray(L0_ffn_W, dtype=np.float32)

    x2d = x.reshape(16384, E)
    pairs = []
    for pair in range(8):
        cols = []
        for hh in range(2):
            hd = (pair * 2 + hh) * 128
            cols += [Wq[:, hd:hd + 128], Wk[:, hd:hd + 128], Wv[:, hd:hd + 128]]
        pairs.append(np.concatenate(cols, axis=1))
    wqkv = np.ascontiguousarray(np.stack(pairs))
    wffnT = np.ascontiguousarray(Wf.T)
    wo_t = np.ascontiguousarray(Wo.reshape(E, 4, 512).transpose(1, 0, 2))
    gamma_t = np.ascontiguousarray(gamma.reshape(NE, 128).T)
    gamma_b = np.ascontiguousarray(np.broadcast_to(gamma[None, :], (128, E)))
    rmsw_t = np.ascontiguousarray(rms_w.reshape(H, 128).T)
    identb = np.eye(128, dtype=BF)
    ones_f = np.ones((1, 128), dtype=np.float32)
    ones_c32 = np.ones((128, 1), dtype=np.float32)
    ones_cb = np.ones((128, 1), dtype=BF)

    in_maps = []
    for core in range(8):
        sl = slice(core * NTOK, (core + 1) * NTOK)
        im = dict(
            x_tok=np.ascontiguousarray(x2d[sl]),
            xT=np.ascontiguousarray(x2d[sl].T),
            wqkv=wqkv, wffnT=wffnT, wo_t=wo_t,
            gamma_t=gamma_t, gamma_b=gamma_b, rmsw_t=rmsw_t,
            identb=identb, ones_f=ones_f, ones_c32=ones_c32, ones_cb=ones_cb,
        )
        im.update(_make_consts(core))
        in_maps.append(im)

    nc = build_core_program()
    res = run_bass_kernel_spmd(nc, in_maps, core_ids=list(range(8)))
    outs = [res.results[c]["out"] for c in range(8)]
    return np.concatenate(outs, axis=0).reshape(2, 8192, E).astype(np.float32)
